# version 06
# baseline (speedup 1.0000x reference)
import math
from collections import defaultdict
from contextlib import ExitStack

import numpy as np
import ml_dtypes
import concourse.bass as bass
import concourse.mybir as mybir
from concourse.bass_utils import run_bass_kernel_spmd

F32 = mybir.dt.float32
BF16 = mybir.dt.bfloat16
AF = mybir.ActivationFunctionType
ALU = mybir.AluOpType
AX = mybir.AxisListType
BFnp = ml_dtypes.bfloat16

NCORES = 8
T = 2048
D = 2048
NH, DQ, DV, L = 8, 128, 256, 128
NC = 16
HF = 8192
VA = DV + 1
LN_SQ = -0.5 * math.log(128.0)
RMS_EPS = 1.1920929e-7

ENGINES = ("sync", "gpsimd", "vector", "scalar", "tensor")
TAPS = set()
SELF_SYNC_ALL = False


class Op:
    __slots__ = ("engine", "fn", "reads", "writes", "inc", "cum", "deps", "tag",
                 "semkey")

    def __init__(self, engine, fn, reads, writes, inc, tag):
        self.engine, self.fn = engine, fn
        self.reads, self.writes = reads, writes
        self.inc, self.cum, self.tag = inc, 0, tag
        self.deps = {}
        self.semkey = None


NDMA_LANES = 8


class Sched:

    def __init__(self, nc, self_sync=False):
        self.nc = nc
        self.ops = []
        self.conf = {}
        self.self_sync = self_sync
        self._cum = defaultdict(int)
        self._dma_rr = defaultdict(int)
        self._cc_idx = 0
        self._last_write = {}
        self._readers = defaultdict(list)

    def add(self, engine, fn, reads=(), writes=(), dma=False, tag="", inc=None,
            kind="", collective=False):
        is_async = dma or collective
        writes = tuple(writes)
        extra = ()
        for w in writes:
            if w in self.conf:
                extra = extra + self.conf.pop(w)
        if collective:
            semkey = ("cc", self._cc_idx)
            self._cc_idx += 1
            the_inc = 1
        elif dma:
            lane = self._dma_rr[engine] % NDMA_LANES
            self._dma_rr[engine] += 1
            semkey = (engine, "d", lane)
            the_inc = 16
        else:
            semkey = (engine, "c")
            the_inc = 1
        if inc is not None:
            the_inc = inc
        op = Op(engine, fn, tuple(reads), writes + extra, the_inc, tag)
        op.semkey = semkey
        deps = {}

        def need(sk, c, a, k, peng):
            if peng == engine and not a and not self.self_sync:
                if engine not in ("vector", "scalar", "gpsimd") \
                        and not (k == "tp" or kind == "tp"):
                    return
            if a:
                c = self._cum[sk]
            if c > deps.get(sk, -1):
                deps[sk] = c

        for r in op.reads:
            lw = self._last_write.get(r)
            if lw:
                need(*lw)
        for w in op.writes:
            lw = self._last_write.get(w)
            if lw:
                need(*lw)
            for ent in self._readers.get(w, ()):
                need(*ent)
        op.deps = deps
        self._cum[semkey] += op.inc
        op.cum = self._cum[semkey]
        rec = (semkey, op.cum, is_async, kind, engine)
        for r in op.reads:
            self._readers[r].append(rec)
        for w in op.writes:
            self._last_write[w] = rec
            self._readers[w] = []
        self.ops.append(op)
        return op

    def dma(self, engine, out, in_, reads=(), writes=(), tag="", **kw):
        return self.add(engine, lambda eng: eng.dma_start(out=out, in_=in_, **kw),
                        reads=reads, writes=writes, dma=True, tag=tag)

    def barrier(self, engine, reads, tag="barrier"):
        return self.add(engine, lambda e: None, reads=reads, tag=tag, inc=0)

    def emit(self, block, sems):
        per_engine = defaultdict(list)
        for op in self.ops:
            per_engine[op.engine].append(op)

        def mk(engine_name, ops):
            def body(eng):
                watermark = defaultdict(int)
                for op in ops:
                    for sk, c in sorted(op.deps.items(), key=str):
                        if c > watermark[sk]:
                            eng.wait_ge(sems[sk], c)
                            watermark[sk] = c
                    ins = op.fn(eng)
                    if ins is not None and op.inc:
                        ins.then_inc(sems[op.semkey], op.inc)
            return body

        for e in ENGINES:
            if per_engine[e]:
                getattr(block, e)(mk(e, per_engine[e]))

    def semkeys(self):
        ks = set()
        for op in self.ops:
            ks.add(op.semkey)
            ks.update(op.deps.keys())
        return ks

    def stats(self):
        n = defaultdict(int)
        nw = 0
        for op in self.ops:
            n[op.engine] += 1
            nw += len(op.deps)
        return dict(n), nw


class Arena:
    def __init__(self, nc, es, nbytes, name="arena"):
        self.nb = nbytes
        self.t = es.enter_context(nc.sbuf_tensor(name, [128, nbytes], mybir.dt.int8))
        self.regions = []
        self.dead = []
        self.hw = 0

    def alloc(self, key, shape, dtype, align=128):
        esz = {BF16: 2, F32: 4}[dtype]
        nb = shape[1] * esz
        self.regions.sort()
        pos = 0
        for s, e, k in self.regions:
            if pos + nb <= s:
                break
            pos = max(pos, (e + align - 1) // align * align)
        end = pos + nb
        assert end <= self.nb, f"arena OOM for {key}: need {nb}B at {pos}"
        self.hw = max(self.hw, end)
        conflicts = tuple(k for s, e, k in self.dead if s < end and e > pos)
        newdead = []
        for s, e, k in self.dead:
            if not (s < end and e > pos):
                newdead.append((s, e, k))
                continue
            if s < pos:
                newdead.append((s, pos, k))
            if e > end:
                newdead.append((end, e, k))
        self.dead = newdead
        self.regions.append((pos, end, key))
        return self.t.ap()[0:shape[0], pos:end].bitcast(dtype), conflicts

    def free(self, *keys):
        for key in keys:
            for i, (s, e, k) in enumerate(self.regions):
                if k == key:
                    self.regions.pop(i)
                    self.dead.append((s, e, k))
                    break
            else:
                raise KeyError(key)


def build_program(taps=()):
    taps = set(taps)
    nc = bass.Bass("TRN2", target_bir_lowering=False, debug=False)
    es = ExitStack()
    tap_names = []

    def ti(name, shape, dt=F32):
        return nc.dram_tensor(name, shape, dt, kind="ExternalInput").ap()

    x_in = ti("x_c", [T, D])
    wq_t = ti("wq_t", [D, 1024])
    wk_t = ti("wk_t", [D, 1024])
    wv_t = ti("wv_t", [D, 2048])
    wog_t = ti("wog_t", [D, 2048])
    wout_t = ti("wout_t", [2048, D])
    wif_t = ti("wif_t", [D, 16])
    bif_in = ti("bif", [1, 16])
    gamma_in = ti("gamma_row", [1, 2048])
    wgu_sh_in = ti("wgu_sh", [256, 16384])
    wproj_sh_in = ti("wproj_sh", [1024, 2048])
    ident_in = ti("ident", [128, 128])
    mask_in = ti("mask_ut", [128, 128])
    fold_in = ti("fold_mask", [1, 8])
    out_ext = nc.dram_tensor("out_c", [T, D], F32, kind="ExternalOutput").ap()

    d_wqgu_sh = nc.dram_tensor("d_wqgu_sh", [256, 16384], BF16).ap()
    d_wqgu = nc.dram_tensor("d_wqgu", [2048, 16384], BF16).ap()
    d_wqpr_sh = nc.dram_tensor("d_wqpr_sh", [1024, 2048], BF16).ap()
    d_wqpr = nc.dram_tensor("d_wqpr", [8192, 2048], BF16).ap()
    d_xnfm = nc.dram_tensor("d_xnfm", [D, T], BF16).ap()
    d_hnorm = nc.dram_tensor("d_hnorm", [T, 2048], BF16).ap()
    d_smix = nc.dram_tensor("d_smix", [T, D], F32).ap()
    d_sigo = nc.dram_tensor("d_sigo", [T, 2048], BF16).ap()
    d_hfm = nc.dram_tensor("d_hfm", [HF, T], BF16).ap()
    d_sum_c = nc.dram_tensor("d_sum_c", [NH * DQ, VA], F32).ap()
    d_sum_all = nc.dram_tensor("d_sum_all", [NCORES * NH * DQ, VA], F32).ap()
    d_scal_c = nc.dram_tensor("d_scal_c", [1, 16], F32).ap()
    d_scal_all = nc.dram_tensor("d_scal_all", [NCORES, 16], F32).ap()
    d_irow = nc.dram_tensor("d_irow", [1, 2048], F32).ap()

    S = Sched(nc, self_sync=SELF_SYNC_ALL)
    AR = Arena(nc, es, 206 * 1024)

    def sb(key, shape, dt=F32):
        ap, conflicts = AR.alloc(key, shape, dt)
        if conflicts:
            S.conf[key] = conflicts
        return ap, conflicts

    pbank = [es.enter_context(nc.psum_tensor(f"pb{i}", [128, 512], F32)).ap()
             for i in range(8)]

    def pbf(i, n=512):
        return pbank[i][:, 0:n]

    def pbb(i, n):
        return pbank[i].bitcast(BF16)[:, 0:n]

    def tap(name, src_ap, shape, dt, reads):
        if name not in taps:
            return
        o = nc.dram_tensor("tap_" + name, list(shape), dt, kind="ExternalOutput").ap()
        tap_names.append(name)
        S.dma("sync", o, src_ap, reads=reads, writes=("tapst_" + name,), tag="tap")
        return "tapst_" + name

    ident, _ = sb("ident", [128, 128], F32)
    identb, _ = sb("identb", [128, 128], BF16)
    maskut, _ = sb("maskut", [128, 128], F32)
    ones_col, _ = sb("ones_col", [128, 1], F32)
    ones_row, _ = sb("ones_row", [1, 128], F32)
    bif_row, _ = sb("bif_row", [1, 16], F32)
    fold_row, _ = sb("fold_row", [1, 8], F32)
    gtmp, _ = sb("gtmp", [1, 2048], F32)
    dump, _ = sb("dump", [128, 2048], F32)

    cvals = [1e-5, 1e-6, LN_SQ, RMS_EPS]
    ctile, _ = sb("cbias", [128, len(cvals)], F32)
    for i, v in enumerate(cvals):
        S.add("vector", lambda e, i=i, v=v: e.memset(ctile[:, i:i + 1], v), writes=("constb",))
        nc.const_aps.register(F32, v, ctile[:, i:i + 1])

    S.dma("sync", ident, ident_in, writes=("ident",))
    S.dma("sync", maskut, mask_in, writes=("maskut",))
    S.dma("sync", bif_row, bif_in, writes=("bif_row",))
    S.dma("sync", fold_row, fold_in, writes=("fold_row",))
    S.dma("sync", gtmp, gamma_in, writes=("gtmp",))
    S.add("vector", lambda e: e.tensor_copy(identb, ident), reads=("ident",), writes=("identb",))
    S.add("vector", lambda e: e.memset(ones_col, 1.0), writes=("ones_col",))
    S.add("vector", lambda e: e.memset(ones_row, 1.0), writes=("ones_row",))

    xt0, c0 = sb("xt0", [128, 2048], F32)
    xt1, c1 = sb("xt1", [128, 2048], F32)
    xnstage, _ = sb("xnstage", [128, 2048], BF16)
    scol0, _ = sb("scol0", [128, 2048], BF16)
    scol1, _ = sb("scol1", [128, 2048], BF16)
    lncol, _ = sb("lncol", [128, 8], F32)
    d_xnfm_v = d_xnfm.rearrange("(dc p) t -> p dc t", p=128)

    ln_spills = []
    for tc in range(16):
        xt, xk = ((xt0, "xt0"), (xt1, "xt1"))[tc % 2]
        scol, sck = ((scol0, "scol0"), (scol1, "scol1"))[tc % 2]
        S.dma("sync", xt, x_in[tc * 128:(tc + 1) * 128, :], writes=(xk,), tag=f"ldx{tc}")
        S.add("scalar", lambda e, xt=xt: e.activation(dump, xt, AF.Copy, accum_out=lncol[:, 0:1]),
              reads=(xk,), writes=("lnsum", "dump"), tag=f"lnA{tc}")
        S.add("scalar", lambda e, xt=xt: e.activation(dump, xt, AF.Square, accum_out=lncol[:, 1:2]),
              reads=(xk,), writes=("lnsq", "dump"), tag=f"lnB{tc}")
        S.add("vector", lambda e: e.tensor_scalar(lncol[:, 2:3], lncol[:, 0:1], 1.0 / 2048.0, None, ALU.mult),
              reads=("lnsum",), writes=("lnmu",))
        S.add("vector", lambda e: e.tensor_scalar(lncol[:, 3:4], lncol[:, 1:2], 1.0 / 2048.0, None, ALU.mult),
              reads=("lnsq",), writes=("lnex2",))
        S.add("vector", lambda e: e.tensor_mul(lncol[:, 4:5], lncol[:, 2:3], lncol[:, 2:3]),
              reads=("lnmu",), writes=("lnmu2",))
        S.add("vector", lambda e: e.tensor_sub(lncol[:, 4:5], lncol[:, 3:4], lncol[:, 4:5]),
              reads=("lnex2", "lnmu2"), writes=("lnvar",))
        S.add("scalar", lambda e: e.activation(lncol[:, 5:6], lncol[:, 4:5], AF.Ln, bias=1e-5),
              reads=("lnvar", "constb"), writes=("lnln",))
        S.add("scalar", lambda e: e.activation(lncol[:, 6:7], lncol[:, 5:6], AF.Exp, scale=-0.5),
              reads=("lnln",), writes=("lnrstd",))
        S.add("vector", lambda e, xt=xt: e.tensor_scalar(xnstage, xt, lncol[:, 2:3], lncol[:, 6:7],
                                                         ALU.subtract, ALU.mult),
              reads=(xk, "lnmu", "lnrstd"), writes=("xnstage",), tag=f"lnap{tc}")
        for dc in range(16):
            bk = dc % 4
            S.add("tensor", lambda e, dc=dc, bk=bk: e.transpose(
                pbb(bk, 128), xnstage[:, dc * 128:(dc + 1) * 128], identb),
                reads=("xnstage", "identb"), writes=(f"pb{bk}",), tag=f"xT{tc}", kind="tp")
            S.add("vector", lambda e, dc=dc, bk=bk, scol=scol: e.tensor_copy(
                scol[:, dc * 128:(dc + 1) * 128], pbb(bk, 128)),
                reads=(f"pb{bk}",), writes=(sck,), tag=f"xTe{tc}")
        S.dma("sync", d_xnfm_v[:, :, tc * 128:(tc + 1) * 128],
              scol.rearrange("p (dc t) -> p dc t", t=128), reads=(sck,),
              writes=(f"dxn{tc}",), tag=f"spxn{tc}")
        ln_spills.append(f"dxn{tc}")
        if tc == 14:
            tap("lncol", lncol, (128, 8), F32, ("lnmu", "lnrstd", "lnsum", "lnsq"))
            tap("xnstage", xnstage, (128, 2048), BF16, ("xnstage",))
            tap("xtile", xt, (128, 2048), F32, (xk,))
    tap("xn", scol0, (128, 2048), BF16, ("scol0",))

    e2, _ = sb("e2", [128, 2], F32)
    e2t, _ = sb("e2t", [2, 128], F32)
    S.add("vector", lambda e: e.memset(e2, 0.0), writes=("e2",))
    S.add("vector", lambda e: e.memset(e2[0:64, 0:1], 1.0), reads=("e2",), writes=("e2",))
    S.add("vector", lambda e: e.memset(e2[64:128, 1:2], 1.0), reads=("e2",), writes=("e2",))
    S.add("tensor", lambda e: e.transpose(pbf(5, 128)[0:2, :], e2, ident),
          reads=("e2", "ident"), writes=("pb5",), tag="e2t_T", kind="tp")
    S.add("vector", lambda e: e.tensor_copy(e2t, pbf(5, 128)[0:2, :]),
          reads=("pb5",), writes=("e2t",))

    qw_bf, _ = sb("qw_bf", [128, 2048], BF16)
    qy, _ = sb("qy", [128, 2048], BF16)
    qwq, _ = sb("qwq", [128, 2048], BF16)
    qm2b, _ = sb("qm2b", [128, 2048], BF16)
    qf1b, _ = sb("qf1b", [128, 2048], BF16)
    qf1, _ = sb("qf1", [128, 2048], F32)
    qf2, _ = sb("qf2", [128, 2048], F32)
    qm2, _ = sb("qm2", [128, 2048], F32)
    qsc, _ = sb("qsc", [2, 2048], F32)
    qs2, _ = sb("qs2", [2, 2048], F32)
    qrc, _ = sb("qrc", [2, 2048], F32)
    qsbt, _ = sb("qsbt", [2, 2048], BF16)
    qsb = qsbt

    def quant_tile(src, dst, rs):
        S.dma("gpsimd", qw_bf, src, writes=("qw_bf",), tag=rs)
        S.add("scalar", lambda e: e.activation(qf1, qw_bf, AF.Abs),
              reads=("qw_bf",), writes=("qf1",))
        for b in range(4):
            bs = slice(b * 512, (b + 1) * 512)
            S.add("tensor", lambda e, bs=bs: e.matmul(pbf(6)[0:2, :], e2, qf1[:, bs],
                                                      start=True, stop=True),
                  reads=("e2", "qf1"), writes=("pb6",), tag=rs + "s")
            S.add("scalar", lambda e, bs=bs: e.copy(qsc[0:2, bs], pbf(6)[0:2, :]),
                  reads=("pb6",), writes=("qsc",))
        S.add("vector", lambda e: e.tensor_single_scalar(qsb, qsc[0:2, :], 1.0 / 64.0, ALU.mult),
              reads=("qsc",), writes=("qscb",))
        S.add("vector", lambda e: e.tensor_single_scalar(qsb, qsb, float(BFnp(1e-8)), ALU.max),
              reads=("qscb",), writes=("qscb",))
        S.add("vector", lambda e: e.tensor_copy(qs2[0:2, :], qsb), reads=("qscb",), writes=("qs2",))
        S.add("vector", lambda e: e.reciprocal(qrc[0:2, :], qsb), reads=("qscb",), writes=("qrec",))
        for b in range(4):
            bs = slice(b * 512, (b + 1) * 512)
            S.add("tensor", lambda e, bs=bs: e.matmul(pbf(6), e2t, qrc[0:2, bs],
                                                      start=True, stop=True),
                  reads=("e2t", "qrec"), writes=("pb6",), tag=rs + "r")
            S.add("vector", lambda e, bs=bs: e.tensor_tensor(qy[:, bs], qw_bf[:, bs], pbf(6), ALU.mult),
                  reads=("qw_bf", "pb6"), writes=("qy",))
            S.add("tensor", lambda e, bs=bs: e.matmul(pbf(7), e2t, qs2[:, bs],
                                                      start=True, stop=True),
                  reads=("e2t", "qs2"), writes=("pb7",), tag=rs + "b")
            S.add("scalar", lambda e, bs=bs: e.copy(qf2[:, bs], pbf(7)),
                  reads=("pb7",), writes=("qf2",))
        S.add("vector", lambda e: e.tensor_single_scalar(qm2b, qy, 0.5, ALU.is_gt),
              reads=("qy",), writes=("qm2b",))
        S.add("vector", lambda e: e.tensor_single_scalar(qf1b, qy, -0.5, ALU.is_lt),
              reads=("qy",), writes=("qf1b",))
        S.add("vector", lambda e: e.tensor_sub(qm2b, qm2b, qf1b), reads=("qm2b", "qf1b"), writes=("qm2b",))
        S.add("vector", lambda e: e.tensor_mul(qm2, qm2b, qf2), reads=("qm2b", "qf2"), writes=("qm2",))
        S.add("vector", lambda e: e.tensor_sub(qy, qm2, qw_bf), reads=("qm2", "qw_bf"), writes=("qy",))
        S.add("vector", lambda e: e.tensor_add(qwq, qw_bf, qy), reads=("qw_bf", "qy"), writes=("qwq",))
        S.dma("sync", dst, qwq, reads=("qwq",), writes=(rs + "_st",), tag=rs)
        return rs + "_st"

    q_st = []
    for pt in range(2):
        for ct in range(8):
            q_st.append(quant_tile(wgu_sh_in[pt * 128:(pt + 1) * 128, ct * 2048:(ct + 1) * 2048],
                                   d_wqgu_sh[pt * 128:(pt + 1) * 128, ct * 2048:(ct + 1) * 2048],
                                   f"qg{pt}{ct}"))
            if pt == 0 and ct == 0:
                tap("qsc0", qsc[0:2, :], (2, 2048), F32, ("qsc",))
                tap("qsb0", qsbt[0:2, :], (2, 2048), BF16, ("qscb",))
                tap("qf10", qf1, (128, 2048), F32, ("qf1",))
                tap("qwbf0", qw_bf, (128, 2048), BF16, ("qw_bf",))
    for pt in range(8):
        q_st.append(quant_tile(wproj_sh_in[pt * 128:(pt + 1) * 128, :],
                               d_wqpr_sh[pt * 128:(pt + 1) * 128, :], f"qp{pt}"))

    AR.free("xt0", "xt1", "xnstage", "scol0", "scol1",
            "qw_bf", "qy", "qwq", "qf1", "qf2", "qm2", "qm2b", "qf1b", "qsc", "qs2", "qrc", "qsbt")


    q_tm, k_tm, v_aug = [], [], []
    for tc in range(16):
        a, _ = sb(f"qtm{tc}", [128, 1024], BF16)
        q_tm.append(a)
    for tc in range(16):
        a, _ = sb(f"ktm{tc}", [128, 1024], BF16)
        k_tm.append(a)
    for tc in range(16):
        a, _ = sb(f"vaug{tc}", [128, NH * VA], BF16)
        v_aug.append(a)
    wblk0, _ = sb("wblk0", [128, 16 * 512], BF16)
    wblk1, _ = sb("wblk1", [128, 16 * 512], BF16)
    xcols = []
    for i in range(2):
        a, _ = sb(f"xcol{i}", [128, 2048], BF16)
        xcols.append(a)
    stage_i, _ = sb("stage_i", [128, 128], F32)
    stage_f, _ = sb("stage_f", [128, 128], F32)
    wifb, _ = sb("wifb", [128, 16 * 16], BF16)

    for dc in range(16):
        S.dma("gpsimd", wifb[:, dc * 16:(dc + 1) * 16], wif_t[dc * 128:(dc + 1) * 128, :],
              writes=("wifb",), tag="ldwif")

    wsrc_v = {"q": wq_t.rearrange("(dc p) n -> p dc n", p=128),
              "k": wk_t.rearrange("(dc p) n -> p dc n", p=128),
              "v": wv_t.rearrange("(dc p) n -> p dc n", p=128),
              "og": wog_t.rearrange("(dc p) n -> p dc n", p=128)}
    wblk_i = [0]

    def load_wblk(which, b):
        wb, wk_ = ((wblk0, "wblk0"), (wblk1, "wblk1"))[wblk_i[0] % 2]
        wblk_i[0] += 1
        src = wsrc_v[which][:, :, b * 512:(b + 1) * 512]
        S.dma("gpsimd", wb.rearrange("p (dc n) -> p dc n", n=512), src,
              writes=(wk_,), tag=f"ldw_{which}{b}")
        return wb, wk_

    def load_xcol(tc, tag):
        xc, xk = xcols[tc % 2], f"xcol{tc % 2}"
        S.dma("sync", xc.rearrange("p (dc t) -> p dc t", t=128),
              d_xnfm_v[:, :, tc * 128:(tc + 1) * 128],
              reads=(f"dxn{tc}",), writes=(xk,), tag=tag)
        return xc, xk

    def proj_pass(which, ncols, evac):
        nblk = ncols // 512
        for b in range(nblk):
            wb, wk_ = load_wblk(which, b)
            for tc in range(16):
                xc, xk = load_xcol(tc, f"x_{which}{b}")
                bk = 2 + (tc % 2)
                for dc in range(16):
                    S.add("tensor", lambda e, dc=dc, bk=bk, wb=wb, xc=xc:
                          e.matmul(pbf(bk), xc[:, dc * 128:(dc + 1) * 128],
                                   wb[:, dc * 512:(dc + 1) * 512], start=(dc == 0), stop=(dc == 15)),
                          reads=(xk, wk_), writes=(f"pb{bk}",), tag=f"{which}mm{b}_{tc}")
                evac(tc, b, bk)

    def ev_q(tc, b, bk):
        S.add("vector", lambda e: e.tensor_copy(q_tm[tc][:, b * 512:(b + 1) * 512], pbf(bk)),
              reads=(f"pb{bk}",), writes=(f"qtm{tc}",), tag="evq")

    def ev_k(tc, b, bk):
        S.add("vector", lambda e: e.tensor_copy(k_tm[tc][:, b * 512:(b + 1) * 512], pbf(bk)),
              reads=(f"pb{bk}",), writes=(f"ktm{tc}",), tag="evk")

    for tc in range(16):
        S.add("vector", lambda e, tc=tc: e.memset(
            v_aug[tc].rearrange("p (h va) -> p h va", va=VA)[:, :, DV:DV + 1], 1.0),
            writes=(f"vaug{tc}",))

    def ev_v(tc, b, bk):
        for hh in range(2):
            h = 2 * b + hh
            S.add("vector", lambda e, h=h, hh=hh, bk=bk: e.tensor_copy(
                v_aug[tc][:, h * VA:h * VA + DV], pbf(bk)[:, hh * 256:(hh + 1) * 256]),
                reads=(f"pb{bk}",), writes=(f"vaug{tc}",), tag="evv")

    proj_pass("q", 1024, ev_q)
    proj_pass("k", 1024, ev_k)
    proj_pass("v", 2048, ev_v)

    for tc in range(16):
        xc, xk = load_xcol(tc, "x_if")
        for dc in range(16):
            S.add("tensor", lambda e, dc=dc, xc=xc: e.matmul(
                pbf(4, 16), xc[:, dc * 128:(dc + 1) * 128], wifb[:, dc * 16:(dc + 1) * 16],
                start=(dc == 0), stop=False),
                reads=(xk, "wifb"), writes=("pb4",), tag=f"ifmm{tc}")
        S.add("tensor", lambda e: e.matmul(pbf(4, 16), ones_row, bif_row, start=False, stop=True),
              reads=("ones_row", "bif_row"), writes=("pb4",), tag=f"ifb{tc}")
        S.add("scalar", lambda e: e.activation(pbf(5, 16), pbf(4, 16), AF.Tanh, scale=1.0 / 15.0),
              reads=("pb4",), writes=("pb5",), tag=f"iftanh{tc}")
        S.add("vector", lambda e, tc=tc: e.tensor_single_scalar(
            stage_i.rearrange("p (h c) -> p h c", c=16)[:, :, tc:tc + 1],
            pbf(5, 16)[:, 0:8].unsqueeze(2), 15.0, ALU.mult),
            reads=("pb5",), writes=("stage_i",), tag=f"ifi{tc}")
        S.add("vector", lambda e, tc=tc: e.tensor_single_scalar(
            stage_f.rearrange("p (h c) -> p h c", c=16)[:, :, tc:tc + 1],
            pbf(5, 16)[:, 8:16].unsqueeze(2), 15.0, ALU.mult),
            reads=("pb5",), writes=("stage_f",), tag=f"iff{tc}")

    tap("q0", q_tm[0], (128, 1024), BF16, ("qtm0",))
    tap("v0", v_aug[0], (128, NH * VA), BF16, ("vaug0",))
    tap("stage_i", stage_i, (128, 128), F32, ("stage_i",))

    AR.free("wblk1")

    g_i, _ = sb("g_i", [128, 128], F32)
    g_f, _ = sb("g_f", [128, 128], F32)
    g_b, _ = sb("g_b", [128, 128], F32)
    g_B, _ = sb("g_B", [128, 128], F32)
    g_at, _ = sb("g_at", [128, 128], F32)
    g_cmx, _ = sb("g_cmx", [128, 128], F32)
    g_s1, _ = sb("g_s1", [128, 128], F32)
    g_s2, _ = sb("g_s2", [128, 128], F32)

    def transpose128(dst, dst_key, src, src_keys, bk, dt=F32, tag="tp"):
        idt = ident if dt == F32 else identb
        ikey = "ident" if dt == F32 else "identb"
        po, pf = dst.shape[0], dst.shape[1]
        pb = (pbf(bk, pf) if dt == F32 else pbb(bk, pf))[0:po, :]
        S.add("tensor", lambda e: e.transpose(pb, src, idt[0:src.shape[0], 0:src.shape[0]]),
              reads=tuple(src_keys) + (ikey,), writes=(f"pb{bk}",), tag=tag, kind="tp")
        S.add("vector", lambda e: e.tensor_copy(dst, pb),
              reads=(f"pb{bk}",), writes=(dst_key,), tag=tag + "e")

    transpose128(g_i, "g_i", stage_i, ("stage_i",), 0, tag="tpi")
    transpose128(g_f, "g_f", stage_f, ("stage_f",), 1, tag="tpf")
    AR.free("stage_i", "stage_f")
    S.add("scalar", lambda e: e.activation(g_s1, g_f, AF.Abs),
          reads=("g_f",), writes=("g_s1",))
    S.add("scalar", lambda e: e.activation(g_s2, g_s1, AF.Exp, scale=-1.0),
          reads=("g_s1",), writes=("g_s2",))
    S.add("scalar", lambda e: e.activation(g_s1, g_s2, AF.Ln, bias=1.0),
          reads=("g_s2",), writes=("g_s1",))
    S.add("vector", lambda e: e.tensor_single_scalar(g_b, g_f, 0.0, ALU.min),
          reads=("g_f",), writes=("g_b",))
    S.add("vector", lambda e: e.tensor_sub(g_b, g_b, g_s1), reads=("g_b", "g_s1"), writes=("g_b",))
    S.add("vector", lambda e: e.tensor_tensor_scan(g_B, g_b, g_b, 0.0, ALU.add, ALU.bypass),
          reads=("g_b",), writes=("g_B",))
    S.add("vector", lambda e: e.tensor_sub(g_at, g_i, g_B), reads=("g_i", "g_B"), writes=("g_at",))
    S.add("vector", lambda e: e.tensor_tensor_scan(g_cmx, g_at, g_at, -1e30, ALU.max, ALU.max),
          reads=("g_at",), writes=("g_cmx",))

    rows = {}
    for rn in ("BL", "cmxL", "Pinc", "Pexc", "e_r", "Ghat", "mh_t", "mhat", "m0bc",
               "G_r", "Gnx", "lam", "d8", "m8", "sc16"):
        rows[rn], _ = sb("r_" + rn, [1, 128], F32)

    transpose128(rows["BL"], "r_BL", g_B[:, 127:128], ("g_B",), 2, tag="tBL")
    transpose128(rows["cmxL"], "r_cmxL", g_cmx[:, 127:128], ("g_cmx",), 3, tag="tcm")
    S.add("vector", lambda e: e.memset(rows["Pexc"], 0.0), writes=("r_Pexc",))
    S.add("vector", lambda e: e.memset(rows["Ghat"], -1e30), writes=("r_Ghat",))
    for h in range(NH):
        o = h * 16
        S.add("vector", lambda e, o=o: e.tensor_tensor_scan(
            rows["Pinc"][0:1, o:o + 16], rows["BL"][0:1, o:o + 16], rows["BL"][0:1, o:o + 16],
            0.0, ALU.add, ALU.bypass), reads=("r_BL",), writes=("r_Pinc",), tag="scP")
        S.add("vector", lambda e, o=o: e.tensor_copy(rows["Pexc"][0:1, o + 1:o + 16],
                                                     rows["Pinc"][0:1, o:o + 15]),
              reads=("r_Pinc",), writes=("r_Pexc",), tag="cpP")
    S.add("vector", lambda e: e.tensor_sub(rows["e_r"], rows["cmxL"], rows["Pexc"]),
          reads=("r_cmxL", "r_Pexc"), writes=("r_e_r",))
    for h in range(NH):
        o = h * 16
        S.add("vector", lambda e, o=o: e.tensor_tensor_scan(
            rows["Ghat"][0:1, o + 1:o + 16], rows["e_r"][0:1, o:o + 15], rows["e_r"][0:1, o:o + 15],
            -1e30, ALU.max, ALU.max), reads=("r_e_r",), writes=("r_Ghat",), tag="scG")
    S.add("vector", lambda e: e.tensor_max(rows["mh_t"], rows["Ghat"], rows["e_r"]),
          reads=("r_Ghat", "r_e_r"), writes=("r_mh_t",))
    v16 = lambda r: r.rearrange("o (h c) -> o h c", c=16)
    S.add("vector", lambda e: e.tensor_copy(
        v16(rows["mhat"]), v16(rows["mh_t"])[:, :, 15:16].broadcast_to([1, 8, 16])),
        reads=("r_mh_t",), writes=("r_mhat",))
    S.add("vector", lambda e: e.tensor_copy(
        v16(rows["d8"]), v16(rows["Pinc"])[:, :, 15:16].broadcast_to([1, 8, 16])),
        reads=("r_Pinc",), writes=("r_d8",))
    S.add("vector", lambda e: e.tensor_copy(rows["sc16"][0:1, 0:8],
                                            rows["d8"].rearrange("o (h c) -> o h c", c=16)[:, :, 0:1].rearrange("o h c -> o (h c)")),
          reads=("r_d8",), writes=("r_sc16",))
    S.add("vector", lambda e: e.tensor_copy(rows["sc16"][0:1, 8:16],
                                            rows["mhat"].rearrange("o (h c) -> o h c", c=16)[:, :, 0:1].rearrange("o h c -> o (h c)")),
          reads=("r_mhat",), writes=("r_sc16",))
    S.dma("sync", d_scal_c, rows["sc16"][0:1, 0:16], reads=("r_sc16",), writes=("d_scal_c",))

    def row2col(dst, dkey, row_ap, rkeys, bk, tag="r2c"):
        S.add("tensor", lambda e: e.matmul(pbf(bk, 1), row_ap, ones_row[0:1, 0:1],
                                           start=True, stop=True),
              reads=tuple(rkeys) + ("ones_row",), writes=(f"pb{bk}",), tag=tag)
        S.add("vector", lambda e: e.tensor_copy(dst, pbf(bk, 1)),
              reads=(f"pb{bk}",), writes=(dkey,), tag=tag + "e")

    cols = {}
    for cn in ("Pexc", "mhat", "G", "Gnx"):
        cols[cn], _ = sb("c_" + cn, [128, 1], F32)
    row2col(cols["Pexc"], "c_Pexc", rows["Pexc"], ("r_Pexc",), 4, tag="rcP")
    row2col(cols["mhat"], "c_mhat", rows["mhat"], ("r_mhat",), 4, tag="rcM")

    g_ks, _ = sb("g_ks", [128, 128], F32)
    S.add("vector", lambda e: e.tensor_scalar(g_s1, g_at, cols["Pexc"], cols["mhat"],
                                              ALU.subtract, ALU.subtract),
          reads=("g_at", "c_Pexc", "c_mhat"), writes=("g_s1",))
    S.add("scalar", lambda e: e.activation(g_ks, g_s1, AF.Exp), reads=("g_s1",), writes=("g_ks",))
    ks_t, _ = sb("ks_t", [128, 128], F32)
    transpose128(ks_t, "ks_t", g_ks, ("g_ks",), 5, tag="tKS")

    kw_s, _ = sb("kw_s", [128, 128], BF16)
    sum_stage, _ = sb("sum_stage", [128, VA], F32)
    sum_sts = []
    for h in range(NH):
        bk = 6 + (h % 2)
        for c in range(NC):
            S.add("vector", lambda e, h=h, c=c: e.tensor_scalar(
                kw_s, k_tm[c][:, h * 128:(h + 1) * 128], ks_t[:, h * 16 + c:h * 16 + c + 1],
                None, ALU.mult),
                reads=(f"ktm{c}", "ks_t"), writes=("kw_s",), tag=f"ksw{h}_{c}")
            S.add("tensor", lambda e, h=h, c=c, bk=bk: e.matmul(
                pbf(bk, VA), kw_s, v_aug[c][:, h * VA:(h + 1) * VA],
                start=(c == 0), stop=(c == NC - 1)),
                reads=("kw_s", f"vaug{c}"), writes=(f"pb{bk}",), tag=f"summ{h}")
        S.add("vector", lambda e, bk=bk: e.tensor_copy(sum_stage, pbf(bk, VA)),
              reads=(f"pb{bk}",), writes=("sum_stage",), tag=f"sume{h}")
        S.dma("sync", d_sum_c[h * 128:(h + 1) * 128, :], sum_stage,
              reads=("sum_stage",), writes=(f"dsum{h}",), tag=f"sumst{h}")
        sum_sts.append(f"dsum{h}")

    sgt_e = dump[:, 0:512]
    sgb_e = dump.bitcast(BF16)[:, 1024:1536]
    for b in range(4):
        wb, wk_ = wblk0, "wblk0"
        S.dma("gpsimd", wb.rearrange("p (dc n) -> p dc n", n=512),
              wsrc_v["og"][:, :, b * 512:(b + 1) * 512], writes=(wk_,), tag=f"ldwog{b}")
        for tc in range(16):
            xc, xk = load_xcol(tc, f"x_og{b}")
            bk = 2 + (tc % 2)
            for dc in range(16):
                S.add("tensor", lambda e, dc=dc, bk=bk, wb=wb, xc=xc: e.matmul(
                    pbf(bk), xc[:, dc * 128:(dc + 1) * 128], wb[:, dc * 512:(dc + 1) * 512],
                    start=(dc == 0), stop=(dc == 15)),
                    reads=(xk, wk_), writes=(f"pb{bk}",), tag=f"ogmm{b}_{tc}")
            S.add("scalar", lambda e, bk=bk: e.activation(sgt_e, pbf(bk), AF.Sigmoid),
                  reads=(f"pb{bk}",), writes=("dump",), tag=f"ogsig{b}_{tc}")
            S.add("vector", lambda e: e.tensor_copy(sgb_e, sgt_e), reads=("dump",), writes=("dump",))
            S.dma("sync", d_sigo[tc * 128:(tc + 1) * 128, b * 512:(b + 1) * 512], sgb_e,
                  reads=("dump",), writes=(f"dsg{b}_{tc}",), tag=f"stsg{b}_{tc}")


    S.add("gpsimd", lambda e: e.collective_compute(
        "AllGather", ALU.bypass, replica_groups=[list(range(NCORES))],
        ins=[d_wqgu_sh.opt()], outs=[d_wqgu.opt()]),
        reads=tuple(q_st[:16]), writes=("d_wqgu",), collective=True, tag="ag_gu")
    S.add("gpsimd", lambda e: e.collective_compute(
        "AllGather", ALU.bypass, replica_groups=[list(range(NCORES))],
        ins=[d_wqpr_sh.opt()], outs=[d_wqpr.opt()]),
        reads=tuple(q_st[16:]), writes=("d_wqpr",), collective=True, tag="ag_pr")

    S.add("gpsimd", lambda e: e.collective_compute(
        "AllGather", ALU.bypass, replica_groups=[list(range(NCORES))],
        ins=[d_scal_c.opt()], outs=[d_scal_all.opt()]),
        reads=("d_scal_c",), writes=("d_scal_all",), collective=True, tag="ag_sc")
    S.add("gpsimd", lambda e: e.collective_compute(
        "AllGather", ALU.bypass, replica_groups=[list(range(NCORES))],
        ins=[d_sum_c.opt()], outs=[d_sum_all.opt()]),
        reads=tuple(sum_sts), writes=("d_sum_all",), collective=True, tag="ag_sum")

    scal_all, _ = sb("scal_all", [1, 128], F32)
    S.dma("sync", scal_all, d_scal_all.rearrange("r k -> (r k)").unsqueeze(0),
          reads=("d_scal_all",), writes=("scal_all",), tag="ld_scal")
    fr = {}
    for rn in ("mst", "nu", "meff", "ea", "eb", "t1"):
        fr[rn], _ = sb("f_" + rn, [1, 8], F32)
    bc_ea, _ = sb("bc_ea", [128, 8], F32)
    bc_eb, _ = sb("bc_eb", [128, 8], F32)
    C_st, C_bf = [], []
    for h in range(NH):
        a, _ = sb(f"Cst{h}", [128, VA], F32)
        C_st.append(a)
        a, _ = sb(f"Cbf{h}", [128, VA], BF16)
        C_bf.append(a)
    csc, _ = sb("csc", [128, VA], F32)
    S.add("vector", lambda e: e.memset(fr["mst"], 0.0), writes=("f_mst",))
    for h in range(NH):
        S.add("vector", lambda e, h=h: e.memset(C_st[h], 0.0), writes=(f"Cst{h}",))
    for r in range(NCORES):
        dr = scal_all[0:1, r * 16:r * 16 + 8]
        mr = scal_all[0:1, r * 16 + 8:r * 16 + 16]
        mk_ = fold_row[0:1, r:r + 1]
        S.add("vector", lambda e, mk_=mk_: e.tensor_scalar(fr["t1"][0:1, 0:1], mk_, -1.0, 1e30,
                                                           ALU.add, ALU.mult),
              reads=("fold_row",), writes=("f_t1",), tag=f"fd{r}a0")
        S.add("vector", lambda e, mr=mr, mk_=mk_: e.tensor_scalar(fr["meff"], mr, mk_,
                                                                  fr["t1"][0:1, 0:1],
                                                                  ALU.mult, ALU.add),
              reads=("scal_all", "fold_row", "f_t1"), writes=("f_meff",), tag=f"fd{r}a")
        S.add("vector", lambda e: e.tensor_max(fr["nu"], fr["mst"], fr["meff"]),
              reads=("f_mst", "f_meff"), writes=("f_nu",))
        S.add("vector", lambda e: e.tensor_sub(fr["t1"], fr["mst"], fr["nu"]),
              reads=("f_mst", "f_nu"), writes=("f_t1",))
        S.add("scalar", lambda e: e.activation(fr["ea"], fr["t1"], AF.Exp),
              reads=("f_t1",), writes=("f_ea",))
        S.add("vector", lambda e: e.tensor_sub(fr["t1"], fr["meff"], fr["nu"]),
              reads=("f_meff", "f_nu"), writes=("f_t1",))
        S.add("scalar", lambda e: e.activation(fr["eb"], fr["t1"], AF.Exp),
              reads=("f_t1",), writes=("f_eb",))
        S.add("vector", lambda e, dr=dr, mk_=mk_: e.tensor_scalar(fr["t1"], dr, mk_, None, ALU.mult),
              reads=("scal_all", "fold_row"), writes=("f_t1",))
        S.add("vector", lambda e: e.tensor_add(fr["mst"], fr["nu"], fr["t1"]),
              reads=("f_nu", "f_t1"), writes=("f_mst",), tag=f"fd{r}m")
        S.add("tensor", lambda e: e.matmul(pbf(0, 8), ones_row, fr["ea"], start=True, stop=True),
              reads=("ones_row", "f_ea"), writes=("pb0",), tag=f"fd{r}ba")
        S.add("vector", lambda e: e.tensor_copy(bc_ea, pbf(0, 8)), reads=("pb0",), writes=("bc_ea",))
        S.add("tensor", lambda e: e.matmul(pbf(0, 8), ones_row, fr["eb"], start=True, stop=True),
              reads=("ones_row", "f_eb"), writes=("pb0",), tag=f"fd{r}bb")
        S.add("vector", lambda e: e.tensor_copy(bc_eb, pbf(0, 8)), reads=("pb0",), writes=("bc_eb",))
        for h in range(NH):
            S.dma("sync", csc, d_sum_all[(r * NH + h) * 128:(r * NH + h + 1) * 128, :],
                  reads=("d_sum_all",), writes=("csc",), tag=f"fl{r}_{h}")
            S.add("vector", lambda e, h=h: e.tensor_scalar(csc, csc, bc_eb[:, h:h + 1], None, ALU.mult),
                  reads=("csc", "bc_eb"), writes=("csc",))
            S.add("vector", lambda e, h=h: e.scalar_tensor_tensor(
                C_st[h], C_st[h], bc_ea[:, h:h + 1], csc, ALU.mult, ALU.add),
                reads=(f"Cst{h}", "bc_ea", "csc"), writes=(f"Cst{h}",), tag=f"fC{r}_{h}")
    for h in range(NH):
        S.add("vector", lambda e, h=h: e.tensor_copy(C_bf[h], C_st[h]),
              reads=(f"Cst{h}",), writes=(f"Cbf{h}",))
    S.add("vector", lambda e: e.tensor_copy(
        v16(rows["m0bc"]), fr["mst"].unsqueeze(2).broadcast_to([1, 8, 16])),
        reads=("f_mst",), writes=("r_m0bc",))
    tap("m0", fr["mst"], (1, 8), F32, ("f_mst",))
    tap("scal_all", scal_all, (1, 128), F32, ("scal_all",))
    tap("sc16", rows["sc16"], (1, 128), F32, ("r_sc16",))
    tap("C0", C_st[0], (128, VA), F32, ("Cst0",))

    AR.free("r_BL", "r_cmxL", "r_Pinc", "r_e_r", "r_d8", "r_m8", "r_sc16",
            "c_mhat", "g_i", "g_b", "g_ks", "ks_t", "kw_s", "sum_stage")

    S.add("vector", lambda e: e.tensor_max(rows["G_r"], rows["Ghat"], rows["m0bc"]),
          reads=("r_Ghat", "r_m0bc"), writes=("r_G_r",))
    for h in range(NH):
        o = h * 16
        S.add("vector", lambda e, o=o: e.tensor_copy(rows["Gnx"][0:1, o:o + 15],
                                                     rows["G_r"][0:1, o + 1:o + 16]),
              reads=("r_G_r",), writes=("r_Gnx",), tag="gnx")
    S.add("vector", lambda e: e.tensor_max(rows["mh_t"], rows["mhat"], rows["m0bc"]),
          reads=("r_mhat", "r_m0bc"), writes=("r_mh_t",))
    S.add("vector", lambda e: e.tensor_copy(
        v16(rows["Gnx"])[:, :, 15:16], v16(rows["mh_t"])[:, :, 15:16]),
        reads=("r_mh_t",), writes=("r_Gnx",))
    S.add("vector", lambda e: e.tensor_sub(rows["lam"], rows["G_r"], rows["Gnx"]),
          reads=("r_G_r", "r_Gnx"), writes=("r_lam",))
    S.add("scalar", lambda e: e.activation(rows["lam"], rows["lam"], AF.Exp),
          reads=("r_lam",), writes=("r_lam",))
    row2col(cols["G"], "c_G", rows["G_r"], ("r_G_r",), 4, tag="rcG")
    row2col(cols["Gnx"], "c_Gnx", rows["Gnx"], ("r_Gnx",), 4, tag="rcGn")

    g_S, _ = sb("g_S", [128, 128], F32)
    qf_t, _ = sb("qf_t", [128, 128], F32)
    row_t, _ = sb("row_t", [128, 128], F32)
    kw_t, _ = sb("kw_t", [128, 128], F32)
    enm_t, _ = sb("enm_t", [128, 128], F32)
    bclam, _ = sb("bclam", [128, 128], F32)

    S.add("vector", lambda e: e.tensor_scalar(g_S, g_cmx, cols["Pexc"], cols["G"],
                                              ALU.subtract, ALU.max),
          reads=("g_cmx", "c_Pexc", "c_G"), writes=("g_S",))
    S.add("vector", lambda e: e.tensor_scalar(g_s1, g_S, cols["G"], None, ALU.subtract),
          reads=("g_S", "c_G"), writes=("g_s1",))
    S.add("scalar", lambda e: e.activation(g_s2, g_s1, AF.Exp, scale=-1.0, bias=LN_SQ),
          reads=("g_s1", "constb"), writes=("g_s2",))
    transpose128(qf_t, "qf_t", g_s2, ("g_s2",), 5, tag="tQF")
    S.add("vector", lambda e: e.tensor_scalar(g_s1, g_at, cols["Pexc"], cols["G"],
                                              ALU.subtract, ALU.subtract),
          reads=("g_at", "c_Pexc", "c_G"), writes=("g_s1",))
    S.add("scalar", lambda e: e.activation(g_s2, g_s1, AF.Exp), reads=("g_s1",), writes=("g_s2",))
    transpose128(row_t, "row_t", g_s2, ("g_s2",), 5, tag="tRW")
    S.add("vector", lambda e: e.tensor_scalar(g_s1, g_at, cols["Pexc"], cols["Gnx"],
                                              ALU.subtract, ALU.subtract),
          reads=("g_at", "c_Pexc", "c_Gnx"), writes=("g_s1",))
    S.add("scalar", lambda e: e.activation(g_s2, g_s1, AF.Exp), reads=("g_s1",), writes=("g_s2",))
    transpose128(kw_t, "kw_t", g_s2, ("g_s2",), 5, tag="tKW")
    S.add("vector", lambda e: e.scalar_tensor_tensor(g_s1, g_B, cols["Pexc"], g_S,
                                                     ALU.add, ALU.add),
          reads=("g_B", "c_Pexc", "g_S"), writes=("g_s1",))
    S.add("scalar", lambda e: e.activation(g_s2, g_s1, AF.Exp, scale=-1.0),
          reads=("g_s1",), writes=("g_s2",))
    transpose128(enm_t, "enm_t", g_s2, ("g_s2",), 5, tag="tEN")
    S.add("tensor", lambda e: e.matmul(pbf(5, 128), ones_row, rows["lam"], start=True, stop=True),
          reads=("ones_row", "r_lam"), writes=("pb5",), tag="bclam")
    S.add("vector", lambda e: e.tensor_copy(bclam, pbf(5, 128)), reads=("pb5",), writes=("bclam",))
    tap("qf_t", qf_t, (128, 128), F32, ("qf_t",))
    tap("enm_t", enm_t, (128, 128), F32, ("enm_t",))

    AR.free("wblk0", "xcol0", "xcol1", "wifb")

    gamma_bc, _ = sb("gamma_bc", [128, 2048], F32)
    for b in range(4):
        S.add("tensor", lambda e, b=b: e.matmul(pbf(5), ones_row, gtmp[:, b * 512:(b + 1) * 512],
                                                start=True, stop=True),
              reads=("ones_row", "gtmp"), writes=("pb5",), tag="gbc")
        S.add("vector", lambda e, b=b: e.tensor_copy(gamma_bc[:, b * 512:(b + 1) * 512], pbf(5)),
              reads=("pb5",), writes=("gamma_bc",))
    qs_s, qfm, kfm, at_b, kws, dcol = [], [], [], [], [], []
    for p in range(2):
        a, _ = sb(f"qs_s{p}", [128, 128], BF16); qs_s.append(a)
        a, _ = sb(f"qfm{p}", [128, 128], BF16); qfm.append(a)
        a, _ = sb(f"kfm{p}", [128, 128], BF16); kfm.append(a)
        a, _ = sb(f"at_b{p}", [128, 128], BF16); at_b.append(a)
        a, _ = sb(f"kws{p}", [128, 128], BF16); kws.append(a)
        a, _ = sb(f"dcol{p}", [128, 4], F32); dcol.append(a)
    h_stage, _ = sb("h_stage", [128, 2048], F32)
    hn_bf, _ = sb("hn_bf", [128, 2048], BF16)
    mh1, _ = sb("mh1", [128, 8], F32)
    mh2, _ = sb("mh2", [128, 8], F32)
    mh3 = dump

    hn_spills = []
    for c in range(NC):
        for h in range(NH):
            hc = h * 16 + c
            it = (c * NH + h) % 2
            bPT, bHN, bCU = 2 + it, 4 + it, 6 + it
            qs, qf_, kf_, ab_, kw_, dc_ = (qs_s[it], qfm[it], kfm[it], at_b[it],
                                           kws[it], dcol[it])
            qsk, qfk, kfk, abk, kwk, dck = (f"qs_s{it}", f"qfm{it}", f"kfm{it}",
                                            f"at_b{it}", f"kws{it}", f"dcol{it}")
            S.add("vector", lambda e, c=c, h=h, hc=hc, qs=qs: e.tensor_scalar(
                qs, q_tm[c][:, h * 128:(h + 1) * 128], qf_t[:, hc:hc + 1], None, ALU.mult),
                reads=(f"qtm{c}", "qf_t"), writes=(qsk,), tag=f"qs{c}_{h}")
            transpose128(qf_, qfk, qs, (qsk,), 0, dt=BF16, tag=f"qT{c}_{h}")
            transpose128(kf_, kfk, k_tm[c][:, h * 128:(h + 1) * 128], (f"ktm{c}",), 1,
                         dt=BF16, tag=f"kT{c}_{h}")
            S.add("tensor", lambda e, kf_=kf_, qf_=qf_, bPT=bPT: e.matmul(
                pbf(bPT, 128), kf_, qf_, start=True, stop=True),
                reads=(kfk, qfk), writes=(f"pb{bPT}",), tag=f"PT{c}_{h}")
            S.add("vector", lambda e, hc=hc, ab_=ab_, bPT=bPT: e.scalar_tensor_tensor(
                ab_, pbf(bPT, 128), row_t[:, hc:hc + 1], maskut, ALU.mult, ALU.mult),
                reads=(f"pb{bPT}", "row_t", "maskut"), writes=(abk,), tag=f"AT{c}_{h}")
            S.add("tensor", lambda e, h=h, qf_=qf_, bHN=bHN: e.matmul(
                pbf(bHN, VA), qf_, C_bf[h], start=True, stop=False),
                reads=(qfk, f"Cbf{h}"), writes=(f"pb{bHN}",), tag=f"hin{c}_{h}")
            S.add("tensor", lambda e, c=c, h=h, ab_=ab_, bHN=bHN: e.matmul(
                pbf(bHN, VA), ab_, v_aug[c][:, h * VA:(h + 1) * VA],
                start=False, stop=True),
                reads=(abk, f"vaug{c}"), writes=(f"pb{bHN}",), tag=f"hia{c}_{h}")
            S.add("vector", lambda e, c=c, h=h, hc=hc, kw_=kw_: e.tensor_scalar(
                kw_, k_tm[c][:, h * 128:(h + 1) * 128], kw_t[:, hc:hc + 1], None, ALU.mult),
                reads=(f"ktm{c}", "kw_t"), writes=(kwk,), tag=f"kw{c}_{h}")
            S.add("tensor", lambda e, c=c, h=h, kw_=kw_, bCU=bCU: e.matmul(
                pbf(bCU, VA), kw_, v_aug[c][:, h * VA:(h + 1) * VA],
                start=True, stop=True),
                reads=(kwk, f"vaug{c}"), writes=(f"pb{bCU}",), tag=f"cu{c}_{h}")
            S.add("vector", lambda e, h=h, hc=hc, bCU=bCU: e.scalar_tensor_tensor(
                C_st[h], C_st[h], bclam[:, hc:hc + 1], pbf(bCU, VA), ALU.mult, ALU.add),
                reads=(f"Cst{h}", "bclam", f"pb{bCU}"), writes=(f"Cst{h}",), tag=f"cn{c}_{h}")
            S.add("vector", lambda e, h=h: e.tensor_copy(C_bf[h], C_st[h]),
                  reads=(f"Cst{h}",), writes=(f"Cbf{h}",), tag=f"cb{c}_{h}")
            S.add("scalar", lambda e, dc_=dc_, bHN=bHN: e.activation(
                dc_[:, 0:1], pbf(bHN, VA)[:, DV:DV + 1], AF.Abs),
                reads=(f"pb{bHN}",), writes=(dck + "a",))
            S.add("vector", lambda e, hc=hc, dc_=dc_: e.tensor_scalar(
                dc_[:, 1:2], dc_[:, 0:1], enm_t[:, hc:hc + 1], 1e-6, ALU.max, ALU.add),
                reads=(dck + "a", "enm_t"), writes=(dck + "b",))
            S.add("vector", lambda e, dc_=dc_: e.reciprocal(dc_[:, 2:3], dc_[:, 1:2]),
                  reads=(dck + "b",), writes=(dck + "c",))
            S.add("vector", lambda e, h=h, dc_=dc_, bHN=bHN: e.tensor_scalar(
                h_stage[:, h * 256:(h + 1) * 256], pbf(bHN, VA)[:, 0:DV], dc_[:, 2:3],
                None, ALU.mult),
                reads=(f"pb{bHN}", dck + "c"), writes=("h_stage",), tag=f"hd{c}_{h}")
        hsv = h_stage.rearrange("p (h v) -> p h v", v=256)
        S.add("vector", lambda e: e.tensor_reduce(mh1, hsv, AX.X, ALU.add),
              reads=("h_stage",), writes=("mh1",), tag=f"mn1{c}")
        S.add("scalar", lambda e: e.activation(mh3, h_stage, AF.Square),
              reads=("h_stage",), writes=("dump",), tag=f"mn2{c}")
        S.add("vector", lambda e: e.tensor_reduce(mh2, mh3.rearrange("p (h v) -> p h v", v=256),
                                                  AX.X, ALU.add),
              reads=("dump",), writes=("mh2",), tag=f"mn3{c}")
        S.add("vector", lambda e: e.tensor_scalar(mh1, mh1, 1.0 / 256.0, None, ALU.mult),
              reads=("mh1",), writes=("mh1",))
        S.add("vector", lambda e: e.tensor_scalar(mh2, mh2, 1.0 / 256.0, None, ALU.mult),
              reads=("mh2",), writes=("mh2",))
        S.add("vector", lambda e: e.tensor_mul(dump[:, 0:8], mh1, mh1), reads=("mh1",), writes=("dump",))
        S.add("vector", lambda e: e.tensor_sub(mh2, mh2, dump[:, 0:8]),
              reads=("mh2", "dump"), writes=("mh2",))
        S.add("scalar", lambda e: e.activation(mh2, mh2, AF.Ln, bias=1e-6),
              reads=("mh2", "constb"), writes=("mh2",))
        S.add("scalar", lambda e: e.activation(mh2, mh2, AF.Exp, scale=-0.5),
              reads=("mh2",), writes=("mh2",))
        S.add("vector", lambda e: e.tensor_tensor(hsv, hsv,
                                                  mh1.unsqueeze(2).broadcast_to([128, 8, 256]),
                                                  ALU.subtract),
              reads=("h_stage", "mh1"), writes=("h_stage",), tag=f"mn4{c}")
        S.add("vector", lambda e: e.tensor_tensor(hsv, hsv,
                                                  mh2.unsqueeze(2).broadcast_to([128, 8, 256]),
                                                  ALU.mult),
              reads=("h_stage", "mh2"), writes=("h_stage",), tag=f"mn5{c}")
        S.add("vector", lambda e: e.tensor_mul(hn_bf, h_stage, gamma_bc),
              reads=("h_stage", "gamma_bc"), writes=("hn_bf",), tag=f"mn6{c}")
        S.dma("sync", d_hnorm[c * 128:(c + 1) * 128, :], hn_bf, reads=("hn_bf",),
              writes=(f"dhn{c}",), tag=f"sthn{c}")
        hn_spills.append(f"dhn{c}")
        if c == 0:
            tap("h0", h_stage, (128, 2048), F32, ("h_stage",))

    AR.free(*[f"qtm{t}" for t in range(16)], *[f"ktm{t}" for t in range(16)],
            *[f"vaug{t}" for t in range(16)])
    AR.free("g_f", "g_B", "g_at", "g_cmx", "g_s1", "g_s2", "g_S",
            "scal_all", "bc_ea", "bc_eb",
            "csc", "qs_s0", "qs_s1", "qfm0", "qfm1", "kfm0", "kfm1", "at_b0", "at_b1", "kws0", "kws1", "dcol0", "dcol1",
            "h_stage", "hn_bf", "mh1", "mh2",
            "qf_t", "row_t", "kw_t", "enm_t", "bclam",
            *[f"Cst{h}" for h in range(8)], *[f"Cbf{h}" for h in range(8)],
            *["r_" + rn for rn in ("Pexc", "Ghat", "mh_t", "mhat", "m0bc",
                                   "G_r", "Gnx", "lam")],
            *["c_" + cn for cn in ("Pexc", "G", "Gnx")],
            *["f_" + rn for rn in ("mst", "nu", "meff", "ea", "eb", "t1")],
            "e2", "e2t")

    if "hnormfull" in taps:
        o = nc.dram_tensor("tap_hnormfull", [T, 2048], BF16, kind="ExternalOutput").ap()
        S.dma("sync", o, d_hnorm, reads=tuple(hn_spills), writes=("taphnf",))

    hout = []
    for tc in range(16):
        a, cf = sb(f"hout{tc}", [128, 2048], BF16)
        hout.append((a, cf))
    hnt0, _ = sb("hnt0", [128, 512], BF16)
    hnt1, _ = sb("hnt1", [128, 512], BF16)
    sgl0, _ = sb("sgl0", [128, 512], BF16)
    sgl1, _ = sb("sgl1", [128, 512], BF16)
    hob, _ = sb("hob", [128, 512], BF16)

    for b in range(4):
        for tc in range(16):
            hnt, hk = ((hnt0, "hnt0"), (hnt1, "hnt1"))[tc % 2]
            sgl, sk_ = ((sgl0, "sgl0"), (sgl1, "sgl1"))[tc % 2]
            S.dma("sync", hnt, d_hnorm[tc * 128:(tc + 1) * 128, b * 512:(b + 1) * 512],
                  reads=(f"dhn{tc}",), writes=(hk,), tag=f"ldhn{b}_{tc}")
            S.dma("gpsimd", sgl, d_sigo[tc * 128:(tc + 1) * 128, b * 512:(b + 1) * 512],
                  reads=(f"dsg{b}_{tc}",), writes=(sk_,), tag=f"ldsg{b}_{tc}")
            S.add("vector", lambda e, hnt=hnt, sgl=sgl: e.tensor_mul(hob, sgl, hnt),
                  reads=(sk_, hk), writes=("hob",), tag=f"ogm{b}_{tc}")
            for i in range(4):
                vc = b * 4 + i
                bk2 = 5 + (i % 2)
                S.add("tensor", lambda e, i=i, bk2=bk2: e.transpose(
                    pbb(bk2, 128), hob[:, i * 128:(i + 1) * 128], identb),
                    reads=("hob", "identb"), writes=(f"pb{bk2}",), tag=f"hoT{b}_{tc}_{i}", kind="tp")
                S.add("vector", lambda e, tc=tc, vc=vc, bk2=bk2: e.tensor_copy(
                    hout[tc][0][:, vc * 128:(vc + 1) * 128], pbb(bk2, 128)),
                    reads=(f"pb{bk2}",), writes=(f"hout{tc}",),
                    tag=f"hoTe{b}_{tc}_{i}")

    wblk0, _ = sb("wblk0", [128, 16 * 512], BF16)
    wblk1, _ = sb("wblk1", [128, 16 * 512], BF16)
    smst, _ = sb("smst", [128, 512], F32)
    wsrc_wout = wout_t.rearrange("(vc p) n -> p vc n", p=128)
    for b in range(4):
        wb, wk_ = ((wblk0, "wblk0"), (wblk1, "wblk1"))[b % 2]
        S.dma("gpsimd", wb.rearrange("p (vc n) -> p vc n", n=512),
              wsrc_wout[:, :, b * 512:(b + 1) * 512], writes=(wk_,), tag=f"ldwo{b}")
        for tc in range(16):
            bk = 2 + (tc % 2)
            for vc in range(16):
                S.add("tensor", lambda e, vc=vc, tc=tc, bk=bk, wb=wb: e.matmul(
                    pbf(bk), hout[tc][0][:, vc * 128:(vc + 1) * 128],
                    wb[:, vc * 512:(vc + 1) * 512], start=(vc == 0), stop=(vc == 15)),
                    reads=(f"hout{tc}", wk_), writes=(f"pb{bk}",), tag=f"womm{b}_{tc}")
            S.add("vector", lambda e, bk=bk: e.tensor_copy(smst, pbf(bk)),
                  reads=(f"pb{bk}",), writes=("smst",), tag=f"woev{b}_{tc}")
            S.dma("sync", d_smix[tc * 128:(tc + 1) * 128, b * 512:(b + 1) * 512], smst,
                  reads=("smst",), writes=(f"dsm{b}_{tc}",), tag=f"stsm{b}_{tc}")
    tap("smix", smst, (128, 512), F32, ("smst",))
    if "smixfull" in taps:
        o = nc.dram_tensor("tap_smixfull", [T, D], F32, kind="ExternalOutput").ap()
        S.dma("sync", o, d_smix, reads=tuple(f"dsm{b}_{t}" for b in range(4) for t in range(16)),
              writes=("tapsmf",))

    AR.free(*[f"hout{t}" for t in range(16)], "wblk0", "wblk1",
            "hnt0", "hnt1", "sgl0", "sgl1", "hob", "smst")

    xn_fm = []
    for dc in range(16):
        a, cf = sb(f"xnr{dc}", [128, 2048], BF16)
        xn_fm.append(a)
        S.dma("sync", a, d_xnfm[dc * 128:(dc + 1) * 128, :],
              reads=(f"dxn{dc}",), writes=(f"xnr{dc}",), tag=f"rldxn{dc}")
    wgu0, _ = sb("wgu0", [128, 2048 + 2048], BF16)
    wgu1, _ = sb("wgu1", [128, 2048 + 2048], BF16)
    sgate, _ = sb("sgate", [128, 512], F32)
    hsq, _ = sb("hsq", [128, 512], F32)
    hrow, _ = sb("hrow", [128, 2048], BF16)
    sqacc, _ = sb("sqacc", [128, 2048], F32)
    S.add("vector", lambda e: e.memset(sqacc, 0.0), writes=("sqacc",))
    wgu_v = d_wqgu.rearrange("(dc p) H -> p dc H", p=128)

    h_spills = []
    for g in range(64):
        wb, wk_ = ((wgu0, "wgu0"), (wgu1, "wgu1"))[g % 2]
        S.dma("sync", wb[:, 0:2048].rearrange("p (dc n) -> p dc n", n=128),
              wgu_v[:, :, g * 128:(g + 1) * 128],
              reads=("d_wqgu",), writes=(wk_,), tag=f"ldgu{g}")
        S.dma("sync", wb[:, 2048:4096].rearrange("p (dc n) -> p dc n", n=128),
              wgu_v[:, :, 8192 + g * 128:8192 + (g + 1) * 128],
              reads=("d_wqgu",), writes=(wk_,), tag=f"ldup{g}")
        for tb in range(4):
            bkg, bku = 2 + (tb % 2) * 2, 3 + (tb % 2) * 2
            for dc in range(16):
                S.add("tensor", lambda e, dc=dc, wb=wb, tb=tb, bkg=bkg: e.matmul(
                    pbf(bkg), wb[:, dc * 128:(dc + 1) * 128],
                    xn_fm[dc][:, tb * 512:(tb + 1) * 512], start=(dc == 0), stop=(dc == 15)),
                    reads=(wk_, f"xnr{dc}"), writes=(f"pb{bkg}",), tag=f"gum{g}_{tb}")
                S.add("tensor", lambda e, dc=dc, wb=wb, tb=tb, bku=bku: e.matmul(
                    pbf(bku), wb[:, 2048 + dc * 128:2048 + (dc + 1) * 128],
                    xn_fm[dc][:, tb * 512:(tb + 1) * 512], start=(dc == 0), stop=(dc == 15)),
                    reads=(wk_, f"xnr{dc}"), writes=(f"pb{bku}",), tag=f"upm{g}_{tb}")
            S.add("scalar", lambda e, bkg=bkg: e.activation(sgate, pbf(bkg), AF.Silu),
                  reads=(f"pb{bkg}",), writes=("sgate",), tag=f"gsil{g}_{tb}")
            S.add("vector", lambda e, tb=tb, bku=bku: e.tensor_tensor(hrow[:, tb * 512:(tb + 1) * 512],
                                                             sgate, pbf(bku), ALU.mult),
                  reads=("sgate", f"pb{bku}"), writes=("hrow",), tag=f"gmul{g}_{tb}")
            S.add("scalar", lambda e, tb=tb: e.activation(hsq, hrow[:, tb * 512:(tb + 1) * 512],
                                                          AF.Square),
                  reads=("hrow",), writes=("hsq",), tag=f"gsq{g}_{tb}")
            S.add("vector", lambda e, tb=tb: e.tensor_add(sqacc[:, tb * 512:(tb + 1) * 512],
                                                          sqacc[:, tb * 512:(tb + 1) * 512], hsq),
                  reads=("sqacc", "hsq"), writes=("sqacc",), tag=f"gacc{g}_{tb}")
        S.dma("sync", d_hfm[g * 128:(g + 1) * 128, :], hrow, reads=("hrow",),
              writes=(f"dh{g}",), tag=f"sth{g}")
        h_spills.append(f"dh{g}")

    if "wqgu" in taps:
        o = nc.dram_tensor("tap_wqgu", [256, 16384], BF16, kind="ExternalOutput").ap()
        S.dma("sync", o[0:128, :], d_wqgu[0:128, :], reads=("d_wqgu",), writes=("tapwq1",))
        S.dma("sync", o[128:256, :], d_wqgu[1024:1152, :], reads=("d_wqgu",), writes=("tapwq2",))
    if "wqsh" in taps:
        o2 = nc.dram_tensor("tap_wqsh", [256, 16384], BF16, kind="ExternalOutput").ap()
        S.dma("sync", o2, d_wqgu_sh, reads=tuple(q_st[:16]), writes=("tapwqsh",))
    tap("sqacc", sqacc, (128, 2048), F32, ("sqacc",))
    tap("hrow", hrow, (128, 2048), BF16, ("hrow",))

    irow, _ = sb("irow", [1, 2048], F32)
    for b in range(4):
        S.add("tensor", lambda e, b=b: e.matmul(pbf(4, 512)[0:1, :], ones_col,
                                                sqacc[:, b * 512:(b + 1) * 512],
                                                start=True, stop=True),
              reads=("ones_col", "sqacc"), writes=("pb4",), tag=f"rsum{b}")
        S.add("vector", lambda e, b=b: e.tensor_copy(irow[0:1, b * 512:(b + 1) * 512],
                                                     pbf(4, 512)[0:1, :]),
              reads=("pb4",), writes=("irow",))
    S.add("scalar", lambda e: e.activation(irow, irow, AF.Ln, scale=1.0 / 8192.0, bias=RMS_EPS),
          reads=("irow", "constb"), writes=("irow",))
    S.add("scalar", lambda e: e.activation(irow, irow, AF.Exp, scale=-0.5),
          reads=("irow",), writes=("irow",))
    S.dma("sync", d_irow, irow, reads=("irow",), writes=("d_irow",), tag="stir")
    inv16, _ = sb("inv16", [16, 128], F32)
    invc, _ = sb("invc", [128, 16], F32)
    S.dma("sync", inv16, d_irow.rearrange("o (a b) -> (o a) b", b=128),
          reads=("d_irow",), writes=("inv16",), tag="ldir")
    S.add("tensor", lambda e: e.transpose(pbf(4, 16), inv16, ident[0:16, 0:16]),
          reads=("inv16", "ident"), writes=("pb4",), tag="tpin", kind="tp")
    S.add("vector", lambda e: e.tensor_copy(invc, pbf(4, 16)), reads=("pb4",), writes=("invc",))
    tap("invc", invc, (128, 16), F32, ("invc",))

    AR.free(*[f"xnr{dc}" for dc in range(16)], "wgu0", "wgu1", "sgate", "hsq",
            "hrow", "sqacc")

    hblk = []
    for pari in range(2):
        row = []
        for i in range(4):
            a, cf = sb(f"hblk{pari}_{i}", [128, 16 * 512], BF16)
            row.append(a)
        hblk.append(row)
    wps = []
    for i in range(8):
        a, _ = sb(f"wp{i}", [128, 512], BF16)
        wps.append(a)
    ot0, _ = sb("ot0", [128, 512], F32)
    ot1, _ = sb("ot1", [128, 512], F32)
    xres0, _ = sb("xres0", [128, 512], F32)
    xres1, _ = sb("xres1", [128, 512], F32)
    sm0, _ = sb("sm0", [128, 512], F32)
    sm1, _ = sb("sm1", [128, 512], F32)
    gsil, _ = sb("gsil", [128, 512], F32)
    hfm_v = d_hfm.rearrange("(hc p) t -> p hc t", p=128)

    out_sts = []
    for tb in range(4):
        pari = tb % 2
        for i in range(4):
            S.dma("sync", hblk[pari][i].rearrange("p (hc t) -> p hc t", t=512),
                  hfm_v[:, i * 16:(i + 1) * 16, tb * 512:(tb + 1) * 512],
                  reads=tuple(h_spills), writes=(f"hblk{pari}_{i}",), tag=f"ldhb{tb}_{i}")
        for db in range(4):
            for hc in range(64):
                wp, wpk = wps[hc % 8], f"wp{hc % 8}"
                S.dma(("sync", "gpsimd")[hc % 2], wp,
                      d_wqpr[hc * 128:(hc + 1) * 128, db * 512:(db + 1) * 512],
                      reads=("d_wqpr",), writes=(wpk,), tag=f"ldwp{tb}_{db}_{hc}")
                for ts in range(4):
                    S.add("tensor", lambda e, hc=hc, ts=ts, wp=wp, pari=pari: e.matmul(
                        pbf(ts), hblk[pari][hc // 16][:, (hc % 16) * 512 + ts * 128:(hc % 16) * 512 + (ts + 1) * 128],
                        wp, start=(hc == 0), stop=(hc == 63)),
                        reads=(f"hblk{pari}_{hc // 16}", wpk), writes=(f"pb{ts}",), tag=f"prm{tb}_{db}_{ts}")
            for ts in range(4):
                tc = tb * 4 + ts
                ot, ok = ((ot0, "ot0"), (ot1, "ot1"))[ts % 2]
                xr, xk2 = ((xres0, "xres0"), (xres1, "xres1"))[ts % 2]
                smt, smk = ((sm0, "sm0"), (sm1, "sm1"))[ts % 2]
                S.dma("sync", xr, x_in[tc * 128:(tc + 1) * 128, db * 512:(db + 1) * 512],
                      writes=(xk2,), tag=f"ldx2{tb}_{db}_{ts}")
                S.dma("sync", smt, d_smix[tc * 128:(tc + 1) * 128, db * 512:(db + 1) * 512],
                      reads=(f"dsm{db}_{tc}",), writes=(smk,), tag=f"ldsm{tb}_{db}_{ts}")
                S.add("vector", lambda e, ts=ts, tc=tc, ot=ot: e.tensor_scalar(
                    ot, pbf(ts), invc[:, tc:tc + 1], None, ALU.mult),
                    reads=(f"pb{ts}", "invc"), writes=(ok,), tag=f"prs{tb}_{db}_{ts}")
                S.add("scalar", lambda e, ot=ot: e.activation(gsil, ot, AF.Silu),
                      reads=(ok,), writes=("gsil",), tag=f"prsil{tb}_{db}_{ts}")
                S.add("vector", lambda e, smt=smt, ot=ot: e.tensor_mul(ot, gsil, smt),
                      reads=("gsil", smk), writes=(ok,), tag=f"prg{tb}_{db}_{ts}")
                S.add("vector", lambda e, xr=xr, ot=ot: e.tensor_add(ot, ot, xr),
                      reads=(ok, xk2), writes=(ok,), tag=f"pradd{tb}_{db}_{ts}")
                st = S.dma("sync", out_ext[tc * 128:(tc + 1) * 128, db * 512:(db + 1) * 512], ot,
                           reads=(ok,), writes=(f"st{tb}_{db}_{ts}",), tag=f"stout{tb}_{db}_{ts}")
                out_sts.append(f"st{tb}_{db}_{ts}")

    S.barrier("vector", reads=tuple(out_sts), tag="final")

    sems = {}
    for sk in S.semkeys():
        name = "s_" + "_".join(str(x) for x in sk)
        sems[sk] = es.enter_context(nc.semaphore(name))
    with nc.Block() as block:
        S.emit(block, sems)
    es.close()
    st, nw = S.stats()
    return nc, dict(ops=st, waits=nw, arena_hw=AR.hw, taps=tap_names)


_cache = {}


def _get_program(taps=()):
    key = tuple(sorted(taps))
    if key not in _cache:
        _cache[key] = build_program(taps)
    return _cache[key]


def kernel(L0_attn_x, L0_attn_Wq, L0_attn_Wk, L0_attn_Wv, L0_attn_Wog, L0_attn_Wig,
           L0_attn_big, L0_attn_Wfg, L0_attn_bfg, L0_attn_gamma, L0_attn_Wout,
           L0_ffn_w_gate_up, L0_ffn_w_proj):
    F = np.float32
    x = np.ascontiguousarray(np.asarray(L0_attn_x, F).reshape(16384, 2048))
    com = dict(
        wq_t=np.ascontiguousarray(np.asarray(L0_attn_Wq, F).T),
        wk_t=np.ascontiguousarray(np.asarray(L0_attn_Wk, F).T),
        wv_t=np.ascontiguousarray(np.asarray(L0_attn_Wv, F).T),
        wog_t=np.ascontiguousarray(np.asarray(L0_attn_Wog, F).T),
        wout_t=np.ascontiguousarray(np.asarray(L0_attn_Wout, F).T),
        wif_t=np.ascontiguousarray(np.concatenate(
            [np.asarray(L0_attn_Wig, F).T, np.asarray(L0_attn_Wfg, F).T], axis=1)),
        bif=np.concatenate([np.asarray(L0_attn_big, F),
                            np.asarray(L0_attn_bfg, F)]).reshape(1, 16),
        gamma_row=np.ascontiguousarray(np.asarray(L0_attn_gamma, F).reshape(1, 2048)),
        ident=np.eye(128, dtype=F),
        mask_ut=np.triu(np.ones((128, 128), F)),
    )
    wgu_t = np.ascontiguousarray(np.asarray(L0_ffn_w_gate_up, F).T)
    wpr_t = np.ascontiguousarray(np.asarray(L0_ffn_w_proj, F).T)

    nc, info = _get_program(tuple(sorted(TAPS)))
    in_maps = []
    for c in range(NCORES):
        m = dict(com)
        m["x_c"] = x[c * T:(c + 1) * T]
        m["wgu_sh"] = np.ascontiguousarray(wgu_t[c * 256:(c + 1) * 256])
        m["wproj_sh"] = np.ascontiguousarray(wpr_t[c * 1024:(c + 1) * 1024])
        m["fold_mask"] = (np.arange(8) < c).astype(F).reshape(1, 8)
        in_maps.append(m)
    res = run_bass_kernel_spmd(nc, in_maps, core_ids=list(range(NCORES)))
    kernel.last_results = res
    globals()["last_results"] = res
    out = np.concatenate([res.results[c]["out_c"] for c in range(NCORES)], axis=0)
    return out.reshape(1, 16384, 2048)


# version 07
# speedup vs baseline: 1.0248x; 1.0248x over previous
import math
from collections import defaultdict
from contextlib import ExitStack

import numpy as np
import ml_dtypes
import concourse.bass as bass
import concourse.mybir as mybir
from concourse.bass_utils import run_bass_kernel_spmd

F32 = mybir.dt.float32
BF16 = mybir.dt.bfloat16
AF = mybir.ActivationFunctionType
ALU = mybir.AluOpType
AX = mybir.AxisListType
BFnp = ml_dtypes.bfloat16

NCORES = 8
T = 2048
D = 2048
NH, DQ, DV, L = 8, 128, 256, 128
NC = 16
HF = 8192
VA = DV + 1
LN_SQ = -0.5 * math.log(128.0)
RMS_EPS = 1.1920929e-7

ENGINES = ("sync", "gpsimd", "vector", "scalar", "tensor")
TAPS = set()
SELF_SYNC_ALL = False


class Op:
    __slots__ = ("engine", "fn", "reads", "writes", "inc", "cum", "deps", "tag",
                 "semkey")

    def __init__(self, engine, fn, reads, writes, inc, tag):
        self.engine, self.fn = engine, fn
        self.reads, self.writes = reads, writes
        self.inc, self.cum, self.tag = inc, 0, tag
        self.deps = {}
        self.semkey = None


NDMA_LANES = 8


class Sched:

    def __init__(self, nc, self_sync=False):
        self.nc = nc
        self.ops = []
        self.conf = {}
        self.self_sync = self_sync
        self._cum = defaultdict(int)
        self._dma_rr = defaultdict(int)
        self._cc_idx = 0
        self._last_write = {}
        self._readers = defaultdict(list)

    def add(self, engine, fn, reads=(), writes=(), dma=False, tag="", inc=None,
            kind="", collective=False):
        is_async = dma or collective
        writes = tuple(writes)
        extra = ()
        for w in writes:
            if w in self.conf:
                extra = extra + self.conf.pop(w)
        if collective:
            semkey = ("cc", self._cc_idx)
            self._cc_idx += 1
            the_inc = 1
        elif dma:
            lane = self._dma_rr[engine] % NDMA_LANES
            self._dma_rr[engine] += 1
            semkey = (engine, "d", lane)
            the_inc = 16
        else:
            semkey = (engine, "c")
            the_inc = 1
        if inc is not None:
            the_inc = inc
        op = Op(engine, fn, tuple(reads), writes + extra, the_inc, tag)
        op.semkey = semkey
        deps = {}

        def need(sk, c, a, k, peng):
            if peng == engine and not a and not self.self_sync:
                if engine not in ("vector", "scalar", "gpsimd") \
                        and not (k == "tp" or kind == "tp"):
                    return
            if a:
                c = self._cum[sk]
            if c > deps.get(sk, -1):
                deps[sk] = c

        for r in op.reads:
            lw = self._last_write.get(r)
            if lw:
                need(*lw)
        for w in op.writes:
            lw = self._last_write.get(w)
            if lw:
                need(*lw)
            for ent in self._readers.get(w, ()):
                need(*ent)
        op.deps = deps
        self._cum[semkey] += op.inc
        op.cum = self._cum[semkey]
        rec = (semkey, op.cum, is_async, kind, engine)
        for r in op.reads:
            self._readers[r].append(rec)
        for w in op.writes:
            self._last_write[w] = rec
            self._readers[w] = []
        self.ops.append(op)
        return op

    def dma(self, engine, out, in_, reads=(), writes=(), tag="", **kw):
        return self.add(engine, lambda eng: eng.dma_start(out=out, in_=in_, **kw),
                        reads=reads, writes=writes, dma=True, tag=tag)

    def barrier(self, engine, reads, tag="barrier"):
        return self.add(engine, lambda e: None, reads=reads, tag=tag, inc=0)

    def emit(self, block, sems):
        per_engine = defaultdict(list)
        for op in self.ops:
            per_engine[op.engine].append(op)

        def mk(engine_name, ops):
            def body(eng):
                watermark = defaultdict(int)
                for op in ops:
                    for sk, c in sorted(op.deps.items(), key=str):
                        if c > watermark[sk]:
                            eng.wait_ge(sems[sk], c)
                            watermark[sk] = c
                    ins = op.fn(eng)
                    if ins is not None and op.inc:
                        ins.then_inc(sems[op.semkey], op.inc)
            return body

        for e in ENGINES:
            if per_engine[e]:
                getattr(block, e)(mk(e, per_engine[e]))

    def semkeys(self):
        ks = set()
        for op in self.ops:
            ks.add(op.semkey)
            ks.update(op.deps.keys())
        return ks

    def stats(self):
        n = defaultdict(int)
        nw = 0
        for op in self.ops:
            n[op.engine] += 1
            nw += len(op.deps)
        return dict(n), nw


class Arena:
    def __init__(self, nc, es, nbytes, name="arena"):
        self.nb = nbytes
        self.t = es.enter_context(nc.sbuf_tensor(name, [128, nbytes], mybir.dt.int8))
        self.regions = []
        self.dead = []
        self.hw = 0

    def alloc(self, key, shape, dtype, align=128):
        esz = {BF16: 2, F32: 4}[dtype]
        nb = shape[1] * esz
        self.regions.sort()
        pos = 0
        for s, e, k in self.regions:
            if pos + nb <= s:
                break
            pos = max(pos, (e + align - 1) // align * align)
        end = pos + nb
        assert end <= self.nb, f"arena OOM for {key}: need {nb}B at {pos}"
        self.hw = max(self.hw, end)
        conflicts = tuple(k for s, e, k in self.dead if s < end and e > pos)
        newdead = []
        for s, e, k in self.dead:
            if not (s < end and e > pos):
                newdead.append((s, e, k))
                continue
            if s < pos:
                newdead.append((s, pos, k))
            if e > end:
                newdead.append((end, e, k))
        self.dead = newdead
        self.regions.append((pos, end, key))
        return self.t.ap()[0:shape[0], pos:end].bitcast(dtype), conflicts

    def free(self, *keys):
        for key in keys:
            for i, (s, e, k) in enumerate(self.regions):
                if k == key:
                    self.regions.pop(i)
                    self.dead.append((s, e, k))
                    break
            else:
                raise KeyError(key)


def build_program(taps=()):
    taps = set(taps)
    nc = bass.Bass("TRN2", target_bir_lowering=False, debug=False)
    es = ExitStack()
    tap_names = []

    def ti(name, shape, dt=F32):
        return nc.dram_tensor(name, shape, dt, kind="ExternalInput").ap()

    x_in = ti("x_c", [T, D])
    wq_t = ti("wq_t", [D, 1024])
    wk_t = ti("wk_t", [D, 1024])
    wv_t = ti("wv_t", [D, 2048])
    wog_t = ti("wog_t", [D, 2048])
    wout_t = ti("wout_t", [2048, D])
    wif_t = ti("wif_t", [D, 16])
    bif_in = ti("bif", [1, 16])
    gamma_in = ti("gamma_row", [1, 2048])
    wgu_sh_in = ti("wgu_sh", [256, 16384])
    wproj_sh_in = ti("wproj_sh", [1024, 2048])
    ident_in = ti("ident", [128, 128])
    mask_in = ti("mask_ut", [128, 128])
    fold_in = ti("fold_mask", [1, 8])
    out_ext = nc.dram_tensor("out_c", [T, D], F32, kind="ExternalOutput").ap()

    d_wqgu_sh = nc.dram_tensor("d_wqgu_sh", [256, 16384], BF16).ap()
    d_wqgu = nc.dram_tensor("d_wqgu", [2048, 16384], BF16).ap()
    d_wqpr_sh = nc.dram_tensor("d_wqpr_sh", [1024, 2048], BF16).ap()
    d_wqpr = nc.dram_tensor("d_wqpr", [8192, 2048], BF16).ap()
    d_xnfm = nc.dram_tensor("d_xnfm", [D, T], BF16).ap()
    d_hnorm = nc.dram_tensor("d_hnorm", [T, 2048], BF16).ap()
    d_smix = nc.dram_tensor("d_smix", [T, D], F32).ap()
    d_sigo = nc.dram_tensor("d_sigo", [T, 2048], BF16).ap()
    d_hfm = nc.dram_tensor("d_hfm", [HF, T], BF16).ap()
    d_sum_c = nc.dram_tensor("d_sum_c", [NH * DQ, VA], F32).ap()
    d_sum_all = nc.dram_tensor("d_sum_all", [NCORES * NH * DQ, VA], F32).ap()
    d_scal_c = nc.dram_tensor("d_scal_c", [1, 16], F32).ap()
    d_scal_all = nc.dram_tensor("d_scal_all", [NCORES, 16], F32).ap()
    d_irow = nc.dram_tensor("d_irow", [1, 2048], F32).ap()

    S = Sched(nc, self_sync=SELF_SYNC_ALL)
    AR = Arena(nc, es, 206 * 1024)

    def sb(key, shape, dt=F32):
        ap, conflicts = AR.alloc(key, shape, dt)
        if conflicts:
            S.conf[key] = conflicts
        return ap, conflicts

    pbank = [es.enter_context(nc.psum_tensor(f"pb{i}", [128, 512], F32)).ap()
             for i in range(8)]

    def pbf(i, n=512):
        return pbank[i][:, 0:n]

    def pbb(i, n):
        return pbank[i].bitcast(BF16)[:, 0:n]

    def tap(name, src_ap, shape, dt, reads):
        if name not in taps:
            return
        o = nc.dram_tensor("tap_" + name, list(shape), dt, kind="ExternalOutput").ap()
        tap_names.append(name)
        S.dma("sync", o, src_ap, reads=reads, writes=("tapst_" + name,), tag="tap")
        return "tapst_" + name

    ident, _ = sb("ident", [128, 128], F32)
    identb, _ = sb("identb", [128, 128], BF16)
    maskut, _ = sb("maskut", [128, 128], F32)
    ones_col, _ = sb("ones_col", [128, 1], F32)
    ones_row, _ = sb("ones_row", [1, 128], F32)
    bif_row, _ = sb("bif_row", [1, 16], F32)
    fold_row, _ = sb("fold_row", [1, 8], F32)
    gtmp, _ = sb("gtmp", [1, 2048], F32)
    dump, _ = sb("dump", [128, 2048], F32)

    cvals = [1e-5, 1e-6, LN_SQ, RMS_EPS]
    ctile, _ = sb("cbias", [128, len(cvals)], F32)
    for i, v in enumerate(cvals):
        S.add("vector", lambda e, i=i, v=v: e.memset(ctile[:, i:i + 1], v), writes=("constb",))
        nc.const_aps.register(F32, v, ctile[:, i:i + 1])

    S.dma("sync", ident, ident_in, writes=("ident",))
    S.dma("sync", maskut, mask_in, writes=("maskut",))
    S.dma("sync", bif_row, bif_in, writes=("bif_row",))
    S.dma("sync", fold_row, fold_in, writes=("fold_row",))
    S.dma("sync", gtmp, gamma_in, writes=("gtmp",))
    S.add("vector", lambda e: e.tensor_copy(identb, ident), reads=("ident",), writes=("identb",))
    S.add("vector", lambda e: e.memset(ones_col, 1.0), writes=("ones_col",))
    S.add("vector", lambda e: e.memset(ones_row, 1.0), writes=("ones_row",))

    xt0, c0 = sb("xt0", [128, 2048], F32)
    xt1, c1 = sb("xt1", [128, 2048], F32)
    xnstage, _ = sb("xnstage", [128, 2048], BF16)
    scol0, _ = sb("scol0", [128, 2048], BF16)
    scol1, _ = sb("scol1", [128, 2048], BF16)
    lncol, _ = sb("lncol", [128, 8], F32)
    d_xnfm_v = d_xnfm.rearrange("(dc p) t -> p dc t", p=128)

    ln_spills = []
    for tc in range(16):
        xt, xk = ((xt0, "xt0"), (xt1, "xt1"))[tc % 2]
        scol, sck = ((scol0, "scol0"), (scol1, "scol1"))[tc % 2]
        S.dma("sync", xt, x_in[tc * 128:(tc + 1) * 128, :], writes=(xk,), tag=f"ldx{tc}")
        S.add("scalar", lambda e, xt=xt: e.activation(dump, xt, AF.Copy, accum_out=lncol[:, 0:1]),
              reads=(xk,), writes=("lnsum", "dump"), tag=f"lnA{tc}")
        S.add("scalar", lambda e, xt=xt: e.activation(dump, xt, AF.Square, accum_out=lncol[:, 1:2]),
              reads=(xk,), writes=("lnsq", "dump"), tag=f"lnB{tc}")
        S.add("vector", lambda e: e.tensor_scalar(lncol[:, 2:3], lncol[:, 0:1], 1.0 / 2048.0, None, ALU.mult),
              reads=("lnsum",), writes=("lnmu",))
        S.add("vector", lambda e: e.tensor_scalar(lncol[:, 3:4], lncol[:, 1:2], 1.0 / 2048.0, None, ALU.mult),
              reads=("lnsq",), writes=("lnex2",))
        S.add("vector", lambda e: e.tensor_mul(lncol[:, 4:5], lncol[:, 2:3], lncol[:, 2:3]),
              reads=("lnmu",), writes=("lnmu2",))
        S.add("vector", lambda e: e.tensor_sub(lncol[:, 4:5], lncol[:, 3:4], lncol[:, 4:5]),
              reads=("lnex2", "lnmu2"), writes=("lnvar",))
        S.add("scalar", lambda e: e.activation(lncol[:, 5:6], lncol[:, 4:5], AF.Ln, bias=1e-5),
              reads=("lnvar", "constb"), writes=("lnln",))
        S.add("scalar", lambda e: e.activation(lncol[:, 6:7], lncol[:, 5:6], AF.Exp, scale=-0.5),
              reads=("lnln",), writes=("lnrstd",))
        S.add("vector", lambda e, xt=xt: e.tensor_scalar(xnstage, xt, lncol[:, 2:3], lncol[:, 6:7],
                                                         ALU.subtract, ALU.mult),
              reads=(xk, "lnmu", "lnrstd"), writes=("xnstage",), tag=f"lnap{tc}")
        for dc in range(16):
            bk = dc % 4
            S.add("tensor", lambda e, dc=dc, bk=bk: e.transpose(
                pbb(bk, 128), xnstage[:, dc * 128:(dc + 1) * 128], identb),
                reads=("xnstage", "identb"), writes=(f"pb{bk}",), tag=f"xT{tc}", kind="tp")
            S.add("vector", lambda e, dc=dc, bk=bk, scol=scol: e.tensor_copy(
                scol[:, dc * 128:(dc + 1) * 128], pbb(bk, 128)),
                reads=(f"pb{bk}",), writes=(sck,), tag=f"xTe{tc}")
        S.dma("sync", d_xnfm_v[:, :, tc * 128:(tc + 1) * 128],
              scol.rearrange("p (dc t) -> p dc t", t=128), reads=(sck,),
              writes=(f"dxn{tc}",), tag=f"spxn{tc}")
        ln_spills.append(f"dxn{tc}")
        if tc == 14:
            tap("lncol", lncol, (128, 8), F32, ("lnmu", "lnrstd", "lnsum", "lnsq"))
            tap("xnstage", xnstage, (128, 2048), BF16, ("xnstage",))
            tap("xtile", xt, (128, 2048), F32, (xk,))
    tap("xn", scol0, (128, 2048), BF16, ("scol0",))

    e2, _ = sb("e2", [128, 2], F32)
    e2t, _ = sb("e2t", [2, 128], F32)
    S.add("vector", lambda e: e.memset(e2, 0.0), writes=("e2",))
    S.add("vector", lambda e: e.memset(e2[0:64, 0:1], 1.0), reads=("e2",), writes=("e2",))
    S.add("vector", lambda e: e.memset(e2[64:128, 1:2], 1.0), reads=("e2",), writes=("e2",))
    S.add("tensor", lambda e: e.transpose(pbf(5, 128)[0:2, :], e2, ident),
          reads=("e2", "ident"), writes=("pb5",), tag="e2t_T", kind="tp")
    S.add("vector", lambda e: e.tensor_copy(e2t, pbf(5, 128)[0:2, :]),
          reads=("pb5",), writes=("e2t",))

    qw_bf, _ = sb("qw_bf", [128, 2048], BF16)
    qy, _ = sb("qy", [128, 2048], BF16)
    qwq, _ = sb("qwq", [128, 2048], BF16)
    qm2b, _ = sb("qm2b", [128, 2048], BF16)
    qf1b, _ = sb("qf1b", [128, 2048], BF16)
    qf1, _ = sb("qf1", [128, 2048], F32)
    qf2, _ = sb("qf2", [128, 2048], F32)
    qm2, _ = sb("qm2", [128, 2048], F32)
    qsc, _ = sb("qsc", [2, 2048], F32)
    qs2, _ = sb("qs2", [2, 2048], F32)
    qrc, _ = sb("qrc", [2, 2048], F32)
    qsbt, _ = sb("qsbt", [2, 2048], BF16)
    qsb = qsbt

    def quant_tile(src, dst, rs):
        S.dma("gpsimd", qw_bf, src, writes=("qw_bf",), tag=rs)
        S.add("scalar", lambda e: e.activation(qf1, qw_bf, AF.Abs),
              reads=("qw_bf",), writes=("qf1",))
        for b in range(4):
            bs = slice(b * 512, (b + 1) * 512)
            S.add("tensor", lambda e, bs=bs: e.matmul(pbf(6)[0:2, :], e2, qf1[:, bs],
                                                      start=True, stop=True),
                  reads=("e2", "qf1"), writes=("pb6",), tag=rs + "s")
            S.add("scalar", lambda e, bs=bs: e.copy(qsc[0:2, bs], pbf(6)[0:2, :]),
                  reads=("pb6",), writes=("qsc",))
        S.add("vector", lambda e: e.tensor_single_scalar(qsb, qsc[0:2, :], 1.0 / 64.0, ALU.mult),
              reads=("qsc",), writes=("qscb",))
        S.add("vector", lambda e: e.tensor_single_scalar(qsb, qsb, float(BFnp(1e-8)), ALU.max),
              reads=("qscb",), writes=("qscb",))
        S.add("vector", lambda e: e.tensor_copy(qs2[0:2, :], qsb), reads=("qscb",), writes=("qs2",))
        S.add("vector", lambda e: e.reciprocal(qrc[0:2, :], qsb), reads=("qscb",), writes=("qrec",))
        for b in range(4):
            bs = slice(b * 512, (b + 1) * 512)
            S.add("tensor", lambda e, bs=bs: e.matmul(pbf(6), e2t, qrc[0:2, bs],
                                                      start=True, stop=True),
                  reads=("e2t", "qrec"), writes=("pb6",), tag=rs + "r")
            S.add("vector", lambda e, bs=bs: e.tensor_tensor(qy[:, bs], qw_bf[:, bs], pbf(6), ALU.mult),
                  reads=("qw_bf", "pb6"), writes=("qy",))
            S.add("tensor", lambda e, bs=bs: e.matmul(pbf(7), e2t, qs2[:, bs],
                                                      start=True, stop=True),
                  reads=("e2t", "qs2"), writes=("pb7",), tag=rs + "b")
            S.add("scalar", lambda e, bs=bs: e.copy(qf2[:, bs], pbf(7)),
                  reads=("pb7",), writes=("qf2",))
        S.add("vector", lambda e: e.tensor_single_scalar(qm2b, qy, 0.5, ALU.is_gt),
              reads=("qy",), writes=("qm2b",))
        S.add("vector", lambda e: e.tensor_single_scalar(qf1b, qy, -0.5, ALU.is_lt),
              reads=("qy",), writes=("qf1b",))
        S.add("vector", lambda e: e.tensor_sub(qm2b, qm2b, qf1b), reads=("qm2b", "qf1b"), writes=("qm2b",))
        S.add("vector", lambda e: e.tensor_mul(qm2, qm2b, qf2), reads=("qm2b", "qf2"), writes=("qm2",))
        S.add("vector", lambda e: e.tensor_sub(qy, qm2, qw_bf), reads=("qm2", "qw_bf"), writes=("qy",))
        S.add("vector", lambda e: e.tensor_add(qwq, qw_bf, qy), reads=("qw_bf", "qy"), writes=("qwq",))
        S.dma("sync", dst, qwq, reads=("qwq",), writes=(rs + "_st",), tag=rs)
        return rs + "_st"

    q_st = []
    for pt in range(2):
        for ct in range(8):
            q_st.append(quant_tile(wgu_sh_in[pt * 128:(pt + 1) * 128, ct * 2048:(ct + 1) * 2048],
                                   d_wqgu_sh[pt * 128:(pt + 1) * 128, ct * 2048:(ct + 1) * 2048],
                                   f"qg{pt}{ct}"))
            if pt == 0 and ct == 0:
                tap("qsc0", qsc[0:2, :], (2, 2048), F32, ("qsc",))
                tap("qsb0", qsbt[0:2, :], (2, 2048), BF16, ("qscb",))
                tap("qf10", qf1, (128, 2048), F32, ("qf1",))
                tap("qwbf0", qw_bf, (128, 2048), BF16, ("qw_bf",))
    for pt in range(8):
        q_st.append(quant_tile(wproj_sh_in[pt * 128:(pt + 1) * 128, :],
                               d_wqpr_sh[pt * 128:(pt + 1) * 128, :], f"qp{pt}"))

    AR.free("xt0", "xt1", "xnstage", "scol0", "scol1",
            "qw_bf", "qy", "qwq", "qf1", "qf2", "qm2", "qm2b", "qf1b", "qsc", "qs2", "qrc", "qsbt")


    q_tm, k_tm, v_aug = [], [], []
    for tc in range(16):
        a, _ = sb(f"qtm{tc}", [128, 1024], BF16)
        q_tm.append(a)
    for tc in range(16):
        a, _ = sb(f"ktm{tc}", [128, 1024], BF16)
        k_tm.append(a)
    for tc in range(16):
        a, _ = sb(f"vaug{tc}", [128, NH * VA], BF16)
        v_aug.append(a)
    wblk0, _ = sb("wblk0", [128, 16 * 512], BF16)
    wblk1, _ = sb("wblk1", [128, 16 * 512], BF16)
    xcols = []
    for i in range(2):
        a, _ = sb(f"xcol{i}", [128, 2048], BF16)
        xcols.append(a)
    stage_i, _ = sb("stage_i", [128, 128], F32)
    stage_f, _ = sb("stage_f", [128, 128], F32)
    wifb, _ = sb("wifb", [128, 16 * 16], BF16)

    for dc in range(16):
        S.dma("gpsimd", wifb[:, dc * 16:(dc + 1) * 16], wif_t[dc * 128:(dc + 1) * 128, :],
              writes=("wifb",), tag="ldwif")

    wsrc_v = {"q": wq_t.rearrange("(dc p) n -> p dc n", p=128),
              "k": wk_t.rearrange("(dc p) n -> p dc n", p=128),
              "v": wv_t.rearrange("(dc p) n -> p dc n", p=128),
              "og": wog_t.rearrange("(dc p) n -> p dc n", p=128)}
    wblk_i = [0]

    def load_wblk(which, b):
        wb, wk_ = ((wblk0, "wblk0"), (wblk1, "wblk1"))[wblk_i[0] % 2]
        wblk_i[0] += 1
        src = wsrc_v[which][:, :, b * 512:(b + 1) * 512]
        S.dma("gpsimd", wb.rearrange("p (dc n) -> p dc n", n=512), src,
              writes=(wk_,), tag=f"ldw_{which}{b}")
        return wb, wk_

    def load_xcol(tc, tag):
        xc, xk = xcols[tc % 2], f"xcol{tc % 2}"
        S.dma("sync", xc.rearrange("p (dc t) -> p dc t", t=128),
              d_xnfm_v[:, :, tc * 128:(tc + 1) * 128],
              reads=(f"dxn{tc}",), writes=(xk,), tag=tag)
        return xc, xk

    def proj_pass(which, ncols, evac):
        nblk = ncols // 512
        for b in range(nblk):
            wb, wk_ = load_wblk(which, b)
            for tc in range(16):
                xc, xk = load_xcol(tc, f"x_{which}{b}")
                bk = 2 + (tc % 2)
                for dc in range(16):
                    S.add("tensor", lambda e, dc=dc, bk=bk, wb=wb, xc=xc:
                          e.matmul(pbf(bk), xc[:, dc * 128:(dc + 1) * 128],
                                   wb[:, dc * 512:(dc + 1) * 512], start=(dc == 0), stop=(dc == 15)),
                          reads=(xk, wk_), writes=(f"pb{bk}",), tag=f"{which}mm{b}_{tc}")
                evac(tc, b, bk)

    def ev_q(tc, b, bk):
        S.add("vector", lambda e: e.tensor_copy(q_tm[tc][:, b * 512:(b + 1) * 512], pbf(bk)),
              reads=(f"pb{bk}",), writes=(f"qtm{tc}",), tag="evq")

    def ev_k(tc, b, bk):
        S.add("vector", lambda e: e.tensor_copy(k_tm[tc][:, b * 512:(b + 1) * 512], pbf(bk)),
              reads=(f"pb{bk}",), writes=(f"ktm{tc}",), tag="evk")

    for tc in range(16):
        S.add("vector", lambda e, tc=tc: e.memset(
            v_aug[tc].rearrange("p (h va) -> p h va", va=VA)[:, :, DV:DV + 1], 1.0),
            writes=(f"vaug{tc}",))

    def ev_v(tc, b, bk):
        for hh in range(2):
            h = 2 * b + hh
            S.add("vector", lambda e, h=h, hh=hh, bk=bk: e.tensor_copy(
                v_aug[tc][:, h * VA:h * VA + DV], pbf(bk)[:, hh * 256:(hh + 1) * 256]),
                reads=(f"pb{bk}",), writes=(f"vaug{tc}",), tag="evv")

    proj_pass("q", 1024, ev_q)
    proj_pass("k", 1024, ev_k)
    proj_pass("v", 2048, ev_v)

    for tc in range(16):
        xc, xk = load_xcol(tc, "x_if")
        for dc in range(16):
            S.add("tensor", lambda e, dc=dc, xc=xc: e.matmul(
                pbf(4, 16), xc[:, dc * 128:(dc + 1) * 128], wifb[:, dc * 16:(dc + 1) * 16],
                start=(dc == 0), stop=False),
                reads=(xk, "wifb"), writes=("pb4",), tag=f"ifmm{tc}")
        S.add("tensor", lambda e: e.matmul(pbf(4, 16), ones_row, bif_row, start=False, stop=True),
              reads=("ones_row", "bif_row"), writes=("pb4",), tag=f"ifb{tc}")
        S.add("scalar", lambda e: e.activation(pbf(5, 16), pbf(4, 16), AF.Tanh, scale=1.0 / 15.0),
              reads=("pb4",), writes=("pb5",), tag=f"iftanh{tc}")
        S.add("vector", lambda e, tc=tc: e.tensor_single_scalar(
            stage_i.rearrange("p (h c) -> p h c", c=16)[:, :, tc:tc + 1],
            pbf(5, 16)[:, 0:8].unsqueeze(2), 15.0, ALU.mult),
            reads=("pb5",), writes=("stage_i",), tag=f"ifi{tc}")
        S.add("vector", lambda e, tc=tc: e.tensor_single_scalar(
            stage_f.rearrange("p (h c) -> p h c", c=16)[:, :, tc:tc + 1],
            pbf(5, 16)[:, 8:16].unsqueeze(2), 15.0, ALU.mult),
            reads=("pb5",), writes=("stage_f",), tag=f"iff{tc}")

    tap("q0", q_tm[0], (128, 1024), BF16, ("qtm0",))
    tap("v0", v_aug[0], (128, NH * VA), BF16, ("vaug0",))
    tap("stage_i", stage_i, (128, 128), F32, ("stage_i",))

    AR.free("wblk1")

    g_i, _ = sb("g_i", [128, 128], F32)
    g_f, _ = sb("g_f", [128, 128], F32)
    g_b, _ = sb("g_b", [128, 128], F32)
    g_B, _ = sb("g_B", [128, 128], F32)
    g_at, _ = sb("g_at", [128, 128], F32)
    g_cmx, _ = sb("g_cmx", [128, 128], F32)
    g_s1, _ = sb("g_s1", [128, 128], F32)
    g_s2, _ = sb("g_s2", [128, 128], F32)

    def transpose128(dst, dst_key, src, src_keys, bk, dt=F32, tag="tp"):
        idt = ident if dt == F32 else identb
        ikey = "ident" if dt == F32 else "identb"
        po, pf = dst.shape[0], dst.shape[1]
        pb = (pbf(bk, pf) if dt == F32 else pbb(bk, pf))[0:po, :]
        S.add("tensor", lambda e: e.transpose(pb, src, idt[0:src.shape[0], 0:src.shape[0]]),
              reads=tuple(src_keys) + (ikey,), writes=(f"pb{bk}",), tag=tag, kind="tp")
        S.add("vector", lambda e: e.tensor_copy(dst, pb),
              reads=(f"pb{bk}",), writes=(dst_key,), tag=tag + "e")

    transpose128(g_i, "g_i", stage_i, ("stage_i",), 0, tag="tpi")
    transpose128(g_f, "g_f", stage_f, ("stage_f",), 1, tag="tpf")
    AR.free("stage_i", "stage_f")
    S.add("scalar", lambda e: e.activation(g_s1, g_f, AF.Abs),
          reads=("g_f",), writes=("g_s1",))
    S.add("scalar", lambda e: e.activation(g_s2, g_s1, AF.Exp, scale=-1.0),
          reads=("g_s1",), writes=("g_s2",))
    S.add("scalar", lambda e: e.activation(g_s1, g_s2, AF.Ln, bias=1.0),
          reads=("g_s2",), writes=("g_s1",))
    S.add("vector", lambda e: e.tensor_single_scalar(g_b, g_f, 0.0, ALU.min),
          reads=("g_f",), writes=("g_b",))
    S.add("vector", lambda e: e.tensor_sub(g_b, g_b, g_s1), reads=("g_b", "g_s1"), writes=("g_b",))
    S.add("vector", lambda e: e.tensor_tensor_scan(g_B, g_b, g_b, 0.0, ALU.add, ALU.bypass),
          reads=("g_b",), writes=("g_B",))
    S.add("vector", lambda e: e.tensor_sub(g_at, g_i, g_B), reads=("g_i", "g_B"), writes=("g_at",))
    S.add("vector", lambda e: e.tensor_tensor_scan(g_cmx, g_at, g_at, -1e30, ALU.max, ALU.max),
          reads=("g_at",), writes=("g_cmx",))

    rows = {}
    for rn in ("BL", "cmxL", "Pinc", "Pexc", "e_r", "Ghat", "mh_t", "mhat", "m0bc",
               "G_r", "Gnx", "lam", "d8", "m8", "sc16"):
        rows[rn], _ = sb("r_" + rn, [1, 128], F32)

    transpose128(rows["BL"], "r_BL", g_B[:, 127:128], ("g_B",), 2, tag="tBL")
    transpose128(rows["cmxL"], "r_cmxL", g_cmx[:, 127:128], ("g_cmx",), 3, tag="tcm")
    S.add("vector", lambda e: e.memset(rows["Pexc"], 0.0), writes=("r_Pexc",))
    S.add("vector", lambda e: e.memset(rows["Ghat"], -1e30), writes=("r_Ghat",))
    for h in range(NH):
        o = h * 16
        S.add("vector", lambda e, o=o: e.tensor_tensor_scan(
            rows["Pinc"][0:1, o:o + 16], rows["BL"][0:1, o:o + 16], rows["BL"][0:1, o:o + 16],
            0.0, ALU.add, ALU.bypass), reads=("r_BL",), writes=("r_Pinc",), tag="scP")
        S.add("vector", lambda e, o=o: e.tensor_copy(rows["Pexc"][0:1, o + 1:o + 16],
                                                     rows["Pinc"][0:1, o:o + 15]),
              reads=("r_Pinc",), writes=("r_Pexc",), tag="cpP")
    S.add("vector", lambda e: e.tensor_sub(rows["e_r"], rows["cmxL"], rows["Pexc"]),
          reads=("r_cmxL", "r_Pexc"), writes=("r_e_r",))
    for h in range(NH):
        o = h * 16
        S.add("vector", lambda e, o=o: e.tensor_tensor_scan(
            rows["Ghat"][0:1, o + 1:o + 16], rows["e_r"][0:1, o:o + 15], rows["e_r"][0:1, o:o + 15],
            -1e30, ALU.max, ALU.max), reads=("r_e_r",), writes=("r_Ghat",), tag="scG")
    S.add("vector", lambda e: e.tensor_max(rows["mh_t"], rows["Ghat"], rows["e_r"]),
          reads=("r_Ghat", "r_e_r"), writes=("r_mh_t",))
    v16 = lambda r: r.rearrange("o (h c) -> o h c", c=16)
    S.add("vector", lambda e: e.tensor_copy(
        v16(rows["mhat"]), v16(rows["mh_t"])[:, :, 15:16].broadcast_to([1, 8, 16])),
        reads=("r_mh_t",), writes=("r_mhat",))
    S.add("vector", lambda e: e.tensor_copy(
        v16(rows["d8"]), v16(rows["Pinc"])[:, :, 15:16].broadcast_to([1, 8, 16])),
        reads=("r_Pinc",), writes=("r_d8",))
    S.add("vector", lambda e: e.tensor_copy(rows["sc16"][0:1, 0:8],
                                            rows["d8"].rearrange("o (h c) -> o h c", c=16)[:, :, 0:1].rearrange("o h c -> o (h c)")),
          reads=("r_d8",), writes=("r_sc16",))
    S.add("vector", lambda e: e.tensor_copy(rows["sc16"][0:1, 8:16],
                                            rows["mhat"].rearrange("o (h c) -> o h c", c=16)[:, :, 0:1].rearrange("o h c -> o (h c)")),
          reads=("r_mhat",), writes=("r_sc16",))
    S.dma("sync", d_scal_c, rows["sc16"][0:1, 0:16], reads=("r_sc16",), writes=("d_scal_c",))

    def row2col(dst, dkey, row_ap, rkeys, bk, tag="r2c"):
        S.add("tensor", lambda e: e.matmul(pbf(bk, 1), row_ap, ones_row[0:1, 0:1],
                                           start=True, stop=True),
              reads=tuple(rkeys) + ("ones_row",), writes=(f"pb{bk}",), tag=tag)
        S.add("vector", lambda e: e.tensor_copy(dst, pbf(bk, 1)),
              reads=(f"pb{bk}",), writes=(dkey,), tag=tag + "e")

    cols = {}
    for cn in ("Pexc", "mhat", "G", "Gnx"):
        cols[cn], _ = sb("c_" + cn, [128, 1], F32)
    row2col(cols["Pexc"], "c_Pexc", rows["Pexc"], ("r_Pexc",), 4, tag="rcP")
    row2col(cols["mhat"], "c_mhat", rows["mhat"], ("r_mhat",), 4, tag="rcM")

    g_ks, _ = sb("g_ks", [128, 128], F32)
    S.add("vector", lambda e: e.tensor_scalar(g_s1, g_at, cols["Pexc"], cols["mhat"],
                                              ALU.subtract, ALU.subtract),
          reads=("g_at", "c_Pexc", "c_mhat"), writes=("g_s1",))
    S.add("scalar", lambda e: e.activation(g_ks, g_s1, AF.Exp), reads=("g_s1",), writes=("g_ks",))
    ks_t, _ = sb("ks_t", [128, 128], F32)
    transpose128(ks_t, "ks_t", g_ks, ("g_ks",), 5, tag="tKS")

    kw_s, _ = sb("kw_s", [128, 128], BF16)
    sum_stage, _ = sb("sum_stage", [128, VA], F32)
    sum_sts = []
    for h in range(NH):
        bk = 6 + (h % 2)
        for c in range(NC):
            S.add("vector", lambda e, h=h, c=c: e.tensor_scalar(
                kw_s, k_tm[c][:, h * 128:(h + 1) * 128], ks_t[:, h * 16 + c:h * 16 + c + 1],
                None, ALU.mult),
                reads=(f"ktm{c}", "ks_t"), writes=("kw_s",), tag=f"ksw{h}_{c}")
            S.add("tensor", lambda e, h=h, c=c, bk=bk: e.matmul(
                pbf(bk, VA), kw_s, v_aug[c][:, h * VA:(h + 1) * VA],
                start=(c == 0), stop=(c == NC - 1)),
                reads=("kw_s", f"vaug{c}"), writes=(f"pb{bk}",), tag=f"summ{h}")
        S.add("vector", lambda e, bk=bk: e.tensor_copy(sum_stage, pbf(bk, VA)),
              reads=(f"pb{bk}",), writes=("sum_stage",), tag=f"sume{h}")
        S.dma("sync", d_sum_c[h * 128:(h + 1) * 128, :], sum_stage,
              reads=("sum_stage",), writes=(f"dsum{h}",), tag=f"sumst{h}")
        sum_sts.append(f"dsum{h}")

    sgt_e = dump[:, 0:512]
    sgb_e = dump.bitcast(BF16)[:, 1024:1536]
    for b in range(4):
        wb, wk_ = wblk0, "wblk0"
        S.dma("gpsimd", wb.rearrange("p (dc n) -> p dc n", n=512),
              wsrc_v["og"][:, :, b * 512:(b + 1) * 512], writes=(wk_,), tag=f"ldwog{b}")
        for tc in range(16):
            xc, xk = load_xcol(tc, f"x_og{b}")
            bk = 2 + (tc % 2)
            for dc in range(16):
                S.add("tensor", lambda e, dc=dc, bk=bk, wb=wb, xc=xc: e.matmul(
                    pbf(bk), xc[:, dc * 128:(dc + 1) * 128], wb[:, dc * 512:(dc + 1) * 512],
                    start=(dc == 0), stop=(dc == 15)),
                    reads=(xk, wk_), writes=(f"pb{bk}",), tag=f"ogmm{b}_{tc}")
            S.add("scalar", lambda e, bk=bk: e.activation(sgt_e, pbf(bk), AF.Sigmoid),
                  reads=(f"pb{bk}",), writes=("dump",), tag=f"ogsig{b}_{tc}")
            S.add("vector", lambda e: e.tensor_copy(sgb_e, sgt_e), reads=("dump",), writes=("dump",))
            S.dma("sync", d_sigo[tc * 128:(tc + 1) * 128, b * 512:(b + 1) * 512], sgb_e,
                  reads=("dump",), writes=(f"dsg{b}_{tc}",), tag=f"stsg{b}_{tc}")


    S.add("gpsimd", lambda e: e.collective_compute(
        "AllGather", ALU.bypass, replica_groups=[list(range(NCORES))],
        ins=[d_wqgu_sh.opt()], outs=[d_wqgu.opt()]),
        reads=tuple(q_st[:16]), writes=("d_wqgu",), collective=True, tag="ag_gu")
    S.add("gpsimd", lambda e: e.collective_compute(
        "AllGather", ALU.bypass, replica_groups=[list(range(NCORES))],
        ins=[d_wqpr_sh.opt()], outs=[d_wqpr.opt()]),
        reads=tuple(q_st[16:]), writes=("d_wqpr",), collective=True, tag="ag_pr")

    S.add("gpsimd", lambda e: e.collective_compute(
        "AllGather", ALU.bypass, replica_groups=[list(range(NCORES))],
        ins=[d_scal_c.opt()], outs=[d_scal_all.opt()]),
        reads=("d_scal_c",), writes=("d_scal_all",), collective=True, tag="ag_sc")
    S.add("gpsimd", lambda e: e.collective_compute(
        "AllGather", ALU.bypass, replica_groups=[list(range(NCORES))],
        ins=[d_sum_c.opt()], outs=[d_sum_all.opt()]),
        reads=tuple(sum_sts), writes=("d_sum_all",), collective=True, tag="ag_sum")

    scal_all, _ = sb("scal_all", [1, 128], F32)
    S.dma("sync", scal_all, d_scal_all.rearrange("r k -> (r k)").unsqueeze(0),
          reads=("d_scal_all",), writes=("scal_all",), tag="ld_scal")
    fr = {}
    for rn in ("mst", "nu", "meff", "ea", "eb", "t1"):
        fr[rn], _ = sb("f_" + rn, [1, 8], F32)
    bc_ea, _ = sb("bc_ea", [128, 8], F32)
    bc_eb, _ = sb("bc_eb", [128, 8], F32)
    C_st, C_bf = [], []
    for h in range(NH):
        a, _ = sb(f"Cst{h}", [128, VA], F32)
        C_st.append(a)
        a, _ = sb(f"Cbf{h}", [128, VA], BF16)
        C_bf.append(a)
    csc, _ = sb("csc", [128, VA], F32)
    S.add("vector", lambda e: e.memset(fr["mst"], 0.0), writes=("f_mst",))
    for h in range(NH):
        S.add("vector", lambda e, h=h: e.memset(C_st[h], 0.0), writes=(f"Cst{h}",))
    for r in range(NCORES):
        dr = scal_all[0:1, r * 16:r * 16 + 8]
        mr = scal_all[0:1, r * 16 + 8:r * 16 + 16]
        mk_ = fold_row[0:1, r:r + 1]
        S.add("vector", lambda e, mk_=mk_: e.tensor_scalar(fr["t1"][0:1, 0:1], mk_, -1.0, 1e30,
                                                           ALU.add, ALU.mult),
              reads=("fold_row",), writes=("f_t1",), tag=f"fd{r}a0")
        S.add("vector", lambda e, mr=mr, mk_=mk_: e.tensor_scalar(fr["meff"], mr, mk_,
                                                                  fr["t1"][0:1, 0:1],
                                                                  ALU.mult, ALU.add),
              reads=("scal_all", "fold_row", "f_t1"), writes=("f_meff",), tag=f"fd{r}a")
        S.add("vector", lambda e: e.tensor_max(fr["nu"], fr["mst"], fr["meff"]),
              reads=("f_mst", "f_meff"), writes=("f_nu",))
        S.add("vector", lambda e: e.tensor_sub(fr["t1"], fr["mst"], fr["nu"]),
              reads=("f_mst", "f_nu"), writes=("f_t1",))
        S.add("scalar", lambda e: e.activation(fr["ea"], fr["t1"], AF.Exp),
              reads=("f_t1",), writes=("f_ea",))
        S.add("vector", lambda e: e.tensor_sub(fr["t1"], fr["meff"], fr["nu"]),
              reads=("f_meff", "f_nu"), writes=("f_t1",))
        S.add("scalar", lambda e: e.activation(fr["eb"], fr["t1"], AF.Exp),
              reads=("f_t1",), writes=("f_eb",))
        S.add("vector", lambda e, dr=dr, mk_=mk_: e.tensor_scalar(fr["t1"], dr, mk_, None, ALU.mult),
              reads=("scal_all", "fold_row"), writes=("f_t1",))
        S.add("vector", lambda e: e.tensor_add(fr["mst"], fr["nu"], fr["t1"]),
              reads=("f_nu", "f_t1"), writes=("f_mst",), tag=f"fd{r}m")
        S.add("tensor", lambda e: e.matmul(pbf(0, 8), ones_row, fr["ea"], start=True, stop=True),
              reads=("ones_row", "f_ea"), writes=("pb0",), tag=f"fd{r}ba")
        S.add("vector", lambda e: e.tensor_copy(bc_ea, pbf(0, 8)), reads=("pb0",), writes=("bc_ea",))
        S.add("tensor", lambda e: e.matmul(pbf(0, 8), ones_row, fr["eb"], start=True, stop=True),
              reads=("ones_row", "f_eb"), writes=("pb0",), tag=f"fd{r}bb")
        S.add("vector", lambda e: e.tensor_copy(bc_eb, pbf(0, 8)), reads=("pb0",), writes=("bc_eb",))
        for h in range(NH):
            S.dma("sync", csc, d_sum_all[(r * NH + h) * 128:(r * NH + h + 1) * 128, :],
                  reads=("d_sum_all",), writes=("csc",), tag=f"fl{r}_{h}")
            S.add("vector", lambda e, h=h: e.tensor_scalar(csc, csc, bc_eb[:, h:h + 1], None, ALU.mult),
                  reads=("csc", "bc_eb"), writes=("csc",))
            S.add("vector", lambda e, h=h: e.scalar_tensor_tensor(
                C_st[h], C_st[h], bc_ea[:, h:h + 1], csc, ALU.mult, ALU.add),
                reads=(f"Cst{h}", "bc_ea", "csc"), writes=(f"Cst{h}",), tag=f"fC{r}_{h}")
    for h in range(NH):
        S.add("vector", lambda e, h=h: e.tensor_copy(C_bf[h], C_st[h]),
              reads=(f"Cst{h}",), writes=(f"Cbf{h}",))
    S.add("vector", lambda e: e.tensor_copy(
        v16(rows["m0bc"]), fr["mst"].unsqueeze(2).broadcast_to([1, 8, 16])),
        reads=("f_mst",), writes=("r_m0bc",))
    tap("m0", fr["mst"], (1, 8), F32, ("f_mst",))
    tap("scal_all", scal_all, (1, 128), F32, ("scal_all",))
    tap("sc16", rows["sc16"], (1, 128), F32, ("r_sc16",))
    tap("C0", C_st[0], (128, VA), F32, ("Cst0",))

    AR.free("r_BL", "r_cmxL", "r_Pinc", "r_e_r", "r_d8", "r_m8", "r_sc16",
            "c_mhat", "g_i", "g_b", "g_ks", "ks_t", "kw_s", "sum_stage")

    S.add("vector", lambda e: e.tensor_max(rows["G_r"], rows["Ghat"], rows["m0bc"]),
          reads=("r_Ghat", "r_m0bc"), writes=("r_G_r",))
    for h in range(NH):
        o = h * 16
        S.add("vector", lambda e, o=o: e.tensor_copy(rows["Gnx"][0:1, o:o + 15],
                                                     rows["G_r"][0:1, o + 1:o + 16]),
              reads=("r_G_r",), writes=("r_Gnx",), tag="gnx")
    S.add("vector", lambda e: e.tensor_max(rows["mh_t"], rows["mhat"], rows["m0bc"]),
          reads=("r_mhat", "r_m0bc"), writes=("r_mh_t",))
    S.add("vector", lambda e: e.tensor_copy(
        v16(rows["Gnx"])[:, :, 15:16], v16(rows["mh_t"])[:, :, 15:16]),
        reads=("r_mh_t",), writes=("r_Gnx",))
    S.add("vector", lambda e: e.tensor_sub(rows["lam"], rows["G_r"], rows["Gnx"]),
          reads=("r_G_r", "r_Gnx"), writes=("r_lam",))
    S.add("scalar", lambda e: e.activation(rows["lam"], rows["lam"], AF.Exp),
          reads=("r_lam",), writes=("r_lam",))
    row2col(cols["G"], "c_G", rows["G_r"], ("r_G_r",), 4, tag="rcG")
    row2col(cols["Gnx"], "c_Gnx", rows["Gnx"], ("r_Gnx",), 4, tag="rcGn")

    g_S, _ = sb("g_S", [128, 128], F32)
    qf_t, _ = sb("qf_t", [128, 128], F32)
    row_t, _ = sb("row_t", [128, 128], F32)
    kw_t, _ = sb("kw_t", [128, 128], F32)
    enm_t, _ = sb("enm_t", [128, 128], F32)
    bclam, _ = sb("bclam", [128, 128], F32)

    S.add("vector", lambda e: e.tensor_scalar(g_S, g_cmx, cols["Pexc"], cols["G"],
                                              ALU.subtract, ALU.max),
          reads=("g_cmx", "c_Pexc", "c_G"), writes=("g_S",))
    S.add("vector", lambda e: e.tensor_scalar(g_s1, g_S, cols["G"], None, ALU.subtract),
          reads=("g_S", "c_G"), writes=("g_s1",))
    S.add("scalar", lambda e: e.activation(g_s2, g_s1, AF.Exp, scale=-1.0, bias=LN_SQ),
          reads=("g_s1", "constb"), writes=("g_s2",))
    transpose128(qf_t, "qf_t", g_s2, ("g_s2",), 5, tag="tQF")
    S.add("vector", lambda e: e.tensor_scalar(g_s1, g_at, cols["Pexc"], cols["G"],
                                              ALU.subtract, ALU.subtract),
          reads=("g_at", "c_Pexc", "c_G"), writes=("g_s1",))
    S.add("scalar", lambda e: e.activation(g_s2, g_s1, AF.Exp), reads=("g_s1",), writes=("g_s2",))
    transpose128(row_t, "row_t", g_s2, ("g_s2",), 5, tag="tRW")
    S.add("vector", lambda e: e.tensor_scalar(g_s1, g_at, cols["Pexc"], cols["Gnx"],
                                              ALU.subtract, ALU.subtract),
          reads=("g_at", "c_Pexc", "c_Gnx"), writes=("g_s1",))
    S.add("scalar", lambda e: e.activation(g_s2, g_s1, AF.Exp), reads=("g_s1",), writes=("g_s2",))
    transpose128(kw_t, "kw_t", g_s2, ("g_s2",), 5, tag="tKW")
    S.add("vector", lambda e: e.scalar_tensor_tensor(g_s1, g_B, cols["Pexc"], g_S,
                                                     ALU.add, ALU.add),
          reads=("g_B", "c_Pexc", "g_S"), writes=("g_s1",))
    S.add("scalar", lambda e: e.activation(g_s2, g_s1, AF.Exp, scale=-1.0),
          reads=("g_s1",), writes=("g_s2",))
    transpose128(enm_t, "enm_t", g_s2, ("g_s2",), 5, tag="tEN")
    S.add("tensor", lambda e: e.matmul(pbf(5, 128), ones_row, rows["lam"], start=True, stop=True),
          reads=("ones_row", "r_lam"), writes=("pb5",), tag="bclam")
    S.add("vector", lambda e: e.tensor_copy(bclam, pbf(5, 128)), reads=("pb5",), writes=("bclam",))
    tap("qf_t", qf_t, (128, 128), F32, ("qf_t",))
    tap("enm_t", enm_t, (128, 128), F32, ("enm_t",))

    AR.free("wblk0", "xcol0", "xcol1", "wifb")

    gamma_bc, _ = sb("gamma_bc", [128, 2048], F32)
    for b in range(4):
        S.add("tensor", lambda e, b=b: e.matmul(pbf(5), ones_row, gtmp[:, b * 512:(b + 1) * 512],
                                                start=True, stop=True),
              reads=("ones_row", "gtmp"), writes=("pb5",), tag="gbc")
        S.add("vector", lambda e, b=b: e.tensor_copy(gamma_bc[:, b * 512:(b + 1) * 512], pbf(5)),
              reads=("pb5",), writes=("gamma_bc",))
    AR.free("gtmp")
    qs_s, qfm, kfm, at_b, kws, dcol = [], [], [], [], [], []
    for p in range(NH):
        a, _ = sb(f"qs_s{p}", [128, 128], BF16); qs_s.append(a)
        a, _ = sb(f"qfm{p}", [128, 128], BF16); qfm.append(a)
        a, _ = sb(f"kfm{p}", [128, 128], BF16); kfm.append(a)
        a, _ = sb(f"at_b{p}", [128, 128], BF16); at_b.append(a)
        a, _ = sb(f"kws{p}", [128, 128], BF16); kws.append(a)
        a, _ = sb(f"dcol{p}", [128, 4], F32); dcol.append(a)
    h_stage, _ = sb("h_stage", [128, 2048], F32)
    hn_bf, _ = sb("hn_bf", [128, 2048], BF16)
    mh1, _ = sb("mh1", [128, 8], F32)
    mh2, _ = sb("mh2", [128, 8], F32)
    mh3 = dump

    hn_spills = []
    for c in range(NC):
        for h in range(NH):
            hc = h * 16 + c
            S.add("vector", lambda e, c=c, h=h, hc=hc: e.tensor_scalar(
                qs_s[h], q_tm[c][:, h * 128:(h + 1) * 128], qf_t[:, hc:hc + 1], None, ALU.mult),
                reads=(f"qtm{c}", "qf_t"), writes=(f"qs_s{h}",), tag=f"qs{c}_{h}")
            S.add("vector", lambda e, c=c, h=h, hc=hc: e.tensor_scalar(
                kws[h], k_tm[c][:, h * 128:(h + 1) * 128], kw_t[:, hc:hc + 1], None, ALU.mult),
                reads=(f"ktm{c}", "kw_t"), writes=(f"kws{h}",), tag=f"kw{c}_{h}")
        for h in range(NH):
            transpose128(qfm[h], f"qfm{h}", qs_s[h], (f"qs_s{h}",), h % 4,
                         dt=BF16, tag=f"qT{c}_{h}")
        for h in range(NH):
            transpose128(kfm[h], f"kfm{h}", k_tm[c][:, h * 128:(h + 1) * 128],
                         (f"ktm{c}",), 4 + h % 4, dt=BF16, tag=f"kT{c}_{h}")
        for h in range(NH):
            S.add("tensor", lambda e, h=h: e.matmul(pbf(h, 128), kfm[h], qfm[h],
                                                    start=True, stop=True),
                  reads=(f"kfm{h}", f"qfm{h}"), writes=(f"pb{h}",), tag=f"PT{c}_{h}")
        for h in range(NH):
            hc = h * 16 + c
            S.add("vector", lambda e, h=h, hc=hc: e.scalar_tensor_tensor(
                at_b[h], pbf(h, 128), row_t[:, hc:hc + 1], maskut, ALU.mult, ALU.mult),
                reads=(f"pb{h}", "row_t", "maskut"), writes=(f"at_b{h}",), tag=f"AT{c}_{h}")
        for h in range(NH):
            S.add("tensor", lambda e, h=h: e.matmul(pbf(h, VA), qfm[h], C_bf[h],
                                                    start=True, stop=False),
                  reads=(f"qfm{h}", f"Cbf{h}"), writes=(f"pb{h}",), tag=f"hin{c}_{h}")
            S.add("tensor", lambda e, c=c, h=h: e.matmul(pbf(h, VA), at_b[h],
                                                         v_aug[c][:, h * VA:(h + 1) * VA],
                                                         start=False, stop=True),
                  reads=(f"at_b{h}", f"vaug{c}"), writes=(f"pb{h}",), tag=f"hia{c}_{h}")
        for h in range(NH):
            S.add("scalar", lambda e, h=h: e.activation(dcol[h][:, 0:1],
                                                        pbf(h, VA)[:, DV:DV + 1], AF.Abs),
                  reads=(f"pb{h}",), writes=(f"dcol{h}a",))
        for h in range(NH):
            hc = h * 16 + c
            S.add("vector", lambda e, h=h, hc=hc: e.tensor_scalar(
                dcol[h][:, 1:2], dcol[h][:, 0:1], enm_t[:, hc:hc + 1], 1e-6, ALU.max, ALU.add),
                reads=(f"dcol{h}a", "enm_t"), writes=(f"dcol{h}b",))
            S.add("vector", lambda e, h=h: e.reciprocal(dcol[h][:, 2:3], dcol[h][:, 1:2]),
                  reads=(f"dcol{h}b",), writes=(f"dcol{h}c",))
        for h in range(NH):
            S.add("vector", lambda e, h=h: e.tensor_scalar(
                h_stage[:, h * 256:(h + 1) * 256], pbf(h, VA)[:, 0:DV], dcol[h][:, 2:3],
                None, ALU.mult),
                reads=(f"pb{h}", f"dcol{h}c"), writes=("h_stage",), tag=f"hd{c}_{h}")
        for h in range(NH):
            S.add("tensor", lambda e, c=c, h=h: e.matmul(pbf(h, VA), kws[h],
                                                         v_aug[c][:, h * VA:(h + 1) * VA],
                                                         start=True, stop=True),
                  reads=(f"kws{h}", f"vaug{c}"), writes=(f"pb{h}",), tag=f"cu{c}_{h}")
        for h in range(NH):
            hc = h * 16 + c
            S.add("vector", lambda e, h=h, hc=hc: e.scalar_tensor_tensor(
                C_st[h], C_st[h], bclam[:, hc:hc + 1], pbf(h, VA), ALU.mult, ALU.add),
                reads=(f"Cst{h}", "bclam", f"pb{h}"), writes=(f"Cst{h}",), tag=f"cn{c}_{h}")
            S.add("vector", lambda e, h=h: e.tensor_copy(C_bf[h], C_st[h]),
                  reads=(f"Cst{h}",), writes=(f"Cbf{h}",), tag=f"cb{c}_{h}")
        hsv = h_stage.rearrange("p (h v) -> p h v", v=256)
        S.add("vector", lambda e: e.tensor_reduce(mh1, hsv, AX.X, ALU.add),
              reads=("h_stage",), writes=("mh1",), tag=f"mn1{c}")
        S.add("scalar", lambda e: e.activation(mh3, h_stage, AF.Square),
              reads=("h_stage",), writes=("dump",), tag=f"mn2{c}")
        S.add("vector", lambda e: e.tensor_reduce(mh2, mh3.rearrange("p (h v) -> p h v", v=256),
                                                  AX.X, ALU.add),
              reads=("dump",), writes=("mh2",), tag=f"mn3{c}")
        S.add("vector", lambda e: e.tensor_scalar(mh1, mh1, 1.0 / 256.0, None, ALU.mult),
              reads=("mh1",), writes=("mh1",))
        S.add("vector", lambda e: e.tensor_scalar(mh2, mh2, 1.0 / 256.0, None, ALU.mult),
              reads=("mh2",), writes=("mh2",))
        S.add("vector", lambda e: e.tensor_mul(dump[:, 0:8], mh1, mh1), reads=("mh1",), writes=("dump",))
        S.add("vector", lambda e: e.tensor_sub(mh2, mh2, dump[:, 0:8]),
              reads=("mh2", "dump"), writes=("mh2",))
        S.add("scalar", lambda e: e.activation(mh2, mh2, AF.Ln, bias=1e-6),
              reads=("mh2", "constb"), writes=("mh2",))
        S.add("scalar", lambda e: e.activation(mh2, mh2, AF.Exp, scale=-0.5),
              reads=("mh2",), writes=("mh2",))
        S.add("vector", lambda e: e.tensor_tensor(hsv, hsv,
                                                  mh1.unsqueeze(2).broadcast_to([128, 8, 256]),
                                                  ALU.subtract),
              reads=("h_stage", "mh1"), writes=("h_stage",), tag=f"mn4{c}")
        S.add("vector", lambda e: e.tensor_tensor(hsv, hsv,
                                                  mh2.unsqueeze(2).broadcast_to([128, 8, 256]),
                                                  ALU.mult),
              reads=("h_stage", "mh2"), writes=("h_stage",), tag=f"mn5{c}")
        S.add("vector", lambda e: e.tensor_mul(hn_bf, h_stage, gamma_bc),
              reads=("h_stage", "gamma_bc"), writes=("hn_bf",), tag=f"mn6{c}")
        S.dma("sync", d_hnorm[c * 128:(c + 1) * 128, :], hn_bf, reads=("hn_bf",),
              writes=(f"dhn{c}",), tag=f"sthn{c}")
        hn_spills.append(f"dhn{c}")
        if c == 0:
            tap("h0", h_stage, (128, 2048), F32, ("h_stage",))

    AR.free(*[f"qtm{t}" for t in range(16)], *[f"ktm{t}" for t in range(16)],
            *[f"vaug{t}" for t in range(16)])
    AR.free("g_f", "g_B", "g_at", "g_cmx", "g_s1", "g_s2", "g_S",
            "scal_all", "bc_ea", "bc_eb",
            "csc", *[f"{nm}{p}" for nm in ("qs_s", "qfm", "kfm", "at_b", "kws", "dcol") for p in range(NH)],
            "h_stage", "hn_bf", "mh1", "mh2",
            "qf_t", "row_t", "kw_t", "enm_t", "bclam",
            *[f"Cst{h}" for h in range(8)], *[f"Cbf{h}" for h in range(8)],
            *["r_" + rn for rn in ("Pexc", "Ghat", "mh_t", "mhat", "m0bc",
                                   "G_r", "Gnx", "lam")],
            *["c_" + cn for cn in ("Pexc", "G", "Gnx")],
            *["f_" + rn for rn in ("mst", "nu", "meff", "ea", "eb", "t1")],
            "e2", "e2t")

    if "hnormfull" in taps:
        o = nc.dram_tensor("tap_hnormfull", [T, 2048], BF16, kind="ExternalOutput").ap()
        S.dma("sync", o, d_hnorm, reads=tuple(hn_spills), writes=("taphnf",))

    hout = []
    for tc in range(16):
        a, cf = sb(f"hout{tc}", [128, 2048], BF16)
        hout.append((a, cf))
    hnt0, _ = sb("hnt0", [128, 512], BF16)
    hnt1, _ = sb("hnt1", [128, 512], BF16)
    sgl0, _ = sb("sgl0", [128, 512], BF16)
    sgl1, _ = sb("sgl1", [128, 512], BF16)
    hob, _ = sb("hob", [128, 512], BF16)

    for b in range(4):
        for tc in range(16):
            hnt, hk = ((hnt0, "hnt0"), (hnt1, "hnt1"))[tc % 2]
            sgl, sk_ = ((sgl0, "sgl0"), (sgl1, "sgl1"))[tc % 2]
            S.dma("sync", hnt, d_hnorm[tc * 128:(tc + 1) * 128, b * 512:(b + 1) * 512],
                  reads=(f"dhn{tc}",), writes=(hk,), tag=f"ldhn{b}_{tc}")
            S.dma("gpsimd", sgl, d_sigo[tc * 128:(tc + 1) * 128, b * 512:(b + 1) * 512],
                  reads=(f"dsg{b}_{tc}",), writes=(sk_,), tag=f"ldsg{b}_{tc}")
            S.add("vector", lambda e, hnt=hnt, sgl=sgl: e.tensor_mul(hob, sgl, hnt),
                  reads=(sk_, hk), writes=("hob",), tag=f"ogm{b}_{tc}")
            for i in range(4):
                vc = b * 4 + i
                bk2 = 5 + (i % 2)
                S.add("tensor", lambda e, i=i, bk2=bk2: e.transpose(
                    pbb(bk2, 128), hob[:, i * 128:(i + 1) * 128], identb),
                    reads=("hob", "identb"), writes=(f"pb{bk2}",), tag=f"hoT{b}_{tc}_{i}", kind="tp")
                S.add("vector", lambda e, tc=tc, vc=vc, bk2=bk2: e.tensor_copy(
                    hout[tc][0][:, vc * 128:(vc + 1) * 128], pbb(bk2, 128)),
                    reads=(f"pb{bk2}",), writes=(f"hout{tc}",),
                    tag=f"hoTe{b}_{tc}_{i}")

    wblk0, _ = sb("wblk0", [128, 16 * 512], BF16)
    wblk1, _ = sb("wblk1", [128, 16 * 512], BF16)
    smst, _ = sb("smst", [128, 512], F32)
    wsrc_wout = wout_t.rearrange("(vc p) n -> p vc n", p=128)
    for b in range(4):
        wb, wk_ = ((wblk0, "wblk0"), (wblk1, "wblk1"))[b % 2]
        S.dma("gpsimd", wb.rearrange("p (vc n) -> p vc n", n=512),
              wsrc_wout[:, :, b * 512:(b + 1) * 512], writes=(wk_,), tag=f"ldwo{b}")
        for tc in range(16):
            bk = 2 + (tc % 2)
            for vc in range(16):
                S.add("tensor", lambda e, vc=vc, tc=tc, bk=bk, wb=wb: e.matmul(
                    pbf(bk), hout[tc][0][:, vc * 128:(vc + 1) * 128],
                    wb[:, vc * 512:(vc + 1) * 512], start=(vc == 0), stop=(vc == 15)),
                    reads=(f"hout{tc}", wk_), writes=(f"pb{bk}",), tag=f"womm{b}_{tc}")
            S.add("vector", lambda e, bk=bk: e.tensor_copy(smst, pbf(bk)),
                  reads=(f"pb{bk}",), writes=("smst",), tag=f"woev{b}_{tc}")
            S.dma("sync", d_smix[tc * 128:(tc + 1) * 128, b * 512:(b + 1) * 512], smst,
                  reads=("smst",), writes=(f"dsm{b}_{tc}",), tag=f"stsm{b}_{tc}")
    tap("smix", smst, (128, 512), F32, ("smst",))
    if "smixfull" in taps:
        o = nc.dram_tensor("tap_smixfull", [T, D], F32, kind="ExternalOutput").ap()
        S.dma("sync", o, d_smix, reads=tuple(f"dsm{b}_{t}" for b in range(4) for t in range(16)),
              writes=("tapsmf",))

    AR.free(*[f"hout{t}" for t in range(16)], "wblk0", "wblk1",
            "hnt0", "hnt1", "sgl0", "sgl1", "hob", "smst")

    xn_fm = []
    for dc in range(16):
        a, cf = sb(f"xnr{dc}", [128, 2048], BF16)
        xn_fm.append(a)
        S.dma("sync", a, d_xnfm[dc * 128:(dc + 1) * 128, :],
              reads=(f"dxn{dc}",), writes=(f"xnr{dc}",), tag=f"rldxn{dc}")
    wgu0, _ = sb("wgu0", [128, 2048 + 2048], BF16)
    wgu1, _ = sb("wgu1", [128, 2048 + 2048], BF16)
    sgate, _ = sb("sgate", [128, 512], F32)
    hsq, _ = sb("hsq", [128, 512], F32)
    hrow, _ = sb("hrow", [128, 2048], BF16)
    sqacc, _ = sb("sqacc", [128, 2048], F32)
    S.add("vector", lambda e: e.memset(sqacc, 0.0), writes=("sqacc",))
    wgu_v = d_wqgu.rearrange("(dc p) H -> p dc H", p=128)

    h_spills = []
    for g in range(64):
        wb, wk_ = ((wgu0, "wgu0"), (wgu1, "wgu1"))[g % 2]
        S.dma("sync", wb[:, 0:2048].rearrange("p (dc n) -> p dc n", n=128),
              wgu_v[:, :, g * 128:(g + 1) * 128],
              reads=("d_wqgu",), writes=(wk_,), tag=f"ldgu{g}")
        S.dma("sync", wb[:, 2048:4096].rearrange("p (dc n) -> p dc n", n=128),
              wgu_v[:, :, 8192 + g * 128:8192 + (g + 1) * 128],
              reads=("d_wqgu",), writes=(wk_,), tag=f"ldup{g}")
        for tb in range(4):
            bkg, bku = 2 + (tb % 2) * 2, 3 + (tb % 2) * 2
            for dc in range(16):
                S.add("tensor", lambda e, dc=dc, wb=wb, tb=tb, bkg=bkg: e.matmul(
                    pbf(bkg), wb[:, dc * 128:(dc + 1) * 128],
                    xn_fm[dc][:, tb * 512:(tb + 1) * 512], start=(dc == 0), stop=(dc == 15)),
                    reads=(wk_, f"xnr{dc}"), writes=(f"pb{bkg}",), tag=f"gum{g}_{tb}")
                S.add("tensor", lambda e, dc=dc, wb=wb, tb=tb, bku=bku: e.matmul(
                    pbf(bku), wb[:, 2048 + dc * 128:2048 + (dc + 1) * 128],
                    xn_fm[dc][:, tb * 512:(tb + 1) * 512], start=(dc == 0), stop=(dc == 15)),
                    reads=(wk_, f"xnr{dc}"), writes=(f"pb{bku}",), tag=f"upm{g}_{tb}")
            S.add("scalar", lambda e, bkg=bkg: e.activation(sgate, pbf(bkg), AF.Silu),
                  reads=(f"pb{bkg}",), writes=("sgate",), tag=f"gsil{g}_{tb}")
            S.add("vector", lambda e, tb=tb, bku=bku: e.tensor_tensor(hrow[:, tb * 512:(tb + 1) * 512],
                                                             sgate, pbf(bku), ALU.mult),
                  reads=("sgate", f"pb{bku}"), writes=("hrow",), tag=f"gmul{g}_{tb}")
            S.add("scalar", lambda e, tb=tb: e.activation(hsq, hrow[:, tb * 512:(tb + 1) * 512],
                                                          AF.Square),
                  reads=("hrow",), writes=("hsq",), tag=f"gsq{g}_{tb}")
            S.add("vector", lambda e, tb=tb: e.tensor_add(sqacc[:, tb * 512:(tb + 1) * 512],
                                                          sqacc[:, tb * 512:(tb + 1) * 512], hsq),
                  reads=("sqacc", "hsq"), writes=("sqacc",), tag=f"gacc{g}_{tb}")
        S.dma("sync", d_hfm[g * 128:(g + 1) * 128, :], hrow, reads=("hrow",),
              writes=(f"dh{g}",), tag=f"sth{g}")
        h_spills.append(f"dh{g}")

    if "wqgu" in taps:
        o = nc.dram_tensor("tap_wqgu", [256, 16384], BF16, kind="ExternalOutput").ap()
        S.dma("sync", o[0:128, :], d_wqgu[0:128, :], reads=("d_wqgu",), writes=("tapwq1",))
        S.dma("sync", o[128:256, :], d_wqgu[1024:1152, :], reads=("d_wqgu",), writes=("tapwq2",))
    if "wqsh" in taps:
        o2 = nc.dram_tensor("tap_wqsh", [256, 16384], BF16, kind="ExternalOutput").ap()
        S.dma("sync", o2, d_wqgu_sh, reads=tuple(q_st[:16]), writes=("tapwqsh",))
    tap("sqacc", sqacc, (128, 2048), F32, ("sqacc",))
    tap("hrow", hrow, (128, 2048), BF16, ("hrow",))

    irow, _ = sb("irow", [1, 2048], F32)
    for b in range(4):
        S.add("tensor", lambda e, b=b: e.matmul(pbf(4, 512)[0:1, :], ones_col,
                                                sqacc[:, b * 512:(b + 1) * 512],
                                                start=True, stop=True),
              reads=("ones_col", "sqacc"), writes=("pb4",), tag=f"rsum{b}")
        S.add("vector", lambda e, b=b: e.tensor_copy(irow[0:1, b * 512:(b + 1) * 512],
                                                     pbf(4, 512)[0:1, :]),
              reads=("pb4",), writes=("irow",))
    S.add("scalar", lambda e: e.activation(irow, irow, AF.Ln, scale=1.0 / 8192.0, bias=RMS_EPS),
          reads=("irow", "constb"), writes=("irow",))
    S.add("scalar", lambda e: e.activation(irow, irow, AF.Exp, scale=-0.5),
          reads=("irow",), writes=("irow",))
    S.dma("sync", d_irow, irow, reads=("irow",), writes=("d_irow",), tag="stir")
    inv16, _ = sb("inv16", [16, 128], F32)
    invc, _ = sb("invc", [128, 16], F32)
    S.dma("sync", inv16, d_irow.rearrange("o (a b) -> (o a) b", b=128),
          reads=("d_irow",), writes=("inv16",), tag="ldir")
    S.add("tensor", lambda e: e.transpose(pbf(4, 16), inv16, ident[0:16, 0:16]),
          reads=("inv16", "ident"), writes=("pb4",), tag="tpin", kind="tp")
    S.add("vector", lambda e: e.tensor_copy(invc, pbf(4, 16)), reads=("pb4",), writes=("invc",))
    tap("invc", invc, (128, 16), F32, ("invc",))

    AR.free(*[f"xnr{dc}" for dc in range(16)], "wgu0", "wgu1", "sgate", "hsq",
            "hrow", "sqacc")

    hblk = []
    for pari in range(2):
        row = []
        for i in range(4):
            a, cf = sb(f"hblk{pari}_{i}", [128, 16 * 512], BF16)
            row.append(a)
        hblk.append(row)
    wps = []
    for i in range(8):
        a, _ = sb(f"wp{i}", [128, 512], BF16)
        wps.append(a)
    ot0, _ = sb("ot0", [128, 512], F32)
    ot1, _ = sb("ot1", [128, 512], F32)
    xres0, _ = sb("xres0", [128, 512], F32)
    xres1, _ = sb("xres1", [128, 512], F32)
    sm0, _ = sb("sm0", [128, 512], F32)
    sm1, _ = sb("sm1", [128, 512], F32)
    gsil, _ = sb("gsil", [128, 512], F32)
    hfm_v = d_hfm.rearrange("(hc p) t -> p hc t", p=128)

    out_sts = []
    for tb in range(4):
        pari = tb % 2
        for i in range(4):
            S.dma("sync", hblk[pari][i].rearrange("p (hc t) -> p hc t", t=512),
                  hfm_v[:, i * 16:(i + 1) * 16, tb * 512:(tb + 1) * 512],
                  reads=tuple(h_spills), writes=(f"hblk{pari}_{i}",), tag=f"ldhb{tb}_{i}")
        for db in range(4):
            for hc in range(64):
                wp, wpk = wps[hc % 8], f"wp{hc % 8}"
                S.dma(("sync", "gpsimd")[hc % 2], wp,
                      d_wqpr[hc * 128:(hc + 1) * 128, db * 512:(db + 1) * 512],
                      reads=("d_wqpr",), writes=(wpk,), tag=f"ldwp{tb}_{db}_{hc}")
                for ts in range(4):
                    S.add("tensor", lambda e, hc=hc, ts=ts, wp=wp, pari=pari: e.matmul(
                        pbf(ts), hblk[pari][hc // 16][:, (hc % 16) * 512 + ts * 128:(hc % 16) * 512 + (ts + 1) * 128],
                        wp, start=(hc == 0), stop=(hc == 63)),
                        reads=(f"hblk{pari}_{hc // 16}", wpk), writes=(f"pb{ts}",), tag=f"prm{tb}_{db}_{ts}")
            for ts in range(4):
                tc = tb * 4 + ts
                ot, ok = ((ot0, "ot0"), (ot1, "ot1"))[ts % 2]
                xr, xk2 = ((xres0, "xres0"), (xres1, "xres1"))[ts % 2]
                smt, smk = ((sm0, "sm0"), (sm1, "sm1"))[ts % 2]
                S.dma("sync", xr, x_in[tc * 128:(tc + 1) * 128, db * 512:(db + 1) * 512],
                      writes=(xk2,), tag=f"ldx2{tb}_{db}_{ts}")
                S.dma("sync", smt, d_smix[tc * 128:(tc + 1) * 128, db * 512:(db + 1) * 512],
                      reads=(f"dsm{db}_{tc}",), writes=(smk,), tag=f"ldsm{tb}_{db}_{ts}")
                S.add("vector", lambda e, ts=ts, tc=tc, ot=ot: e.tensor_scalar(
                    ot, pbf(ts), invc[:, tc:tc + 1], None, ALU.mult),
                    reads=(f"pb{ts}", "invc"), writes=(ok,), tag=f"prs{tb}_{db}_{ts}")
                S.add("scalar", lambda e, ot=ot: e.activation(gsil, ot, AF.Silu),
                      reads=(ok,), writes=("gsil",), tag=f"prsil{tb}_{db}_{ts}")
                S.add("vector", lambda e, smt=smt, ot=ot: e.tensor_mul(ot, gsil, smt),
                      reads=("gsil", smk), writes=(ok,), tag=f"prg{tb}_{db}_{ts}")
                S.add("vector", lambda e, xr=xr, ot=ot: e.tensor_add(ot, ot, xr),
                      reads=(ok, xk2), writes=(ok,), tag=f"pradd{tb}_{db}_{ts}")
                st = S.dma("sync", out_ext[tc * 128:(tc + 1) * 128, db * 512:(db + 1) * 512], ot,
                           reads=(ok,), writes=(f"st{tb}_{db}_{ts}",), tag=f"stout{tb}_{db}_{ts}")
                out_sts.append(f"st{tb}_{db}_{ts}")

    S.barrier("vector", reads=tuple(out_sts), tag="final")

    sems = {}
    for sk in S.semkeys():
        name = "s_" + "_".join(str(x) for x in sk)
        sems[sk] = es.enter_context(nc.semaphore(name))
    with nc.Block() as block:
        S.emit(block, sems)
    es.close()
    st, nw = S.stats()
    return nc, dict(ops=st, waits=nw, arena_hw=AR.hw, taps=tap_names)


_cache = {}


def _get_program(taps=()):
    key = tuple(sorted(taps))
    if key not in _cache:
        _cache[key] = build_program(taps)
    return _cache[key]


def kernel(L0_attn_x, L0_attn_Wq, L0_attn_Wk, L0_attn_Wv, L0_attn_Wog, L0_attn_Wig,
           L0_attn_big, L0_attn_Wfg, L0_attn_bfg, L0_attn_gamma, L0_attn_Wout,
           L0_ffn_w_gate_up, L0_ffn_w_proj):
    F = np.float32
    x = np.ascontiguousarray(np.asarray(L0_attn_x, F).reshape(16384, 2048))
    com = dict(
        wq_t=np.ascontiguousarray(np.asarray(L0_attn_Wq, F).T),
        wk_t=np.ascontiguousarray(np.asarray(L0_attn_Wk, F).T),
        wv_t=np.ascontiguousarray(np.asarray(L0_attn_Wv, F).T),
        wog_t=np.ascontiguousarray(np.asarray(L0_attn_Wog, F).T),
        wout_t=np.ascontiguousarray(np.asarray(L0_attn_Wout, F).T),
        wif_t=np.ascontiguousarray(np.concatenate(
            [np.asarray(L0_attn_Wig, F).T, np.asarray(L0_attn_Wfg, F).T], axis=1)),
        bif=np.concatenate([np.asarray(L0_attn_big, F),
                            np.asarray(L0_attn_bfg, F)]).reshape(1, 16),
        gamma_row=np.ascontiguousarray(np.asarray(L0_attn_gamma, F).reshape(1, 2048)),
        ident=np.eye(128, dtype=F),
        mask_ut=np.triu(np.ones((128, 128), F)),
    )
    wgu_t = np.ascontiguousarray(np.asarray(L0_ffn_w_gate_up, F).T)
    wpr_t = np.ascontiguousarray(np.asarray(L0_ffn_w_proj, F).T)

    nc, info = _get_program(tuple(sorted(TAPS)))
    in_maps = []
    for c in range(NCORES):
        m = dict(com)
        m["x_c"] = x[c * T:(c + 1) * T]
        m["wgu_sh"] = np.ascontiguousarray(wgu_t[c * 256:(c + 1) * 256])
        m["wproj_sh"] = np.ascontiguousarray(wpr_t[c * 1024:(c + 1) * 1024])
        m["fold_mask"] = (np.arange(8) < c).astype(F).reshape(1, 8)
        in_maps.append(m)
    res = run_bass_kernel_spmd(nc, in_maps, core_ids=list(range(NCORES)))
    kernel.last_results = res
    globals()["last_results"] = res
    out = np.concatenate([res.results[c]["out_c"] for c in range(NCORES)], axis=0)
    return out.reshape(1, 16384, 2048)


# version 08
# speedup vs baseline: 1.0410x; 1.0157x over previous
import math
from collections import defaultdict
from contextlib import ExitStack

import numpy as np
import ml_dtypes
import concourse.bass as bass
import concourse.mybir as mybir
from concourse.bass_utils import run_bass_kernel_spmd

F32 = mybir.dt.float32
BF16 = mybir.dt.bfloat16
AF = mybir.ActivationFunctionType
ALU = mybir.AluOpType
AX = mybir.AxisListType
BFnp = ml_dtypes.bfloat16

NCORES = 8
T = 2048
D = 2048
NH, DQ, DV, L = 8, 128, 256, 128
NC = 16
HF = 8192
VA = DV + 1
LN_SQ = -0.5 * math.log(128.0)
RMS_EPS = 1.1920929e-7

ENGINES = ("sync", "gpsimd", "vector", "scalar", "tensor")
TAPS = set()
SELF_SYNC_ALL = False


class Op:
    __slots__ = ("engine", "fn", "reads", "writes", "inc", "cum", "deps", "tag",
                 "semkey")

    def __init__(self, engine, fn, reads, writes, inc, tag):
        self.engine, self.fn = engine, fn
        self.reads, self.writes = reads, writes
        self.inc, self.cum, self.tag = inc, 0, tag
        self.deps = {}
        self.semkey = None


NDMA_LANES = 8


class Sched:

    def __init__(self, nc, self_sync=False):
        self.nc = nc
        self.ops = []
        self.conf = {}
        self.self_sync = self_sync
        self._cum = defaultdict(int)
        self._dma_rr = defaultdict(int)
        self._cc_idx = 0
        self._last_write = {}
        self._readers = defaultdict(list)

    def add(self, engine, fn, reads=(), writes=(), dma=False, tag="", inc=None,
            kind="", collective=False):
        is_async = dma or collective
        writes = tuple(writes)
        extra = ()
        for w in writes:
            if w in self.conf:
                extra = extra + self.conf.pop(w)
        if collective:
            semkey = ("cc", self._cc_idx)
            self._cc_idx += 1
            the_inc = 1
        elif dma:
            lane = self._dma_rr[engine] % NDMA_LANES
            self._dma_rr[engine] += 1
            semkey = (engine, "d", lane)
            the_inc = 16
        else:
            semkey = (engine, "c")
            the_inc = 1
        if inc is not None:
            the_inc = inc
        op = Op(engine, fn, tuple(reads), writes + extra, the_inc, tag)
        op.semkey = semkey
        deps = {}

        def need(sk, c, a, k, peng):
            if peng == engine and not a and not self.self_sync:
                if engine not in ("vector", "scalar", "gpsimd") \
                        and not (k == "tp" or kind == "tp"):
                    return
            if a:
                c = self._cum[sk]
            if c > deps.get(sk, -1):
                deps[sk] = c

        for r in op.reads:
            lw = self._last_write.get(r)
            if lw:
                need(*lw)
        for w in op.writes:
            lw = self._last_write.get(w)
            if lw:
                need(*lw)
            for ent in self._readers.get(w, ()):
                need(*ent)
        op.deps = deps
        self._cum[semkey] += op.inc
        op.cum = self._cum[semkey]
        rec = (semkey, op.cum, is_async, kind, engine)
        for r in op.reads:
            self._readers[r].append(rec)
        for w in op.writes:
            self._last_write[w] = rec
            self._readers[w] = []
        self.ops.append(op)
        return op

    def dma(self, engine, out, in_, reads=(), writes=(), tag="", **kw):
        return self.add(engine, lambda eng: eng.dma_start(out=out, in_=in_, **kw),
                        reads=reads, writes=writes, dma=True, tag=tag)

    def barrier(self, engine, reads, tag="barrier"):
        return self.add(engine, lambda e: None, reads=reads, tag=tag, inc=0)

    def emit(self, block, sems):
        per_engine = defaultdict(list)
        for op in self.ops:
            per_engine[op.engine].append(op)

        def mk(engine_name, ops):
            def body(eng):
                watermark = defaultdict(int)
                for op in ops:
                    for sk, c in sorted(op.deps.items(), key=str):
                        if c > watermark[sk]:
                            eng.wait_ge(sems[sk], c)
                            watermark[sk] = c
                    ins = op.fn(eng)
                    if ins is not None and op.inc:
                        ins.then_inc(sems[op.semkey], op.inc)
            return body

        for e in ENGINES:
            if per_engine[e]:
                getattr(block, e)(mk(e, per_engine[e]))

    def semkeys(self):
        ks = set()
        for op in self.ops:
            ks.add(op.semkey)
            ks.update(op.deps.keys())
        return ks

    def stats(self):
        n = defaultdict(int)
        nw = 0
        for op in self.ops:
            n[op.engine] += 1
            nw += len(op.deps)
        return dict(n), nw


class Arena:
    def __init__(self, nc, es, nbytes, name="arena"):
        self.nb = nbytes
        self.t = es.enter_context(nc.sbuf_tensor(name, [128, nbytes], mybir.dt.int8))
        self.regions = []
        self.dead = []
        self.hw = 0

    def alloc(self, key, shape, dtype, align=128):
        esz = {BF16: 2, F32: 4}[dtype]
        nb = shape[1] * esz
        self.regions.sort()
        pos = 0
        for s, e, k in self.regions:
            if pos + nb <= s:
                break
            pos = max(pos, (e + align - 1) // align * align)
        end = pos + nb
        assert end <= self.nb, f"arena OOM for {key}: need {nb}B at {pos}"
        self.hw = max(self.hw, end)
        conflicts = tuple(k for s, e, k in self.dead if s < end and e > pos)
        newdead = []
        for s, e, k in self.dead:
            if not (s < end and e > pos):
                newdead.append((s, e, k))
                continue
            if s < pos:
                newdead.append((s, pos, k))
            if e > end:
                newdead.append((end, e, k))
        self.dead = newdead
        self.regions.append((pos, end, key))
        return self.t.ap()[0:shape[0], pos:end].bitcast(dtype), conflicts

    def free(self, *keys):
        for key in keys:
            for i, (s, e, k) in enumerate(self.regions):
                if k == key:
                    self.regions.pop(i)
                    self.dead.append((s, e, k))
                    break
            else:
                raise KeyError(key)


def build_program(taps=()):
    taps = set(taps)
    nc = bass.Bass("TRN2", target_bir_lowering=False, debug=False)
    es = ExitStack()
    tap_names = []

    def ti(name, shape, dt=F32):
        return nc.dram_tensor(name, shape, dt, kind="ExternalInput").ap()

    x_in = ti("x_c", [T, D])
    wq_t = ti("wq_t", [D, 1024])
    wk_t = ti("wk_t", [D, 1024])
    wv_t = ti("wv_t", [D, 2048])
    wog_t = ti("wog_t", [D, 2048])
    wout_t = ti("wout_t", [2048, D])
    wif_t = ti("wif_t", [D, 16])
    bif_in = ti("bif", [1, 16])
    gamma_in = ti("gamma_row", [1, 2048])
    wgu_sh_in = ti("wgu_sh", [256, 16384])
    wproj_sh_in = ti("wproj_sh", [1024, 2048])
    ident_in = ti("ident", [128, 128])
    mask_in = ti("mask_ut", [128, 128])
    fold_in = ti("fold_mask", [1, 8])
    out_ext = nc.dram_tensor("out_c", [T, D], F32, kind="ExternalOutput").ap()

    d_wqgu_sh = nc.dram_tensor("d_wqgu_sh", [256, 16384], BF16).ap()
    d_wqgu = nc.dram_tensor("d_wqgu", [2048, 16384], BF16).ap()
    d_wqpr_sh = nc.dram_tensor("d_wqpr_sh", [1024, 2048], BF16).ap()
    d_wqpr = nc.dram_tensor("d_wqpr", [8192, 2048], BF16).ap()
    d_xnfm = nc.dram_tensor("d_xnfm", [D, T], BF16).ap()
    d_hnorm = nc.dram_tensor("d_hnorm", [T, 2048], BF16).ap()
    d_smix = nc.dram_tensor("d_smix", [T, D], F32).ap()
    d_sigo = nc.dram_tensor("d_sigo", [T, 2048], BF16).ap()
    d_hfm = nc.dram_tensor("d_hfm", [HF, T], BF16).ap()
    d_sum_c = nc.dram_tensor("d_sum_c", [NH * DQ, VA], F32).ap()
    d_sum_all = nc.dram_tensor("d_sum_all", [NCORES * NH * DQ, VA], F32).ap()
    d_scal_c = nc.dram_tensor("d_scal_c", [1, 16], F32).ap()
    d_scal_all = nc.dram_tensor("d_scal_all", [NCORES, 16], F32).ap()
    d_irow = nc.dram_tensor("d_irow", [1, 2048], F32).ap()

    S = Sched(nc, self_sync=SELF_SYNC_ALL)
    AR = Arena(nc, es, 206 * 1024)

    def sb(key, shape, dt=F32):
        ap, conflicts = AR.alloc(key, shape, dt)
        if conflicts:
            S.conf[key] = conflicts
        return ap, conflicts

    pbank = [es.enter_context(nc.psum_tensor(f"pb{i}", [128, 512], F32)).ap()
             for i in range(8)]

    def pbf(i, n=512):
        return pbank[i][:, 0:n]

    def pbb(i, n):
        return pbank[i].bitcast(BF16)[:, 0:n]

    def tap(name, src_ap, shape, dt, reads):
        if name not in taps:
            return
        o = nc.dram_tensor("tap_" + name, list(shape), dt, kind="ExternalOutput").ap()
        tap_names.append(name)
        S.dma("sync", o, src_ap, reads=reads, writes=("tapst_" + name,), tag="tap")
        return "tapst_" + name

    ident, _ = sb("ident", [128, 128], F32)
    identb, _ = sb("identb", [128, 128], BF16)
    maskut, _ = sb("maskut", [128, 128], F32)
    ones_col, _ = sb("ones_col", [128, 1], F32)
    ones_row, _ = sb("ones_row", [1, 128], F32)
    bif_row, _ = sb("bif_row", [1, 16], F32)
    fold_row, _ = sb("fold_row", [1, 8], F32)
    gtmp, _ = sb("gtmp", [1, 2048], F32)
    dump, _ = sb("dump", [128, 2048], F32)

    cvals = [1e-5, 1e-6, LN_SQ, RMS_EPS]
    ctile, _ = sb("cbias", [128, len(cvals)], F32)
    for i, v in enumerate(cvals):
        S.add("vector", lambda e, i=i, v=v: e.memset(ctile[:, i:i + 1], v), writes=("constb",))
        nc.const_aps.register(F32, v, ctile[:, i:i + 1])

    S.dma("sync", ident, ident_in, writes=("ident",))
    S.dma("sync", maskut, mask_in, writes=("maskut",))
    S.dma("sync", bif_row, bif_in, writes=("bif_row",))
    S.dma("sync", fold_row, fold_in, writes=("fold_row",))
    S.dma("sync", gtmp, gamma_in, writes=("gtmp",))
    S.add("vector", lambda e: e.tensor_copy(identb, ident), reads=("ident",), writes=("identb",))
    S.add("vector", lambda e: e.memset(ones_col, 1.0), writes=("ones_col",))
    S.add("vector", lambda e: e.memset(ones_row, 1.0), writes=("ones_row",))

    xt0, c0 = sb("xt0", [128, 2048], F32)
    xt1, c1 = sb("xt1", [128, 2048], F32)
    xnstage, _ = sb("xnstage", [128, 2048], BF16)
    scol0, _ = sb("scol0", [128, 2048], BF16)
    scol1, _ = sb("scol1", [128, 2048], BF16)
    lncol, _ = sb("lncol", [128, 8], F32)
    d_xnfm_v = d_xnfm.rearrange("(dc p) t -> p dc t", p=128)

    ln_spills = []
    for tc in range(16):
        xt, xk = ((xt0, "xt0"), (xt1, "xt1"))[tc % 2]
        scol, sck = ((scol0, "scol0"), (scol1, "scol1"))[tc % 2]
        S.dma("sync", xt, x_in[tc * 128:(tc + 1) * 128, :], writes=(xk,), tag=f"ldx{tc}")
        S.add("scalar", lambda e, xt=xt: e.activation(dump, xt, AF.Copy, accum_out=lncol[:, 0:1]),
              reads=(xk,), writes=("lnsum", "dump"), tag=f"lnA{tc}")
        S.add("scalar", lambda e, xt=xt: e.activation(dump, xt, AF.Square, accum_out=lncol[:, 1:2]),
              reads=(xk,), writes=("lnsq", "dump"), tag=f"lnB{tc}")
        S.add("vector", lambda e: e.tensor_scalar(lncol[:, 2:3], lncol[:, 0:1], 1.0 / 2048.0, None, ALU.mult),
              reads=("lnsum",), writes=("lnmu",))
        S.add("vector", lambda e: e.tensor_scalar(lncol[:, 3:4], lncol[:, 1:2], 1.0 / 2048.0, None, ALU.mult),
              reads=("lnsq",), writes=("lnex2",))
        S.add("vector", lambda e: e.tensor_mul(lncol[:, 4:5], lncol[:, 2:3], lncol[:, 2:3]),
              reads=("lnmu",), writes=("lnmu2",))
        S.add("vector", lambda e: e.tensor_sub(lncol[:, 4:5], lncol[:, 3:4], lncol[:, 4:5]),
              reads=("lnex2", "lnmu2"), writes=("lnvar",))
        S.add("scalar", lambda e: e.activation(lncol[:, 5:6], lncol[:, 4:5], AF.Ln, bias=1e-5),
              reads=("lnvar", "constb"), writes=("lnln",))
        S.add("scalar", lambda e: e.activation(lncol[:, 6:7], lncol[:, 5:6], AF.Exp, scale=-0.5),
              reads=("lnln",), writes=("lnrstd",))
        S.add("vector", lambda e, xt=xt: e.tensor_scalar(xnstage, xt, lncol[:, 2:3], lncol[:, 6:7],
                                                         ALU.subtract, ALU.mult),
              reads=(xk, "lnmu", "lnrstd"), writes=("xnstage",), tag=f"lnap{tc}")
        for dc in range(16):
            bk = dc % 4
            S.add("tensor", lambda e, dc=dc, bk=bk: e.transpose(
                pbb(bk, 128), xnstage[:, dc * 128:(dc + 1) * 128], identb),
                reads=("xnstage", "identb"), writes=(f"pb{bk}",), tag=f"xT{tc}", kind="tp")
            S.add("vector", lambda e, dc=dc, bk=bk, scol=scol: e.tensor_copy(
                scol[:, dc * 128:(dc + 1) * 128], pbb(bk, 128)),
                reads=(f"pb{bk}",), writes=(sck,), tag=f"xTe{tc}")
        S.dma("sync", d_xnfm_v[:, :, tc * 128:(tc + 1) * 128],
              scol.rearrange("p (dc t) -> p dc t", t=128), reads=(sck,),
              writes=(f"dxn{tc}",), tag=f"spxn{tc}")
        ln_spills.append(f"dxn{tc}")
        if tc == 14:
            tap("lncol", lncol, (128, 8), F32, ("lnmu", "lnrstd", "lnsum", "lnsq"))
            tap("xnstage", xnstage, (128, 2048), BF16, ("xnstage",))
            tap("xtile", xt, (128, 2048), F32, (xk,))
    tap("xn", scol0, (128, 2048), BF16, ("scol0",))

    e2, _ = sb("e2", [128, 2], F32)
    e2t, _ = sb("e2t", [2, 128], F32)
    S.add("vector", lambda e: e.memset(e2, 0.0), writes=("e2",))
    S.add("vector", lambda e: e.memset(e2[0:64, 0:1], 1.0), reads=("e2",), writes=("e2",))
    S.add("vector", lambda e: e.memset(e2[64:128, 1:2], 1.0), reads=("e2",), writes=("e2",))
    S.add("tensor", lambda e: e.transpose(pbf(5, 128)[0:2, :], e2, ident),
          reads=("e2", "ident"), writes=("pb5",), tag="e2t_T", kind="tp")
    S.add("vector", lambda e: e.tensor_copy(e2t, pbf(5, 128)[0:2, :]),
          reads=("pb5",), writes=("e2t",))

    qw_bf, _ = sb("qw_bf", [128, 2048], BF16)
    qy, _ = sb("qy", [128, 2048], BF16)
    qwq, _ = sb("qwq", [128, 2048], BF16)
    qm2b, _ = sb("qm2b", [128, 2048], BF16)
    qf1b, _ = sb("qf1b", [128, 2048], BF16)
    qf1, _ = sb("qf1", [128, 2048], F32)
    qf2, _ = sb("qf2", [128, 2048], F32)
    qm2, _ = sb("qm2", [128, 2048], F32)
    qsc, _ = sb("qsc", [2, 2048], F32)
    qs2, _ = sb("qs2", [2, 2048], F32)
    qrc, _ = sb("qrc", [2, 2048], F32)
    qsbt, _ = sb("qsbt", [2, 2048], BF16)
    qsb = qsbt

    def quant_tile(src, dst, rs):
        S.dma("gpsimd", qw_bf, src, writes=("qw_bf",), tag=rs)
        S.add("scalar", lambda e: e.activation(qf1, qw_bf, AF.Abs),
              reads=("qw_bf",), writes=("qf1",))
        for b in range(4):
            bs = slice(b * 512, (b + 1) * 512)
            S.add("tensor", lambda e, bs=bs: e.matmul(pbf(6)[0:2, :], e2, qf1[:, bs],
                                                      start=True, stop=True),
                  reads=("e2", "qf1"), writes=("pb6",), tag=rs + "s")
            S.add("scalar", lambda e, bs=bs: e.copy(qsc[0:2, bs], pbf(6)[0:2, :]),
                  reads=("pb6",), writes=("qsc",))
        S.add("vector", lambda e: e.tensor_single_scalar(qsb, qsc[0:2, :], 1.0 / 64.0, ALU.mult),
              reads=("qsc",), writes=("qscb",))
        S.add("vector", lambda e: e.tensor_single_scalar(qsb, qsb, float(BFnp(1e-8)), ALU.max),
              reads=("qscb",), writes=("qscb",))
        S.add("vector", lambda e: e.tensor_copy(qs2[0:2, :], qsb), reads=("qscb",), writes=("qs2",))
        S.add("vector", lambda e: e.reciprocal(qrc[0:2, :], qsb), reads=("qscb",), writes=("qrec",))
        for b in range(4):
            bs = slice(b * 512, (b + 1) * 512)
            S.add("tensor", lambda e, bs=bs: e.matmul(pbf(6), e2t, qrc[0:2, bs],
                                                      start=True, stop=True),
                  reads=("e2t", "qrec"), writes=("pb6",), tag=rs + "r")
            S.add("vector", lambda e, bs=bs: e.tensor_tensor(qy[:, bs], qw_bf[:, bs], pbf(6), ALU.mult),
                  reads=("qw_bf", "pb6"), writes=("qy",))
            S.add("tensor", lambda e, bs=bs: e.matmul(pbf(7), e2t, qs2[:, bs],
                                                      start=True, stop=True),
                  reads=("e2t", "qs2"), writes=("pb7",), tag=rs + "b")
            S.add("scalar", lambda e, bs=bs: e.copy(qf2[:, bs], pbf(7)),
                  reads=("pb7",), writes=("qf2",))
        S.add("vector", lambda e: e.tensor_single_scalar(qm2b, qy, 0.5, ALU.is_gt),
              reads=("qy",), writes=("qm2b",))
        S.add("vector", lambda e: e.tensor_single_scalar(qf1b, qy, -0.5, ALU.is_lt),
              reads=("qy",), writes=("qf1b",))
        S.add("vector", lambda e: e.tensor_sub(qm2b, qm2b, qf1b), reads=("qm2b", "qf1b"), writes=("qm2b",))
        S.add("vector", lambda e: e.tensor_mul(qm2, qm2b, qf2), reads=("qm2b", "qf2"), writes=("qm2",))
        S.add("vector", lambda e: e.tensor_sub(qy, qm2, qw_bf), reads=("qm2", "qw_bf"), writes=("qy",))
        S.add("vector", lambda e: e.tensor_add(qwq, qw_bf, qy), reads=("qw_bf", "qy"), writes=("qwq",))
        S.dma("sync", dst, qwq, reads=("qwq",), writes=(rs + "_st",), tag=rs)
        return rs + "_st"

    q_st = []
    for pt in range(2):
        for ct in range(8):
            q_st.append(quant_tile(wgu_sh_in[pt * 128:(pt + 1) * 128, ct * 2048:(ct + 1) * 2048],
                                   d_wqgu_sh[pt * 128:(pt + 1) * 128, ct * 2048:(ct + 1) * 2048],
                                   f"qg{pt}{ct}"))
            if pt == 0 and ct == 0:
                tap("qsc0", qsc[0:2, :], (2, 2048), F32, ("qsc",))
                tap("qsb0", qsbt[0:2, :], (2, 2048), BF16, ("qscb",))
                tap("qf10", qf1, (128, 2048), F32, ("qf1",))
                tap("qwbf0", qw_bf, (128, 2048), BF16, ("qw_bf",))
    for pt in range(8):
        q_st.append(quant_tile(wproj_sh_in[pt * 128:(pt + 1) * 128, :],
                               d_wqpr_sh[pt * 128:(pt + 1) * 128, :], f"qp{pt}"))

    AR.free("xt0", "xt1", "xnstage", "scol0", "scol1",
            "qw_bf", "qy", "qwq", "qf1", "qf2", "qm2", "qm2b", "qf1b", "qsc", "qs2", "qrc", "qsbt")


    q_tm, k_tm, v_aug = [], [], []
    for tc in range(16):
        a, _ = sb(f"qtm{tc}", [128, 1024], BF16)
        q_tm.append(a)
    for tc in range(16):
        a, _ = sb(f"ktm{tc}", [128, 1024], BF16)
        k_tm.append(a)
    for tc in range(16):
        a, _ = sb(f"vaug{tc}", [128, NH * VA], BF16)
        v_aug.append(a)
    wblk0, _ = sb("wblk0", [128, 16 * 512], BF16)
    wblk1, _ = sb("wblk1", [128, 16 * 512], BF16)
    xcols = []
    for i in range(2):
        a, _ = sb(f"xcol{i}", [128, 2048], BF16)
        xcols.append(a)
    stage_i, _ = sb("stage_i", [128, 128], F32)
    stage_f, _ = sb("stage_f", [128, 128], F32)
    wifb, _ = sb("wifb", [128, 16 * 16], BF16)

    for dc in range(16):
        S.dma("gpsimd", wifb[:, dc * 16:(dc + 1) * 16], wif_t[dc * 128:(dc + 1) * 128, :],
              writes=("wifb",), tag="ldwif")

    wsrc_v = {"q": wq_t.rearrange("(dc p) n -> p dc n", p=128),
              "k": wk_t.rearrange("(dc p) n -> p dc n", p=128),
              "v": wv_t.rearrange("(dc p) n -> p dc n", p=128),
              "og": wog_t.rearrange("(dc p) n -> p dc n", p=128)}
    wblk_i = [0]

    def load_wblk(which, b):
        wb, wk_ = ((wblk0, "wblk0"), (wblk1, "wblk1"))[wblk_i[0] % 2]
        wblk_i[0] += 1
        src = wsrc_v[which][:, :, b * 512:(b + 1) * 512]
        S.dma("gpsimd", wb.rearrange("p (dc n) -> p dc n", n=512), src,
              writes=(wk_,), tag=f"ldw_{which}{b}")
        return wb, wk_

    def load_xcol(tc, tag):
        xc, xk = xcols[tc % 2], f"xcol{tc % 2}"
        S.dma("sync", xc.rearrange("p (dc t) -> p dc t", t=128),
              d_xnfm_v[:, :, tc * 128:(tc + 1) * 128],
              reads=(f"dxn{tc}",), writes=(xk,), tag=tag)
        return xc, xk

    def proj_pass(which, ncols, evac):
        nblk = ncols // 512
        for b in range(nblk):
            wb, wk_ = load_wblk(which, b)
            for tc in range(16):
                xc, xk = load_xcol(tc, f"x_{which}{b}")
                bk = 2 + (tc % 2)
                for dc in range(16):
                    S.add("tensor", lambda e, dc=dc, bk=bk, wb=wb, xc=xc:
                          e.matmul(pbf(bk), xc[:, dc * 128:(dc + 1) * 128],
                                   wb[:, dc * 512:(dc + 1) * 512], start=(dc == 0), stop=(dc == 15)),
                          reads=(xk, wk_), writes=(f"pb{bk}",), tag=f"{which}mm{b}_{tc}")
                evac(tc, b, bk)

    def ev_q(tc, b, bk):
        S.add("vector", lambda e: e.tensor_copy(q_tm[tc][:, b * 512:(b + 1) * 512], pbf(bk)),
              reads=(f"pb{bk}",), writes=(f"qtm{tc}",), tag="evq")

    def ev_k(tc, b, bk):
        S.add("vector", lambda e: e.tensor_copy(k_tm[tc][:, b * 512:(b + 1) * 512], pbf(bk)),
              reads=(f"pb{bk}",), writes=(f"ktm{tc}",), tag="evk")

    for tc in range(16):
        S.add("vector", lambda e, tc=tc: e.memset(
            v_aug[tc].rearrange("p (h va) -> p h va", va=VA)[:, :, DV:DV + 1], 1.0),
            writes=(f"vaug{tc}",))

    def ev_v(tc, b, bk):
        for hh in range(2):
            h = 2 * b + hh
            S.add("vector", lambda e, h=h, hh=hh, bk=bk: e.tensor_copy(
                v_aug[tc][:, h * VA:h * VA + DV], pbf(bk)[:, hh * 256:(hh + 1) * 256]),
                reads=(f"pb{bk}",), writes=(f"vaug{tc}",), tag="evv")

    proj_pass("q", 1024, ev_q)
    proj_pass("k", 1024, ev_k)
    proj_pass("v", 2048, ev_v)

    for tc in range(16):
        xc, xk = load_xcol(tc, "x_if")
        for dc in range(16):
            S.add("tensor", lambda e, dc=dc, xc=xc: e.matmul(
                pbf(4, 16), xc[:, dc * 128:(dc + 1) * 128], wifb[:, dc * 16:(dc + 1) * 16],
                start=(dc == 0), stop=False),
                reads=(xk, "wifb"), writes=("pb4",), tag=f"ifmm{tc}")
        S.add("tensor", lambda e: e.matmul(pbf(4, 16), ones_row, bif_row, start=False, stop=True),
              reads=("ones_row", "bif_row"), writes=("pb4",), tag=f"ifb{tc}")
        S.add("scalar", lambda e: e.activation(pbf(5, 16), pbf(4, 16), AF.Tanh, scale=1.0 / 15.0),
              reads=("pb4",), writes=("pb5",), tag=f"iftanh{tc}")
        S.add("vector", lambda e, tc=tc: e.tensor_single_scalar(
            stage_i.rearrange("p (h c) -> p h c", c=16)[:, :, tc:tc + 1],
            pbf(5, 16)[:, 0:8].unsqueeze(2), 15.0, ALU.mult),
            reads=("pb5",), writes=("stage_i",), tag=f"ifi{tc}")
        S.add("vector", lambda e, tc=tc: e.tensor_single_scalar(
            stage_f.rearrange("p (h c) -> p h c", c=16)[:, :, tc:tc + 1],
            pbf(5, 16)[:, 8:16].unsqueeze(2), 15.0, ALU.mult),
            reads=("pb5",), writes=("stage_f",), tag=f"iff{tc}")

    tap("q0", q_tm[0], (128, 1024), BF16, ("qtm0",))
    tap("v0", v_aug[0], (128, NH * VA), BF16, ("vaug0",))
    tap("stage_i", stage_i, (128, 128), F32, ("stage_i",))

    AR.free("wblk1")

    g_i, _ = sb("g_i", [128, 128], F32)
    g_f, _ = sb("g_f", [128, 128], F32)
    g_b, _ = sb("g_b", [128, 128], F32)
    g_B, _ = sb("g_B", [128, 128], F32)
    g_at, _ = sb("g_at", [128, 128], F32)
    g_cmx, _ = sb("g_cmx", [128, 128], F32)
    g_s1, _ = sb("g_s1", [128, 128], F32)
    g_s2, _ = sb("g_s2", [128, 128], F32)

    def transpose128(dst, dst_key, src, src_keys, bk, dt=F32, tag="tp"):
        idt = ident if dt == F32 else identb
        ikey = "ident" if dt == F32 else "identb"
        po, pf = dst.shape[0], dst.shape[1]
        pb = (pbf(bk, pf) if dt == F32 else pbb(bk, pf))[0:po, :]
        S.add("tensor", lambda e: e.transpose(pb, src, idt[0:src.shape[0], 0:src.shape[0]]),
              reads=tuple(src_keys) + (ikey,), writes=(f"pb{bk}",), tag=tag, kind="tp")
        S.add("vector", lambda e: e.tensor_copy(dst, pb),
              reads=(f"pb{bk}",), writes=(dst_key,), tag=tag + "e")

    transpose128(g_i, "g_i", stage_i, ("stage_i",), 0, tag="tpi")
    transpose128(g_f, "g_f", stage_f, ("stage_f",), 1, tag="tpf")
    AR.free("stage_i", "stage_f")
    S.add("scalar", lambda e: e.activation(g_s1, g_f, AF.Abs),
          reads=("g_f",), writes=("g_s1",))
    S.add("scalar", lambda e: e.activation(g_s2, g_s1, AF.Exp, scale=-1.0),
          reads=("g_s1",), writes=("g_s2",))
    S.add("scalar", lambda e: e.activation(g_s1, g_s2, AF.Ln, bias=1.0),
          reads=("g_s2",), writes=("g_s1",))
    S.add("vector", lambda e: e.tensor_single_scalar(g_b, g_f, 0.0, ALU.min),
          reads=("g_f",), writes=("g_b",))
    S.add("vector", lambda e: e.tensor_sub(g_b, g_b, g_s1), reads=("g_b", "g_s1"), writes=("g_b",))
    S.add("vector", lambda e: e.tensor_tensor_scan(g_B, g_b, g_b, 0.0, ALU.add, ALU.bypass),
          reads=("g_b",), writes=("g_B",))
    S.add("vector", lambda e: e.tensor_sub(g_at, g_i, g_B), reads=("g_i", "g_B"), writes=("g_at",))
    S.add("vector", lambda e: e.tensor_tensor_scan(g_cmx, g_at, g_at, -1e30, ALU.max, ALU.max),
          reads=("g_at",), writes=("g_cmx",))

    rows = {}
    for rn in ("BL", "cmxL", "Pinc", "Pexc", "e_r", "Ghat", "mh_t", "mhat", "m0bc",
               "G_r", "Gnx", "lam", "d8", "m8", "sc16"):
        rows[rn], _ = sb("r_" + rn, [1, 128], F32)

    transpose128(rows["BL"], "r_BL", g_B[:, 127:128], ("g_B",), 2, tag="tBL")
    transpose128(rows["cmxL"], "r_cmxL", g_cmx[:, 127:128], ("g_cmx",), 3, tag="tcm")
    S.add("vector", lambda e: e.memset(rows["Pexc"], 0.0), writes=("r_Pexc",))
    S.add("vector", lambda e: e.memset(rows["Ghat"], -1e30), writes=("r_Ghat",))
    for h in range(NH):
        o = h * 16
        S.add("vector", lambda e, o=o: e.tensor_tensor_scan(
            rows["Pinc"][0:1, o:o + 16], rows["BL"][0:1, o:o + 16], rows["BL"][0:1, o:o + 16],
            0.0, ALU.add, ALU.bypass), reads=("r_BL",), writes=("r_Pinc",), tag="scP")
        S.add("vector", lambda e, o=o: e.tensor_copy(rows["Pexc"][0:1, o + 1:o + 16],
                                                     rows["Pinc"][0:1, o:o + 15]),
              reads=("r_Pinc",), writes=("r_Pexc",), tag="cpP")
    S.add("vector", lambda e: e.tensor_sub(rows["e_r"], rows["cmxL"], rows["Pexc"]),
          reads=("r_cmxL", "r_Pexc"), writes=("r_e_r",))
    for h in range(NH):
        o = h * 16
        S.add("vector", lambda e, o=o: e.tensor_tensor_scan(
            rows["Ghat"][0:1, o + 1:o + 16], rows["e_r"][0:1, o:o + 15], rows["e_r"][0:1, o:o + 15],
            -1e30, ALU.max, ALU.max), reads=("r_e_r",), writes=("r_Ghat",), tag="scG")
    S.add("vector", lambda e: e.tensor_max(rows["mh_t"], rows["Ghat"], rows["e_r"]),
          reads=("r_Ghat", "r_e_r"), writes=("r_mh_t",))
    v16 = lambda r: r.rearrange("o (h c) -> o h c", c=16)
    S.add("vector", lambda e: e.tensor_copy(
        v16(rows["mhat"]), v16(rows["mh_t"])[:, :, 15:16].broadcast_to([1, 8, 16])),
        reads=("r_mh_t",), writes=("r_mhat",))
    S.add("vector", lambda e: e.tensor_copy(
        v16(rows["d8"]), v16(rows["Pinc"])[:, :, 15:16].broadcast_to([1, 8, 16])),
        reads=("r_Pinc",), writes=("r_d8",))
    S.add("vector", lambda e: e.tensor_copy(rows["sc16"][0:1, 0:8],
                                            rows["d8"].rearrange("o (h c) -> o h c", c=16)[:, :, 0:1].rearrange("o h c -> o (h c)")),
          reads=("r_d8",), writes=("r_sc16",))
    S.add("vector", lambda e: e.tensor_copy(rows["sc16"][0:1, 8:16],
                                            rows["mhat"].rearrange("o (h c) -> o h c", c=16)[:, :, 0:1].rearrange("o h c -> o (h c)")),
          reads=("r_mhat",), writes=("r_sc16",))
    S.dma("sync", d_scal_c, rows["sc16"][0:1, 0:16], reads=("r_sc16",), writes=("d_scal_c",))

    def row2col(dst, dkey, row_ap, rkeys, bk, tag="r2c"):
        S.add("tensor", lambda e: e.matmul(pbf(bk, 1), row_ap, ones_row[0:1, 0:1],
                                           start=True, stop=True),
              reads=tuple(rkeys) + ("ones_row",), writes=(f"pb{bk}",), tag=tag)
        S.add("vector", lambda e: e.tensor_copy(dst, pbf(bk, 1)),
              reads=(f"pb{bk}",), writes=(dkey,), tag=tag + "e")

    cols = {}
    for cn in ("Pexc", "mhat", "G", "Gnx"):
        cols[cn], _ = sb("c_" + cn, [128, 1], F32)
    row2col(cols["Pexc"], "c_Pexc", rows["Pexc"], ("r_Pexc",), 4, tag="rcP")
    row2col(cols["mhat"], "c_mhat", rows["mhat"], ("r_mhat",), 4, tag="rcM")

    g_ks, _ = sb("g_ks", [128, 128], F32)
    S.add("vector", lambda e: e.tensor_scalar(g_s1, g_at, cols["Pexc"], cols["mhat"],
                                              ALU.subtract, ALU.subtract),
          reads=("g_at", "c_Pexc", "c_mhat"), writes=("g_s1",))
    S.add("scalar", lambda e: e.activation(g_ks, g_s1, AF.Exp), reads=("g_s1",), writes=("g_ks",))
    ks_t, _ = sb("ks_t", [128, 128], F32)
    transpose128(ks_t, "ks_t", g_ks, ("g_ks",), 5, tag="tKS")

    kw_s, _ = sb("kw_s", [128, 128], BF16)
    sum_stage, _ = sb("sum_stage", [128, VA], F32)
    sum_sts = []
    for h in range(NH):
        bk = 6 + (h % 2)
        for c in range(NC):
            S.add("vector", lambda e, h=h, c=c: e.tensor_scalar(
                kw_s, k_tm[c][:, h * 128:(h + 1) * 128], ks_t[:, h * 16 + c:h * 16 + c + 1],
                None, ALU.mult),
                reads=(f"ktm{c}", "ks_t"), writes=("kw_s",), tag=f"ksw{h}_{c}")
            S.add("tensor", lambda e, h=h, c=c, bk=bk: e.matmul(
                pbf(bk, VA), kw_s, v_aug[c][:, h * VA:(h + 1) * VA],
                start=(c == 0), stop=(c == NC - 1)),
                reads=("kw_s", f"vaug{c}"), writes=(f"pb{bk}",), tag=f"summ{h}")
        S.add("vector", lambda e, bk=bk: e.tensor_copy(sum_stage, pbf(bk, VA)),
              reads=(f"pb{bk}",), writes=("sum_stage",), tag=f"sume{h}")
        S.dma("sync", d_sum_c[h * 128:(h + 1) * 128, :], sum_stage,
              reads=("sum_stage",), writes=(f"dsum{h}",), tag=f"sumst{h}")
        sum_sts.append(f"dsum{h}")

    sgt_e = dump[:, 0:512]
    sgb_e = dump.bitcast(BF16)[:, 1024:1536]
    for b in range(4):
        wb, wk_ = wblk0, "wblk0"
        S.dma("gpsimd", wb.rearrange("p (dc n) -> p dc n", n=512),
              wsrc_v["og"][:, :, b * 512:(b + 1) * 512], writes=(wk_,), tag=f"ldwog{b}")
        for tc in range(16):
            xc, xk = load_xcol(tc, f"x_og{b}")
            bk = 2 + (tc % 2)
            for dc in range(16):
                S.add("tensor", lambda e, dc=dc, bk=bk, wb=wb, xc=xc: e.matmul(
                    pbf(bk), xc[:, dc * 128:(dc + 1) * 128], wb[:, dc * 512:(dc + 1) * 512],
                    start=(dc == 0), stop=(dc == 15)),
                    reads=(xk, wk_), writes=(f"pb{bk}",), tag=f"ogmm{b}_{tc}")
            S.add("scalar", lambda e, bk=bk: e.activation(sgt_e, pbf(bk), AF.Sigmoid),
                  reads=(f"pb{bk}",), writes=("dump",), tag=f"ogsig{b}_{tc}")
            S.add("vector", lambda e: e.tensor_copy(sgb_e, sgt_e), reads=("dump",), writes=("dump",))
            S.dma("sync", d_sigo[tc * 128:(tc + 1) * 128, b * 512:(b + 1) * 512], sgb_e,
                  reads=("dump",), writes=(f"dsg{b}_{tc}",), tag=f"stsg{b}_{tc}")


    S.add("gpsimd", lambda e: e.collective_compute(
        "AllGather", ALU.bypass, replica_groups=[list(range(NCORES))],
        ins=[d_wqgu_sh.opt()], outs=[d_wqgu.opt()]),
        reads=tuple(q_st[:16]), writes=("d_wqgu",), collective=True, tag="ag_gu")
    S.add("gpsimd", lambda e: e.collective_compute(
        "AllGather", ALU.bypass, replica_groups=[list(range(NCORES))],
        ins=[d_wqpr_sh.opt()], outs=[d_wqpr.opt()]),
        reads=tuple(q_st[16:]), writes=("d_wqpr",), collective=True, tag="ag_pr")

    S.add("gpsimd", lambda e: e.collective_compute(
        "AllGather", ALU.bypass, replica_groups=[list(range(NCORES))],
        ins=[d_scal_c.opt()], outs=[d_scal_all.opt()]),
        reads=("d_scal_c",), writes=("d_scal_all",), collective=True, tag="ag_sc")
    S.add("gpsimd", lambda e: e.collective_compute(
        "AllGather", ALU.bypass, replica_groups=[list(range(NCORES))],
        ins=[d_sum_c.opt()], outs=[d_sum_all.opt()]),
        reads=tuple(sum_sts), writes=("d_sum_all",), collective=True, tag="ag_sum")

    scal_all, _ = sb("scal_all", [1, 128], F32)
    S.dma("sync", scal_all, d_scal_all.rearrange("r k -> (r k)").unsqueeze(0),
          reads=("d_scal_all",), writes=("scal_all",), tag="ld_scal")
    fr = {}
    for rn in ("mst", "nu", "meff", "ea", "eb", "t1"):
        fr[rn], _ = sb("f_" + rn, [1, 8], F32)
    bc_ea, _ = sb("bc_ea", [128, 8], F32)
    bc_eb, _ = sb("bc_eb", [128, 8], F32)
    C_st, C_bf = [], []
    for h in range(NH):
        a, _ = sb(f"Cst{h}", [128, VA], F32)
        C_st.append(a)
        a, _ = sb(f"Cbf{h}", [128, VA], BF16)
        C_bf.append(a)
    csc, _ = sb("csc", [128, VA], F32)
    S.add("vector", lambda e: e.memset(fr["mst"], 0.0), writes=("f_mst",))
    for h in range(NH):
        S.add("vector", lambda e, h=h: e.memset(C_st[h], 0.0), writes=(f"Cst{h}",))
    for r in range(NCORES):
        dr = scal_all[0:1, r * 16:r * 16 + 8]
        mr = scal_all[0:1, r * 16 + 8:r * 16 + 16]
        mk_ = fold_row[0:1, r:r + 1]
        S.add("vector", lambda e, mk_=mk_: e.tensor_scalar(fr["t1"][0:1, 0:1], mk_, -1.0, 1e30,
                                                           ALU.add, ALU.mult),
              reads=("fold_row",), writes=("f_t1",), tag=f"fd{r}a0")
        S.add("vector", lambda e, mr=mr, mk_=mk_: e.tensor_scalar(fr["meff"], mr, mk_,
                                                                  fr["t1"][0:1, 0:1],
                                                                  ALU.mult, ALU.add),
              reads=("scal_all", "fold_row", "f_t1"), writes=("f_meff",), tag=f"fd{r}a")
        S.add("vector", lambda e: e.tensor_max(fr["nu"], fr["mst"], fr["meff"]),
              reads=("f_mst", "f_meff"), writes=("f_nu",))
        S.add("vector", lambda e: e.tensor_sub(fr["t1"], fr["mst"], fr["nu"]),
              reads=("f_mst", "f_nu"), writes=("f_t1",))
        S.add("scalar", lambda e: e.activation(fr["ea"], fr["t1"], AF.Exp),
              reads=("f_t1",), writes=("f_ea",))
        S.add("vector", lambda e: e.tensor_sub(fr["t1"], fr["meff"], fr["nu"]),
              reads=("f_meff", "f_nu"), writes=("f_t1",))
        S.add("scalar", lambda e: e.activation(fr["eb"], fr["t1"], AF.Exp),
              reads=("f_t1",), writes=("f_eb",))
        S.add("vector", lambda e, dr=dr, mk_=mk_: e.tensor_scalar(fr["t1"], dr, mk_, None, ALU.mult),
              reads=("scal_all", "fold_row"), writes=("f_t1",))
        S.add("vector", lambda e: e.tensor_add(fr["mst"], fr["nu"], fr["t1"]),
              reads=("f_nu", "f_t1"), writes=("f_mst",), tag=f"fd{r}m")
        S.add("tensor", lambda e: e.matmul(pbf(0, 8), ones_row, fr["ea"], start=True, stop=True),
              reads=("ones_row", "f_ea"), writes=("pb0",), tag=f"fd{r}ba")
        S.add("vector", lambda e: e.tensor_copy(bc_ea, pbf(0, 8)), reads=("pb0",), writes=("bc_ea",))
        S.add("tensor", lambda e: e.matmul(pbf(0, 8), ones_row, fr["eb"], start=True, stop=True),
              reads=("ones_row", "f_eb"), writes=("pb0",), tag=f"fd{r}bb")
        S.add("vector", lambda e: e.tensor_copy(bc_eb, pbf(0, 8)), reads=("pb0",), writes=("bc_eb",))
        for h in range(NH):
            S.dma("sync", csc, d_sum_all[(r * NH + h) * 128:(r * NH + h + 1) * 128, :],
                  reads=("d_sum_all",), writes=("csc",), tag=f"fl{r}_{h}")
            S.add("vector", lambda e, h=h: e.tensor_scalar(csc, csc, bc_eb[:, h:h + 1], None, ALU.mult),
                  reads=("csc", "bc_eb"), writes=("csc",))
            S.add("vector", lambda e, h=h: e.scalar_tensor_tensor(
                C_st[h], C_st[h], bc_ea[:, h:h + 1], csc, ALU.mult, ALU.add),
                reads=(f"Cst{h}", "bc_ea", "csc"), writes=(f"Cst{h}",), tag=f"fC{r}_{h}")
    for h in range(NH):
        S.add("vector", lambda e, h=h: e.tensor_copy(C_bf[h], C_st[h]),
              reads=(f"Cst{h}",), writes=(f"Cbf{h}",))
    S.add("vector", lambda e: e.tensor_copy(
        v16(rows["m0bc"]), fr["mst"].unsqueeze(2).broadcast_to([1, 8, 16])),
        reads=("f_mst",), writes=("r_m0bc",))
    tap("m0", fr["mst"], (1, 8), F32, ("f_mst",))
    tap("scal_all", scal_all, (1, 128), F32, ("scal_all",))
    tap("sc16", rows["sc16"], (1, 128), F32, ("r_sc16",))
    tap("C0", C_st[0], (128, VA), F32, ("Cst0",))

    AR.free("r_BL", "r_cmxL", "r_Pinc", "r_e_r", "r_d8", "r_m8", "r_sc16",
            "c_mhat", "g_i", "g_b", "g_ks", "ks_t", "kw_s", "sum_stage")

    S.add("vector", lambda e: e.tensor_max(rows["G_r"], rows["Ghat"], rows["m0bc"]),
          reads=("r_Ghat", "r_m0bc"), writes=("r_G_r",))
    for h in range(NH):
        o = h * 16
        S.add("vector", lambda e, o=o: e.tensor_copy(rows["Gnx"][0:1, o:o + 15],
                                                     rows["G_r"][0:1, o + 1:o + 16]),
              reads=("r_G_r",), writes=("r_Gnx",), tag="gnx")
    S.add("vector", lambda e: e.tensor_max(rows["mh_t"], rows["mhat"], rows["m0bc"]),
          reads=("r_mhat", "r_m0bc"), writes=("r_mh_t",))
    S.add("vector", lambda e: e.tensor_copy(
        v16(rows["Gnx"])[:, :, 15:16], v16(rows["mh_t"])[:, :, 15:16]),
        reads=("r_mh_t",), writes=("r_Gnx",))
    S.add("vector", lambda e: e.tensor_sub(rows["lam"], rows["G_r"], rows["Gnx"]),
          reads=("r_G_r", "r_Gnx"), writes=("r_lam",))
    S.add("scalar", lambda e: e.activation(rows["lam"], rows["lam"], AF.Exp),
          reads=("r_lam",), writes=("r_lam",))
    row2col(cols["G"], "c_G", rows["G_r"], ("r_G_r",), 4, tag="rcG")
    row2col(cols["Gnx"], "c_Gnx", rows["Gnx"], ("r_Gnx",), 4, tag="rcGn")

    g_S, _ = sb("g_S", [128, 128], F32)
    qf_t, _ = sb("qf_t", [128, 128], F32)
    row_t, _ = sb("row_t", [128, 128], F32)
    kw_t, _ = sb("kw_t", [128, 128], F32)
    enm_t, _ = sb("enm_t", [128, 128], F32)
    bclam, _ = sb("bclam", [128, 128], F32)

    S.add("vector", lambda e: e.tensor_scalar(g_S, g_cmx, cols["Pexc"], cols["G"],
                                              ALU.subtract, ALU.max),
          reads=("g_cmx", "c_Pexc", "c_G"), writes=("g_S",))
    S.add("vector", lambda e: e.tensor_scalar(g_s1, g_S, cols["G"], None, ALU.subtract),
          reads=("g_S", "c_G"), writes=("g_s1",))
    S.add("scalar", lambda e: e.activation(g_s2, g_s1, AF.Exp, scale=-1.0, bias=LN_SQ),
          reads=("g_s1", "constb"), writes=("g_s2",))
    transpose128(qf_t, "qf_t", g_s2, ("g_s2",), 5, tag="tQF")
    S.add("vector", lambda e: e.tensor_scalar(g_s1, g_at, cols["Pexc"], cols["G"],
                                              ALU.subtract, ALU.subtract),
          reads=("g_at", "c_Pexc", "c_G"), writes=("g_s1",))
    S.add("scalar", lambda e: e.activation(g_s2, g_s1, AF.Exp), reads=("g_s1",), writes=("g_s2",))
    transpose128(row_t, "row_t", g_s2, ("g_s2",), 5, tag="tRW")
    S.add("vector", lambda e: e.tensor_scalar(g_s1, g_at, cols["Pexc"], cols["Gnx"],
                                              ALU.subtract, ALU.subtract),
          reads=("g_at", "c_Pexc", "c_Gnx"), writes=("g_s1",))
    S.add("scalar", lambda e: e.activation(g_s2, g_s1, AF.Exp), reads=("g_s1",), writes=("g_s2",))
    transpose128(kw_t, "kw_t", g_s2, ("g_s2",), 5, tag="tKW")
    S.add("vector", lambda e: e.scalar_tensor_tensor(g_s1, g_B, cols["Pexc"], g_S,
                                                     ALU.add, ALU.add),
          reads=("g_B", "c_Pexc", "g_S"), writes=("g_s1",))
    S.add("scalar", lambda e: e.activation(g_s2, g_s1, AF.Exp, scale=-1.0),
          reads=("g_s1",), writes=("g_s2",))
    transpose128(enm_t, "enm_t", g_s2, ("g_s2",), 5, tag="tEN")
    S.add("tensor", lambda e: e.matmul(pbf(5, 128), ones_row, rows["lam"], start=True, stop=True),
          reads=("ones_row", "r_lam"), writes=("pb5",), tag="bclam")
    S.add("vector", lambda e: e.tensor_copy(bclam, pbf(5, 128)), reads=("pb5",), writes=("bclam",))
    tap("qf_t", qf_t, (128, 128), F32, ("qf_t",))
    tap("enm_t", enm_t, (128, 128), F32, ("enm_t",))

    AR.free("wblk0", "xcol0", "xcol1", "wifb")

    gamma_bc, _ = sb("gamma_bc", [128, 2048], F32)
    for b in range(4):
        S.add("tensor", lambda e, b=b: e.matmul(pbf(5), ones_row, gtmp[:, b * 512:(b + 1) * 512],
                                                start=True, stop=True),
              reads=("ones_row", "gtmp"), writes=("pb5",), tag="gbc")
        S.add("vector", lambda e, b=b: e.tensor_copy(gamma_bc[:, b * 512:(b + 1) * 512], pbf(5)),
              reads=("pb5",), writes=("gamma_bc",))
    AR.free("gtmp")
    qs_s, qfm, kfm, at_b, kws, dcol = [], [], [], [], [], []
    for p in range(NH):
        a, _ = sb(f"qs_s{p}", [128, 128], BF16); qs_s.append(a)
        a, _ = sb(f"qfm{p}", [128, 128], BF16); qfm.append(a)
        a, _ = sb(f"kfm{p}", [128, 128], BF16); kfm.append(a)
        a, _ = sb(f"at_b{p}", [128, 128], BF16); at_b.append(a)
        a, _ = sb(f"kws{p}", [128, 128], BF16); kws.append(a)
        a, _ = sb(f"dcol{p}", [128, 4], F32); dcol.append(a)
    h_stage, _ = sb("h_stage", [128, 2048], F32)
    hn_bf, _ = sb("hn_bf", [128, 2048], BF16)
    mh1, _ = sb("mh1", [128, 8], F32)
    mh2, _ = sb("mh2", [128, 8], F32)
    mh3 = dump

    hn_spills = []
    for c in range(NC):
        for h in range(NH):
            hc = h * 16 + c
            S.add("vector", lambda e, c=c, h=h, hc=hc: e.tensor_scalar(
                qs_s[h], q_tm[c][:, h * 128:(h + 1) * 128], qf_t[:, hc:hc + 1], None, ALU.mult),
                reads=(f"qtm{c}", "qf_t"), writes=(f"qs_s{h}",), tag=f"qs{c}_{h}")
            S.add("vector", lambda e, c=c, h=h, hc=hc: e.tensor_scalar(
                kws[h], k_tm[c][:, h * 128:(h + 1) * 128], kw_t[:, hc:hc + 1], None, ALU.mult),
                reads=(f"ktm{c}", "kw_t"), writes=(f"kws{h}",), tag=f"kw{c}_{h}")
        for h in range(NH):
            transpose128(qfm[h], f"qfm{h}", qs_s[h], (f"qs_s{h}",), h % 4,
                         dt=BF16, tag=f"qT{c}_{h}")
        for h in range(NH):
            transpose128(kfm[h], f"kfm{h}", k_tm[c][:, h * 128:(h + 1) * 128],
                         (f"ktm{c}",), 4 + h % 4, dt=BF16, tag=f"kT{c}_{h}")
        for h in range(NH):
            S.add("tensor", lambda e, h=h: e.matmul(pbf(h, 128), kfm[h], qfm[h],
                                                    start=True, stop=True),
                  reads=(f"kfm{h}", f"qfm{h}"), writes=(f"pb{h}",), tag=f"PT{c}_{h}")
        for h in range(NH):
            hc = h * 16 + c
            S.add("vector", lambda e, h=h, hc=hc: e.scalar_tensor_tensor(
                at_b[h], pbf(h, 128), row_t[:, hc:hc + 1], maskut, ALU.mult, ALU.mult),
                reads=(f"pb{h}", "row_t", "maskut"), writes=(f"at_b{h}",), tag=f"AT{c}_{h}")
        for h in range(NH):
            S.add("tensor", lambda e, h=h: e.matmul(pbf(h, VA), qfm[h], C_bf[h],
                                                    start=True, stop=False),
                  reads=(f"qfm{h}", f"Cbf{h}"), writes=(f"pb{h}",), tag=f"hin{c}_{h}")
            S.add("tensor", lambda e, c=c, h=h: e.matmul(pbf(h, VA), at_b[h],
                                                         v_aug[c][:, h * VA:(h + 1) * VA],
                                                         start=False, stop=True),
                  reads=(f"at_b{h}", f"vaug{c}"), writes=(f"pb{h}",), tag=f"hia{c}_{h}")
        for h in range(NH):
            S.add("scalar", lambda e, h=h: e.activation(dcol[h][:, 0:1],
                                                        pbf(h, VA)[:, DV:DV + 1], AF.Abs),
                  reads=(f"pb{h}",), writes=(f"dcol{h}a",))
        for h in range(NH):
            hc = h * 16 + c
            S.add("vector", lambda e, h=h, hc=hc: e.tensor_scalar(
                dcol[h][:, 1:2], dcol[h][:, 0:1], enm_t[:, hc:hc + 1], 1e-6, ALU.max, ALU.add),
                reads=(f"dcol{h}a", "enm_t"), writes=(f"dcol{h}b",))
            S.add("vector", lambda e, h=h: e.reciprocal(dcol[h][:, 2:3], dcol[h][:, 1:2]),
                  reads=(f"dcol{h}b",), writes=(f"dcol{h}c",))
        for h in range(NH):
            S.add("vector", lambda e, h=h: e.tensor_scalar(
                h_stage[:, h * 256:(h + 1) * 256], pbf(h, VA)[:, 0:DV], dcol[h][:, 2:3],
                None, ALU.mult),
                reads=(f"pb{h}", f"dcol{h}c"), writes=("h_stage",), tag=f"hd{c}_{h}")
        for h in range(NH):
            S.add("tensor", lambda e, c=c, h=h: e.matmul(pbf(h, VA), kws[h],
                                                         v_aug[c][:, h * VA:(h + 1) * VA],
                                                         start=True, stop=True),
                  reads=(f"kws{h}", f"vaug{c}"), writes=(f"pb{h}",), tag=f"cu{c}_{h}")
        for h in range(NH):
            hc = h * 16 + c
            S.add("vector", lambda e, h=h, hc=hc: e.scalar_tensor_tensor(
                C_st[h], C_st[h], bclam[:, hc:hc + 1], pbf(h, VA), ALU.mult, ALU.add),
                reads=(f"Cst{h}", "bclam", f"pb{h}"), writes=(f"Cst{h}",), tag=f"cn{c}_{h}")
            S.add("vector", lambda e, h=h: e.tensor_copy(C_bf[h], C_st[h]),
                  reads=(f"Cst{h}",), writes=(f"Cbf{h}",), tag=f"cb{c}_{h}")
        hsv = h_stage.rearrange("p (h v) -> p h v", v=256)
        S.add("vector", lambda e: e.tensor_reduce(mh1, hsv, AX.X, ALU.add),
              reads=("h_stage",), writes=("mh1",), tag=f"mn1{c}")
        S.add("scalar", lambda e: e.activation(mh3, h_stage, AF.Square),
              reads=("h_stage",), writes=("dump",), tag=f"mn2{c}")
        S.add("vector", lambda e: e.tensor_reduce(mh2, mh3.rearrange("p (h v) -> p h v", v=256),
                                                  AX.X, ALU.add),
              reads=("dump",), writes=("mh2",), tag=f"mn3{c}")
        S.add("vector", lambda e: e.tensor_scalar(mh1, mh1, 1.0 / 256.0, None, ALU.mult),
              reads=("mh1",), writes=("mh1",))
        S.add("vector", lambda e: e.tensor_scalar(mh2, mh2, 1.0 / 256.0, None, ALU.mult),
              reads=("mh2",), writes=("mh2",))
        S.add("vector", lambda e: e.tensor_mul(dump[:, 0:8], mh1, mh1), reads=("mh1",), writes=("dump",))
        S.add("vector", lambda e: e.tensor_sub(mh2, mh2, dump[:, 0:8]),
              reads=("mh2", "dump"), writes=("mh2",))
        S.add("scalar", lambda e: e.activation(mh2, mh2, AF.Ln, bias=1e-6),
              reads=("mh2", "constb"), writes=("mh2",))
        S.add("scalar", lambda e: e.activation(mh2, mh2, AF.Exp, scale=-0.5),
              reads=("mh2",), writes=("mh2",))
        S.add("vector", lambda e: e.tensor_tensor(hsv, hsv,
                                                  mh1.unsqueeze(2).broadcast_to([128, 8, 256]),
                                                  ALU.subtract),
              reads=("h_stage", "mh1"), writes=("h_stage",), tag=f"mn4{c}")
        S.add("vector", lambda e: e.tensor_tensor(hsv, hsv,
                                                  mh2.unsqueeze(2).broadcast_to([128, 8, 256]),
                                                  ALU.mult),
              reads=("h_stage", "mh2"), writes=("h_stage",), tag=f"mn5{c}")
        S.add("vector", lambda e: e.tensor_mul(hn_bf, h_stage, gamma_bc),
              reads=("h_stage", "gamma_bc"), writes=("hn_bf",), tag=f"mn6{c}")
        S.dma("sync", d_hnorm[c * 128:(c + 1) * 128, :], hn_bf, reads=("hn_bf",),
              writes=(f"dhn{c}",), tag=f"sthn{c}")
        hn_spills.append(f"dhn{c}")
        if c == 0:
            tap("h0", h_stage, (128, 2048), F32, ("h_stage",))

    AR.free(*[f"qtm{t}" for t in range(16)], *[f"ktm{t}" for t in range(16)],
            *[f"vaug{t}" for t in range(16)])
    AR.free("g_f", "g_B", "g_at", "g_cmx", "g_s1", "g_s2", "g_S",
            "scal_all", "bc_ea", "bc_eb",
            "csc", *[f"{nm}{p}" for nm in ("qs_s", "qfm", "kfm", "at_b", "kws", "dcol") for p in range(NH)],
            "h_stage", "hn_bf", "mh1", "mh2",
            "qf_t", "row_t", "kw_t", "enm_t", "bclam",
            *[f"Cst{h}" for h in range(8)], *[f"Cbf{h}" for h in range(8)],
            *["r_" + rn for rn in ("Pexc", "Ghat", "mh_t", "mhat", "m0bc",
                                   "G_r", "Gnx", "lam")],
            *["c_" + cn for cn in ("Pexc", "G", "Gnx")],
            *["f_" + rn for rn in ("mst", "nu", "meff", "ea", "eb", "t1")],
            "e2", "e2t")

    if "hnormfull" in taps:
        o = nc.dram_tensor("tap_hnormfull", [T, 2048], BF16, kind="ExternalOutput").ap()
        S.dma("sync", o, d_hnorm, reads=tuple(hn_spills), writes=("taphnf",))

    hout = []
    for tc in range(16):
        a, cf = sb(f"hout{tc}", [128, 2048], BF16)
        hout.append((a, cf))
    hnt0, _ = sb("hnt0", [128, 512], BF16)
    hnt1, _ = sb("hnt1", [128, 512], BF16)
    sgl0, _ = sb("sgl0", [128, 512], BF16)
    sgl1, _ = sb("sgl1", [128, 512], BF16)
    hob, _ = sb("hob", [128, 512], BF16)

    for b in range(4):
        for tc in range(16):
            hnt, hk = ((hnt0, "hnt0"), (hnt1, "hnt1"))[tc % 2]
            sgl, sk_ = ((sgl0, "sgl0"), (sgl1, "sgl1"))[tc % 2]
            S.dma("sync", hnt, d_hnorm[tc * 128:(tc + 1) * 128, b * 512:(b + 1) * 512],
                  reads=(f"dhn{tc}",), writes=(hk,), tag=f"ldhn{b}_{tc}")
            S.dma("gpsimd", sgl, d_sigo[tc * 128:(tc + 1) * 128, b * 512:(b + 1) * 512],
                  reads=(f"dsg{b}_{tc}",), writes=(sk_,), tag=f"ldsg{b}_{tc}")
            S.add("vector", lambda e, hnt=hnt, sgl=sgl: e.tensor_mul(hob, sgl, hnt),
                  reads=(sk_, hk), writes=("hob",), tag=f"ogm{b}_{tc}")
            for i in range(4):
                vc = b * 4 + i
                bk2 = 5 + (i % 2)
                S.add("tensor", lambda e, i=i, bk2=bk2: e.transpose(
                    pbb(bk2, 128), hob[:, i * 128:(i + 1) * 128], identb),
                    reads=("hob", "identb"), writes=(f"pb{bk2}",), tag=f"hoT{b}_{tc}_{i}", kind="tp")
                S.add("vector", lambda e, tc=tc, vc=vc, bk2=bk2: e.tensor_copy(
                    hout[tc][0][:, vc * 128:(vc + 1) * 128], pbb(bk2, 128)),
                    reads=(f"pb{bk2}",), writes=(f"hout{tc}",),
                    tag=f"hoTe{b}_{tc}_{i}")

    wblk0, _ = sb("wblk0", [128, 16 * 512], BF16)
    wblk1, _ = sb("wblk1", [128, 16 * 512], BF16)
    smst, _ = sb("smst", [128, 512], F32)
    wsrc_wout = wout_t.rearrange("(vc p) n -> p vc n", p=128)
    for b in range(4):
        wb, wk_ = ((wblk0, "wblk0"), (wblk1, "wblk1"))[b % 2]
        S.dma("gpsimd", wb.rearrange("p (vc n) -> p vc n", n=512),
              wsrc_wout[:, :, b * 512:(b + 1) * 512], writes=(wk_,), tag=f"ldwo{b}")
        for tc in range(16):
            bk = 2 + (tc % 2)
            for vc in range(16):
                S.add("tensor", lambda e, vc=vc, tc=tc, bk=bk, wb=wb: e.matmul(
                    pbf(bk), hout[tc][0][:, vc * 128:(vc + 1) * 128],
                    wb[:, vc * 512:(vc + 1) * 512], start=(vc == 0), stop=(vc == 15)),
                    reads=(f"hout{tc}", wk_), writes=(f"pb{bk}",), tag=f"womm{b}_{tc}")
            S.add("vector", lambda e, bk=bk: e.tensor_copy(smst, pbf(bk)),
                  reads=(f"pb{bk}",), writes=("smst",), tag=f"woev{b}_{tc}")
            S.dma("sync", d_smix[tc * 128:(tc + 1) * 128, b * 512:(b + 1) * 512], smst,
                  reads=("smst",), writes=(f"dsm{b}_{tc}",), tag=f"stsm{b}_{tc}")
    tap("smix", smst, (128, 512), F32, ("smst",))
    if "smixfull" in taps:
        o = nc.dram_tensor("tap_smixfull", [T, D], F32, kind="ExternalOutput").ap()
        S.dma("sync", o, d_smix, reads=tuple(f"dsm{b}_{t}" for b in range(4) for t in range(16)),
              writes=("tapsmf",))

    AR.free(*[f"hout{t}" for t in range(16)], "wblk0", "wblk1",
            "hnt0", "hnt1", "sgl0", "sgl1", "hob", "smst")

    xn_fm = []
    for dc in range(16):
        a, cf = sb(f"xnr{dc}", [128, 2048], BF16)
        xn_fm.append(a)
        S.dma("sync", a, d_xnfm[dc * 128:(dc + 1) * 128, :],
              reads=(f"dxn{dc}",), writes=(f"xnr{dc}",), tag=f"rldxn{dc}")
    wgu0, _ = sb("wgu0", [128, 2048 + 2048], BF16)
    wgu1, _ = sb("wgu1", [128, 2048 + 2048], BF16)
    sgate, _ = sb("sgate", [128, 512], F32)
    hsq, _ = sb("hsq", [128, 512], F32)
    hrow, _ = sb("hrow", [128, 2048], BF16)
    sqacc, _ = sb("sqacc", [128, 2048], F32)
    S.add("vector", lambda e: e.memset(sqacc, 0.0), writes=("sqacc",))
    wgu_v = d_wqgu.rearrange("(dc p) H -> p dc H", p=128)

    h_spills = []
    for g in range(64):
        wb, wk_ = ((wgu0, "wgu0"), (wgu1, "wgu1"))[g % 2]
        S.dma("sync", wb[:, 0:2048].rearrange("p (dc n) -> p dc n", n=128),
              wgu_v[:, :, g * 128:(g + 1) * 128],
              reads=("d_wqgu",), writes=(wk_,), tag=f"ldgu{g}")
        S.dma("sync", wb[:, 2048:4096].rearrange("p (dc n) -> p dc n", n=128),
              wgu_v[:, :, 8192 + g * 128:8192 + (g + 1) * 128],
              reads=("d_wqgu",), writes=(wk_,), tag=f"ldup{g}")
        for tb in range(4):
            bkg, bku = 2 + (tb % 2) * 2, 3 + (tb % 2) * 2
            for dc in range(16):
                S.add("tensor", lambda e, dc=dc, wb=wb, tb=tb, bkg=bkg: e.matmul(
                    pbf(bkg), wb[:, dc * 128:(dc + 1) * 128],
                    xn_fm[dc][:, tb * 512:(tb + 1) * 512], start=(dc == 0), stop=(dc == 15)),
                    reads=(wk_, f"xnr{dc}"), writes=(f"pb{bkg}",), tag=f"gum{g}_{tb}")
                S.add("tensor", lambda e, dc=dc, wb=wb, tb=tb, bku=bku: e.matmul(
                    pbf(bku), wb[:, 2048 + dc * 128:2048 + (dc + 1) * 128],
                    xn_fm[dc][:, tb * 512:(tb + 1) * 512], start=(dc == 0), stop=(dc == 15)),
                    reads=(wk_, f"xnr{dc}"), writes=(f"pb{bku}",), tag=f"upm{g}_{tb}")
            S.add("scalar", lambda e, bkg=bkg: e.activation(sgate, pbf(bkg), AF.Silu),
                  reads=(f"pb{bkg}",), writes=("sgate",), tag=f"gsil{g}_{tb}")
            S.add("vector", lambda e, tb=tb, bku=bku: e.tensor_tensor(hrow[:, tb * 512:(tb + 1) * 512],
                                                             sgate, pbf(bku), ALU.mult),
                  reads=("sgate", f"pb{bku}"), writes=("hrow",), tag=f"gmul{g}_{tb}")
            S.add("scalar", lambda e, tb=tb: e.activation(hsq, hrow[:, tb * 512:(tb + 1) * 512],
                                                          AF.Square),
                  reads=("hrow",), writes=("hsq",), tag=f"gsq{g}_{tb}")
            S.add("vector", lambda e, tb=tb: e.tensor_add(sqacc[:, tb * 512:(tb + 1) * 512],
                                                          sqacc[:, tb * 512:(tb + 1) * 512], hsq),
                  reads=("sqacc", "hsq"), writes=("sqacc",), tag=f"gacc{g}_{tb}")
        S.dma("sync", d_hfm[g * 128:(g + 1) * 128, :], hrow, reads=("hrow",),
              writes=(f"dh{g}",), tag=f"sth{g}")
        h_spills.append(f"dh{g}")

    if "wqgu" in taps:
        o = nc.dram_tensor("tap_wqgu", [256, 16384], BF16, kind="ExternalOutput").ap()
        S.dma("sync", o[0:128, :], d_wqgu[0:128, :], reads=("d_wqgu",), writes=("tapwq1",))
        S.dma("sync", o[128:256, :], d_wqgu[1024:1152, :], reads=("d_wqgu",), writes=("tapwq2",))
    if "wqsh" in taps:
        o2 = nc.dram_tensor("tap_wqsh", [256, 16384], BF16, kind="ExternalOutput").ap()
        S.dma("sync", o2, d_wqgu_sh, reads=tuple(q_st[:16]), writes=("tapwqsh",))
    tap("sqacc", sqacc, (128, 2048), F32, ("sqacc",))
    tap("hrow", hrow, (128, 2048), BF16, ("hrow",))

    irow, _ = sb("irow", [1, 2048], F32)
    for b in range(4):
        S.add("tensor", lambda e, b=b: e.matmul(pbf(4, 512)[0:1, :], ones_col,
                                                sqacc[:, b * 512:(b + 1) * 512],
                                                start=True, stop=True),
              reads=("ones_col", "sqacc"), writes=("pb4",), tag=f"rsum{b}")
        S.add("vector", lambda e, b=b: e.tensor_copy(irow[0:1, b * 512:(b + 1) * 512],
                                                     pbf(4, 512)[0:1, :]),
              reads=("pb4",), writes=("irow",))
    S.add("scalar", lambda e: e.activation(irow, irow, AF.Ln, scale=1.0 / 8192.0, bias=RMS_EPS),
          reads=("irow", "constb"), writes=("irow",))
    S.add("scalar", lambda e: e.activation(irow, irow, AF.Exp, scale=-0.5),
          reads=("irow",), writes=("irow",))
    S.dma("sync", d_irow, irow, reads=("irow",), writes=("d_irow",), tag="stir")
    inv16, _ = sb("inv16", [16, 128], F32)
    invc, _ = sb("invc", [128, 16], F32)
    S.dma("sync", inv16, d_irow.rearrange("o (a b) -> (o a) b", b=128),
          reads=("d_irow",), writes=("inv16",), tag="ldir")
    S.add("tensor", lambda e: e.transpose(pbf(4, 16), inv16, ident[0:16, 0:16]),
          reads=("inv16", "ident"), writes=("pb4",), tag="tpin", kind="tp")
    S.add("vector", lambda e: e.tensor_copy(invc, pbf(4, 16)), reads=("pb4",), writes=("invc",))
    tap("invc", invc, (128, 16), F32, ("invc",))

    AR.free(*[f"xnr{dc}" for dc in range(16)], "wgu0", "wgu1", "sgate", "hsq",
            "hrow", "sqacc")

    hblk = []
    for pari in range(2):
        row = []
        for i in range(4):
            a, cf = sb(f"hblk{pari}_{i}", [128, 16 * 512], BF16)
            row.append(a)
        hblk.append(row)
    wps = []
    for i in range(16):
        a, _ = sb(f"wp{i}", [128, 512], BF16)
        wps.append(a)
    ot0, _ = sb("ot0", [128, 512], F32)
    ot1, _ = sb("ot1", [128, 512], F32)
    xres0, _ = sb("xres0", [128, 512], F32)
    xres1, _ = sb("xres1", [128, 512], F32)
    sm0, _ = sb("sm0", [128, 512], F32)
    sm1, _ = sb("sm1", [128, 512], F32)
    gsil, _ = sb("gsil", [128, 512], F32)
    hfm_v = d_hfm.rearrange("(hc p) t -> p hc t", p=128)

    out_sts = []
    for tb in range(4):
        pari = tb % 2
        for i in range(4):
            S.dma("sync", hblk[pari][i].rearrange("p (hc t) -> p hc t", t=512),
                  hfm_v[:, i * 16:(i + 1) * 16, tb * 512:(tb + 1) * 512],
                  reads=tuple(h_spills), writes=(f"hblk{pari}_{i}",), tag=f"ldhb{tb}_{i}")
        for db in range(4):
            dbp = 4 * (db % 2)
            for hc in range(64):
                wi = (db * 64 + hc) % 16
                wp, wpk = wps[wi], f"wp{wi}"
                S.dma(("sync", "gpsimd")[hc % 2], wp,
                      d_wqpr[hc * 128:(hc + 1) * 128, db * 512:(db + 1) * 512],
                      reads=("d_wqpr",), writes=(wpk,), tag=f"ldwp{tb}_{db}_{hc}")
                for ts in range(4):
                    S.add("tensor", lambda e, hc=hc, ts=ts, wp=wp, pari=pari, dbp=dbp: e.matmul(
                        pbf(dbp + ts), hblk[pari][hc // 16][:, (hc % 16) * 512 + ts * 128:(hc % 16) * 512 + (ts + 1) * 128],
                        wp, start=(hc == 0), stop=(hc == 63)),
                        reads=(f"hblk{pari}_{hc // 16}", wpk), writes=(f"pb{dbp + ts}",), tag=f"prm{tb}_{db}_{ts}")
            for ts in range(4):
                tc = tb * 4 + ts
                ot, ok = ((ot0, "ot0"), (ot1, "ot1"))[ts % 2]
                xr, xk2 = ((xres0, "xres0"), (xres1, "xres1"))[ts % 2]
                smt, smk = ((sm0, "sm0"), (sm1, "sm1"))[ts % 2]
                S.dma("sync", xr, x_in[tc * 128:(tc + 1) * 128, db * 512:(db + 1) * 512],
                      writes=(xk2,), tag=f"ldx2{tb}_{db}_{ts}")
                S.dma("sync", smt, d_smix[tc * 128:(tc + 1) * 128, db * 512:(db + 1) * 512],
                      reads=(f"dsm{db}_{tc}",), writes=(smk,), tag=f"ldsm{tb}_{db}_{ts}")
                S.add("vector", lambda e, ts=ts, tc=tc, ot=ot, dbp=dbp: e.tensor_scalar(
                    ot, pbf(dbp + ts), invc[:, tc:tc + 1], None, ALU.mult),
                    reads=(f"pb{dbp + ts}", "invc"), writes=(ok,), tag=f"prs{tb}_{db}_{ts}")
                S.add("scalar", lambda e, ot=ot: e.activation(gsil, ot, AF.Silu),
                      reads=(ok,), writes=("gsil",), tag=f"prsil{tb}_{db}_{ts}")
                S.add("vector", lambda e, smt=smt, ot=ot: e.tensor_mul(ot, gsil, smt),
                      reads=("gsil", smk), writes=(ok,), tag=f"prg{tb}_{db}_{ts}")
                S.add("vector", lambda e, xr=xr, ot=ot: e.tensor_add(ot, ot, xr),
                      reads=(ok, xk2), writes=(ok,), tag=f"pradd{tb}_{db}_{ts}")
                st = S.dma("sync", out_ext[tc * 128:(tc + 1) * 128, db * 512:(db + 1) * 512], ot,
                           reads=(ok,), writes=(f"st{tb}_{db}_{ts}",), tag=f"stout{tb}_{db}_{ts}")
                out_sts.append(f"st{tb}_{db}_{ts}")

    S.barrier("vector", reads=tuple(out_sts), tag="final")

    sems = {}
    for sk in S.semkeys():
        name = "s_" + "_".join(str(x) for x in sk)
        sems[sk] = es.enter_context(nc.semaphore(name))
    with nc.Block() as block:
        S.emit(block, sems)
    es.close()
    st, nw = S.stats()
    return nc, dict(ops=st, waits=nw, arena_hw=AR.hw, taps=tap_names)


_cache = {}


def _get_program(taps=()):
    key = tuple(sorted(taps))
    if key not in _cache:
        _cache[key] = build_program(taps)
    return _cache[key]


def kernel(L0_attn_x, L0_attn_Wq, L0_attn_Wk, L0_attn_Wv, L0_attn_Wog, L0_attn_Wig,
           L0_attn_big, L0_attn_Wfg, L0_attn_bfg, L0_attn_gamma, L0_attn_Wout,
           L0_ffn_w_gate_up, L0_ffn_w_proj):
    F = np.float32
    x = np.ascontiguousarray(np.asarray(L0_attn_x, F).reshape(16384, 2048))
    com = dict(
        wq_t=np.ascontiguousarray(np.asarray(L0_attn_Wq, F).T),
        wk_t=np.ascontiguousarray(np.asarray(L0_attn_Wk, F).T),
        wv_t=np.ascontiguousarray(np.asarray(L0_attn_Wv, F).T),
        wog_t=np.ascontiguousarray(np.asarray(L0_attn_Wog, F).T),
        wout_t=np.ascontiguousarray(np.asarray(L0_attn_Wout, F).T),
        wif_t=np.ascontiguousarray(np.concatenate(
            [np.asarray(L0_attn_Wig, F).T, np.asarray(L0_attn_Wfg, F).T], axis=1)),
        bif=np.concatenate([np.asarray(L0_attn_big, F),
                            np.asarray(L0_attn_bfg, F)]).reshape(1, 16),
        gamma_row=np.ascontiguousarray(np.asarray(L0_attn_gamma, F).reshape(1, 2048)),
        ident=np.eye(128, dtype=F),
        mask_ut=np.triu(np.ones((128, 128), F)),
    )
    wgu_t = np.ascontiguousarray(np.asarray(L0_ffn_w_gate_up, F).T)
    wpr_t = np.ascontiguousarray(np.asarray(L0_ffn_w_proj, F).T)

    nc, info = _get_program(tuple(sorted(TAPS)))
    in_maps = []
    for c in range(NCORES):
        m = dict(com)
        m["x_c"] = x[c * T:(c + 1) * T]
        m["wgu_sh"] = np.ascontiguousarray(wgu_t[c * 256:(c + 1) * 256])
        m["wproj_sh"] = np.ascontiguousarray(wpr_t[c * 1024:(c + 1) * 1024])
        m["fold_mask"] = (np.arange(8) < c).astype(F).reshape(1, 8)
        in_maps.append(m)
    res = run_bass_kernel_spmd(nc, in_maps, core_ids=list(range(NCORES)))
    kernel.last_results = res
    globals()["last_results"] = res
    out = np.concatenate([res.results[c]["out_c"] for c in range(NCORES)], axis=0)
    return out.reshape(1, 16384, 2048)
